# version 05
# baseline (speedup 1.0000x reference)
import numpy as np
import contextlib
import concourse.bass as bass
import concourse.mybir as mybir
from concourse.bass_utils import run_bass_kernel_spmd
import ml_dtypes

F32 = mybir.dt.float32
BF16 = mybir.dt.bfloat16
AF = mybir.ActivationFunctionType
ALU = mybir.AluOpType
AX = mybir.AxisListType

D = 1024
NTILE = 16
TOK = 2048
EPS = 1e-5
FFN = "ffn"
ATT = "attn"

DEBUG_TAPS = ()
DEBUG_DUMPS = ()


def tmap(cloc):
    return sorted([8 * B + cloc for B in range(8)] + [8 * B + 7 - cloc for B in range(8)])


class Ctx:

    def __init__(self, me, eng, sems):
        self.me = me
        self.eng = eng
        self.sems = sems
        self.cnt = {k: 0 for k in sems}
        self.marks = {}
        self.lastw = {}

    def do(self, engname, fn, inc=None, n=1):
        inst = None
        if self.me == engname:
            inst = fn(self.eng)
        if inc is not None:
            self.cnt[inc] += n
            if inst is not None:
                inst.then_inc(self.sems[inc], n)

    SELF = {"vector": "vs", "scalar": "ss", "tensor": "ts", "gpsimd": "ps"}

    def dof(self, engname, fn):
        sem = Ctx.SELF[engname]
        self.do(engname, fn, inc=sem)
        if self.me == engname:
            self.eng.wait_ge(self.sems[sem], self.cnt[sem])
            self.lastw[sem] = self.cnt[sem]

    def dma(self, engname, out, in_, **kw):
        q = "qs" if engname == "sync" else "qg"
        self.do(engname, lambda e: e.dma_start(out=out, in_=in_, **kw), inc=q, n=16)

    def mark(self, name, sem):
        self.marks[name] = (sem, self.cnt[sem])

    def wait(self, engname, markname):
        sem, val = self.marks[markname]
        if self.me == engname and val > 0 and self.lastw.get(sem, -1) < val:
            self.eng.wait_ge(self.sems[sem], val)
            self.lastw[sem] = val


def build_program(sublayers=(FFN, ATT, FFN), debug_taps=None):
    if debug_taps is None:
        debug_taps = DEBUG_TAPS
    nc = bass.Bass("TRN2", target_bir_lowering=False)

    x_in = nc.dram_tensor("x_in", [TOK, D], F32, kind="ExternalInput").ap()
    x_out = nc.dram_tensor("x_out", [TOK, D], F32, kind="ExternalOutput").ap()
    taps = {i: nc.dram_tensor(f"x_tap{i}", [TOK, D], F32, kind="ExternalOutput").ap()
            for i in debug_taps}
    _dump_shapes = {"kloc0": ([D, TOK // 2], BF16), "vloc0": ([TOK // 2, D], BF16),
                    "kg0": ([4 * D, TOK // 2], BF16), "vg0": ([2 * TOK, D], BF16),
                    "bndg0": ([128, 1], F32), "QT": ([128, 16384], BF16),
                    "sums": ([64, 2048], F32), "AVacc": ([128, 4096], F32),
                    "attnT": ([128, 16384], BF16), "negC": ([128, 1], F32),
                    "x_res_d": ([128, NTILE * D], F32)}
    dumps_out = {nm: nc.dram_tensor(f"dump_{nm}", _dump_shapes[nm][0],
                                    _dump_shapes[nm][1], kind="ExternalOutput").ap()
                 for nm in DEBUG_DUMPS}

    aWq = nc.dram_tensor("aWq", [D, D], F32, kind="ExternalInput").ap()
    aWk = nc.dram_tensor("aWk", [D, D], F32, kind="ExternalInput").ap()
    aWv = nc.dram_tensor("aWv", [D, D], F32, kind="ExternalInput").ap()
    aWo = nc.dram_tensor("aWo", [D, D], F32, kind="ExternalInput").ap()
    abqT = nc.dram_tensor("abqT", [128, 8], F32, kind="ExternalInput").ap()
    abkT = nc.dram_tensor("abkT", [128, 8], F32, kind="ExternalInput").ap()
    abvR = nc.dram_tensor("abvR", [128, D], F32, kind="ExternalInput").ap()
    fqwT = nc.dram_tensor("fqwT", [D, D], F32, kind="ExternalInput").ap()
    fkwT = nc.dram_tensor("fkwT", [D, D], F32, kind="ExternalInput").ap()
    fvwT = nc.dram_tensor("fvwT", [D, D], F32, kind="ExternalInput").ap()
    fowT = nc.dram_tensor("fowT", [D, D], F32, kind="ExternalInput").ap()
    fqbT = nc.dram_tensor("fqbT", [128, 8], F32, kind="ExternalInput").ap()
    fkbT = nc.dram_tensor("fkbT", [128, 8], F32, kind="ExternalInput").ap()
    fvbR = nc.dram_tensor("fvbR", [128, D], F32, kind="ExternalInput").ap()
    fnwT = nc.dram_tensor("fnwT", [128, 8], F32, kind="ExternalInput").ap()
    fnbT = nc.dram_tensor("fnbT", [128, 8], F32, kind="ExternalInput").ap()
    fobR = nc.dram_tensor("fobR", [128, D], F32, kind="ExternalInput").ap()
    maskin = nc.dram_tensor("maskin", [128, 8 * 256], F32, kind="ExternalInput").ap()

    ident_c = nc.inline_tensor(np.eye(128, dtype=ml_dtypes.bfloat16), name="ident_c").ap()
    identf_c = nc.inline_tensor(np.eye(128, dtype=np.float32), name="identf_c").ap()
    o32 = np.zeros((128, 32), dtype=ml_dtypes.bfloat16); o32[:, 0] = 1
    ones32_c = nc.inline_tensor(o32, name="ones32_c").ap()
    onesf_c = nc.inline_tensor(np.ones((128, 128), np.float32), name="onesf_c").ap()
    ehF_np = np.zeros((128, 256), dtype=ml_dtypes.bfloat16)
    for h in range(8):
        ehF_np[:, h * 32 + h] = 1
    ehF_c = nc.inline_tensor(ehF_np, name="ehF_c").ap()
    ehA_np = np.zeros((128, 256), dtype=ml_dtypes.bfloat16)
    for o in range(8):
        ehA_np[0:64, o * 32 + 2 * o] = 1
        ehA_np[64:128, o * 32 + 2 * o + 1] = 1
    ehA_c = nc.inline_tensor(ehA_np, name="ehA_c").ap()

    NS = len(sublayers)
    HT = TOK // 2
    klocA = [nc.dram_tensor(f"klocA{s}", [D, HT], BF16, kind="Internal").ap() for s in range(NS)]
    klocB = [nc.dram_tensor(f"klocB{s}", [D, HT], BF16, kind="Internal").ap() for s in range(NS)]
    vlocA = [nc.dram_tensor(f"vlocA{s}", [HT, D], BF16, kind="Internal").ap() for s in range(NS)]
    vlocB = [nc.dram_tensor(f"vlocB{s}", [HT, D], BF16, kind="Internal").ap() for s in range(NS)]
    kgA = [nc.dram_tensor(f"kgA{s}", [4 * D, HT], BF16, kind="Internal").ap() for s in range(NS)]
    kgB = [nc.dram_tensor(f"kgB{s}", [4 * D, HT], BF16, kind="Internal").ap() for s in range(NS)]
    vgA = [nc.dram_tensor(f"vgA{s}", [4 * HT, D], BF16, kind="Internal").ap() for s in range(NS)]
    vgB = [nc.dram_tensor(f"vgB{s}", [4 * HT, D], BF16, kind="Internal").ap() for s in range(NS)]
    bnd = [nc.dram_tensor(f"bnd{s}", [32, 1], F32, kind="Internal").ap() for s in range(NS)]
    bndg = [nc.dram_tensor(f"bndg{s}", [128, 1], F32, kind="Internal").ap() for s in range(NS)]

    S = lambda n, sh, dt=F32: nc.alloc_sbuf_tensor(n, sh, dt).ap()
    x_res = S("x_res", [128, NTILE * D])
    regA = S("regA", [128, 18560], BF16)
    regB = S("regB", [128, 16384], BF16)
    QT = S("QT", [128, 16384], BF16)
    AVacc = S("AVacc", [128, 4096])
    sumsSB = S("sumsSB", [64, 2048])
    wstage = S("wstage", [128, 1024])
    vbias = S("vbias", [128, D])
    biasQK = S("biasQK", [128, 136])
    bqkv = S("bqkv", [128, 24])
    sqbuf = S("sqbuf", [128, 512], BF16)
    small = S("small", [128, 64])
    kqmax = S("kqmax", [64, 1])
    kngT = S("kngT", [64, 4])
    negC = S("negC", [128, 1])
    tiny = S("tiny", [4, 4])
    ident = S("ident", [128, 128], BF16)
    identf = S("identf", [128, 128])
    ones32 = S("ones32", [128, 32], BF16)
    onesf = S("onesf", [128, 128])
    ehF = S("ehF", [128, 256], BF16)
    ehA = S("ehA", [128, 256], BF16)
    mask01 = S("mask01", [128, 8 * 256], BF16)

    A_WF = 0
    A_KST = 8192
    A_VST = 12288
    A_SLAB = 0
    A_P = 16512

    ar = nc.alloc_psum_tensor("ar", [128, 4096], F32).ap()

    C_LNS, C_MU, C_VAR, C_R1, C_R12, C_T1, C_T2 = 0, 1, 2, 3, 4, 5, 6
    C_MUSAVE, C_R1SAVE = 16, 32

    rg = [[0, 1, 2, 3], [4, 5, 6, 7]]

    with contextlib.ExitStack() as ctx_stack:
        sems = {}
        for name in ("qs", "qg", "cc", "ts", "vs", "ss", "ps"):
            sems[name] = ctx_stack.enter_context(nc.semaphore(name))
        blk = ctx_stack.enter_context(nc.Block())

        def load_w_slabs(c, tag, wt, gates):
            sync_gate, dve_gate = gates if gates else (None, None)
            for m in range(8):
                if m >= 1:
                    c.wait("sync", tag + f"wf{m - 1}")
                if m == 0 and sync_gate is not None:
                    c.wait("sync", sync_gate)
                c.dma("sync", wstage, wt[m * 128:(m + 1) * 128, :])
                c.mark(tag + f"wl{m}", "qs")
                c.wait("vector", tag + f"wl{m}")
                if m == 0 and dve_gate is not None:
                    c.wait("vector", dve_gate)
                c.do("vector", lambda e, m=m: e.tensor_copy(
                    out=regA[:, A_WF + m * D:A_WF + (m + 1) * D], in_=wstage), inc="vs")
                c.mark(tag + f"wf{m}", "vs")

        def emit_sublayer(c, si, kind):
            pfx = f"s{si}_"
            isffn = kind == FFN
            scale = (1.0 / 32.0) if isffn else (1.0 / 8.0)
            nm_tc = [8, 4, 2, 1] if isffn else [8, 8, 8, 8]
            mout_tc = [1024, 512, 256, 128] if isffn else [1024] * 4
            WQ, WK, WV, WO = (fqwT, fkwT, fvwT, fowT) if isffn else (aWq, aWk, aWv, aWo)
            BQ, BK, BVR = (fqbT, fkbT, fvbR) if isffn else (abqT, abkT, abvR)

            c.dma("gpsimd", bqkv[:, 0:8], BQ)
            c.dma("gpsimd", bqkv[:, 8:16], BK)
            if isffn:
                c.dma("gpsimd", bqkv[:, 16:24], fnbT)
                c.dma("gpsimd", biasQK[:, 128:136], fnwT)
            c.mark(pfx + "bias_in", "qg")

            c.wait("vector", "x_ready")
            if isffn:
                c.wait("vector", pfx + "bias_in")
            zt_n = 0
            for tb in range(NTILE):
                xb = x_res[:, tb * D:(tb + 1) * D]
                c.dof("vector", lambda e, xb=xb: e.reduce_sum(
                    out=small[:, C_LNS:C_LNS + 1], in_=xb, axis=AX.X))
                c.dof("vector", lambda e: e.tensor_scalar_mul(
                    out=small[:, C_MU:C_MU + 1], in0=small[:, C_LNS:C_LNS + 1],
                    scalar1=1.0 / D))
                if tb >= 1:
                    c.wait("vector", pfx + f"ztt{(tb - 1) * 8 + 7}")
                c.dof("vector", lambda e, xb=xb: e.scalar_tensor_tensor(
                    out=wstage, in0=xb, scalar=small[:, C_MU:C_MU + 1], in1=xb,
                    op0=ALU.subtract, op1=ALU.mult,
                    accum_out=small[:, C_VAR:C_VAR + 1]))
                c.dof("vector", lambda e: e.tensor_scalar(
                    out=small[:, C_VAR:C_VAR + 1], in0=small[:, C_VAR:C_VAR + 1],
                    scalar1=1.0 / D, scalar2=EPS, op0=ALU.mult, op1=ALU.add))
                c.mark(pfx + f"lnv{tb}", "vs")
                c.wait("scalar", pfx + f"lnv{tb}")
                c.dof("scalar", lambda e: e.activation(
                    out=small[:, C_T1:C_T1 + 1], in_=small[:, C_VAR:C_VAR + 1],
                    func=AF.Ln))
                c.dof("scalar", lambda e: e.activation(
                    out=small[:, C_R1:C_R1 + 1], in_=small[:, C_T1:C_T1 + 1],
                    func=AF.Exp, scale=-0.5))
                c.mark(pfx + f"r1_{tb}", "ss")
                c.wait("vector", pfx + f"r1_{tb}")
                c.do("vector", lambda e, tb=tb: e.tensor_copy(
                    out=small[:, C_R1SAVE + tb:C_R1SAVE + tb + 1],
                    in_=small[:, C_R1:C_R1 + 1]), inc="vs")
                c.do("vector", lambda e, tb=tb: e.tensor_copy(
                    out=small[:, C_MUSAVE + tb:C_MUSAVE + tb + 1],
                    in_=small[:, C_MU:C_MU + 1]), inc="vs")
                if isffn:
                    c.dof("vector", lambda e: e.reciprocal(
                        out=small[:, C_T1:C_T1 + 1], in_=small[:, C_VAR:C_VAR + 1]))
                    c.dof("vector", lambda e: e.tensor_scalar(
                        out=small[:, C_T2:C_T2 + 1], in0=small[:, C_VAR:C_VAR + 1],
                        scalar1=-EPS, scalar2=None, op0=ALU.add))
                    c.dof("vector", lambda e: e.tensor_tensor(
                        out=small[:, C_T2:C_T2 + 1], in0=small[:, C_T2:C_T2 + 1],
                        in1=small[:, C_T1:C_T1 + 1], op=ALU.mult))
                    c.dof("vector", lambda e: e.tensor_scalar(
                        out=small[:, C_T2:C_T2 + 1], in0=small[:, C_T2:C_T2 + 1],
                        scalar1=EPS, scalar2=None, op0=ALU.add))
                    c.mark(pfx + f"vy{tb}", "vs")
                    c.wait("scalar", pfx + f"vy{tb}")
                    c.dof("scalar", lambda e: e.activation(
                        out=small[:, C_T1:C_T1 + 1], in_=small[:, C_T2:C_T2 + 1],
                        func=AF.Ln))
                    c.dof("scalar", lambda e: e.activation(
                        out=small[:, C_T2:C_T2 + 1], in_=small[:, C_T1:C_T1 + 1],
                        func=AF.Exp, scale=-0.5))
                    c.mark(pfx + f"r2_{tb}", "ss")
                    c.wait("vector", pfx + f"r2_{tb}")
                    c.dof("vector", lambda e: e.tensor_tensor(
                        out=small[:, C_R12:C_R12 + 1], in0=small[:, C_R1:C_R1 + 1],
                        in1=small[:, C_T2:C_T2 + 1], op=ALU.mult))
                else:
                    c.dof("vector", lambda e: e.tensor_copy(
                        out=small[:, C_R12:C_R12 + 1], in_=small[:, C_R1:C_R1 + 1]))
                c.dof("vector", lambda e, xb=xb: e.tensor_scalar(
                    out=wstage, in0=xb,
                    scalar1=small[:, C_MU:C_MU + 1], scalar2=small[:, C_R12:C_R12 + 1],
                    op0=ALU.subtract, op1=ALU.mult))
                c.mark(pfx + f"zp{tb}", "vs")
                for mt in range(8):
                    sl = zt_n % 2
                    if zt_n >= 2:
                        c.wait("tensor", pfx + f"zte{zt_n - 2}")
                    c.wait("tensor", pfx + f"zp{tb}")
                    c.do("tensor", lambda e, mt=mt, sl=sl: e.transpose(
                        ar[:, sl * 512:sl * 512 + 128],
                        wstage[:, mt * 128:(mt + 1) * 128], identf), inc="ts")
                    c.mark(pfx + f"ztt{zt_n}", "ts")
                    c.wait("vector", pfx + f"ztt{zt_n}")
                    if isffn:
                        c.do("vector", lambda e, mt=mt, sl=sl, tb=tb: e.tensor_scalar(
                            out=regB[:, mt * TOK + tb * 128:mt * TOK + (tb + 1) * 128],
                            in0=ar[:, sl * 512:sl * 512 + 128],
                            scalar1=biasQK[:, 128 + mt:129 + mt],
                            scalar2=bqkv[:, 16 + mt:17 + mt],
                            op0=ALU.mult, op1=ALU.add), inc="vs")
                    else:
                        c.do("vector", lambda e, mt=mt, sl=sl, tb=tb: e.tensor_copy(
                            out=regB[:, mt * TOK + tb * 128:mt * TOK + (tb + 1) * 128],
                            in_=ar[:, sl * 512:sl * 512 + 128]), inc="vs")
                    c.mark(pfx + f"zte{zt_n}", "vs")
                    zt_n += 1
            c.mark(pfx + "zT_done", "vs")

            load_w_slabs(c, pfx + "v", WV, (pfx + "zT_done", "x_ready"))
            nvm = 0
            for tc in range(4):
                nm = nm_tc[tc]
                if tc == 0:
                    c.wait("gpsimd", "x_ready")
                    c.dma("gpsimd", vbias, BVR)
                    c.mark(pfx + "vbias0", "qg")
                for tb in range(tc * 4, tc * 4 + 4):
                    vslot = A_VST + (tb % 2) * D
                    if tb >= 2:
                        c.wait("vector", pfx + f"vw{tb - 2}")
                    for oh in range(2):
                        par = nvm % 2
                        if nvm >= 2:
                            c.wait("tensor", pfx + f"vev{nvm - 2}")
                        for m in range(nm):
                            c.wait("tensor", pfx + f"vwf{m}")
                            c.do("tensor", lambda e, m=m, tb=tb, oh=oh, par=par, nm=nm: e.matmul(
                                ar[:, 1024 + par * 512:1024 + (par + 1) * 512],
                                regB[:, m * TOK + tb * 128:m * TOK + (tb + 1) * 128],
                                regA[:, A_WF + m * D + oh * 512:A_WF + m * D + (oh + 1) * 512],
                                start=(m == 0), stop=(m == nm - 1)), inc="ts")
                        c.mark(pfx + f"vmm{nvm}", "ts")
                        c.wait("vector", pfx + f"vmm{nvm}")
                        c.wait("vector", pfx + "vbias0")
                        c.do("vector", lambda e, par=par, oh=oh, vslot=vslot: e.tensor_tensor(
                            out=regA[:, vslot + oh * 512:vslot + (oh + 1) * 512],
                            in0=ar[:, 1024 + par * 512:1024 + (par + 1) * 512],
                            in1=vbias[:, oh * 512:(oh + 1) * 512], op=ALU.add), inc="vs")
                        c.mark(pfx + f"vev{nvm}", "vs")
                        nvm += 1
                    c.wait("gpsimd", pfx + f"vev{nvm - 1}")
                    vdst = vlocA[si] if tb < 8 else vlocB[si]
                    c.dma("gpsimd", vdst[(tb % 8) * 128:(tb % 8 + 1) * 128, :],
                          regA[:, vslot:vslot + D])
                    c.mark(pfx + f"vw{tb}", "qg")
                    if tb == 7:
                        c.mark(pfx + "vwA", "qg")
            c.mark(pfx + "vwB", "qg")
            c.mark(pfx + "v_p1_done", "ts")

            load_w_slabs(c, pfx + "k", WK, (pfx + "vwf7", pfx + "v_p1_done"))
            c.wait("vector", pfx + "bias_in")
            c.do("vector", lambda e: e.tensor_copy(
                out=biasQK[:, 8:16], in_=bqkv[:, 8:16]), inc="vs")
            c.mark(pfx + "kbias", "vs")
            c.wait("tensor", pfx + "zT_done")
            eh = ehF if isffn else ehA
            nmm = 0
            for tc in range(4):
                nm = nm_tc[tc]
                for o in range(8):
                    par = nmm % 2
                    if nmm >= 2:
                        c.wait("tensor", pfx + f"kev{nmm - 2}")
                    for m in range(nm):
                        c.wait("tensor", pfx + f"kwf{m}")
                        c.do("tensor", lambda e, o=o, m=m, tc=tc, par=par, nm=nm: e.matmul(
                            ar[:, par * 512:(par + 1) * 512],
                            regA[:, A_WF + m * D + o * 128:A_WF + m * D + (o + 1) * 128],
                            regB[:, m * TOK + tc * 512:m * TOK + (tc + 1) * 512],
                            start=(m == 0), stop=(m == nm - 1)), inc="ts")
                    c.mark(pfx + f"kmm{nmm}", "ts")
                    c.wait("scalar", pfx + f"kmm{nmm}")
                    c.wait("scalar", pfx + "kbias")
                    if nmm >= 8:
                        c.wait("scalar", pfx + f"kw{nmm - 8}")
                        c.wait("scalar", pfx + f"ksq{nmm - 8}")
                    c.do("scalar", lambda e, o=o, par=par: e.activation(
                        out=regA[:, A_KST + o * 512:A_KST + (o + 1) * 512],
                        in_=ar[:, par * 512:(par + 1) * 512],
                        func=AF.Identity, bias=biasQK[:, 8 + o:9 + o]), inc="ss")
                    c.mark(pfx + f"kev{nmm}", "ss")
                    c.wait("vector", pfx + f"kev{nmm}")
                    if nmm > 0:
                        c.wait("vector", pfx + f"knmm{nmm - 1}")
                    c.do("vector", lambda e, o=o: e.tensor_tensor(
                        out=sqbuf,
                        in0=regA[:, A_KST + o * 512:A_KST + (o + 1) * 512],
                        in1=regA[:, A_KST + o * 512:A_KST + (o + 1) * 512],
                        op=ALU.mult), inc="vs")
                    c.mark(pfx + f"ksq{nmm}", "vs")
                    c.wait("tensor", pfx + f"ksq{nmm}")
                    c.wait("tensor", "consts")
                    if o == 0 and tc > 0:
                        c.wait("tensor", pfx + f"kqmax{tc - 1}")
                    c.do("tensor", lambda e, o=o, eh=eh: e.matmul(
                        ar[0:32, 2048:2560], eh[:, o * 32:(o + 1) * 32], sqbuf,
                        start=(o == 0), stop=(o == 7),
                        tile_position=(0, 0)), inc="ts")
                    c.mark(pfx + f"knmm{nmm}", "ts")
                    c.wait("gpsimd", pfx + f"kev{nmm}")
                    kdst = klocA[si] if tc < 2 else klocB[si]
                    c.dma("gpsimd",
                          kdst[o * 128:(o + 1) * 128, (tc % 2) * 512:(tc % 2 + 1) * 512],
                          regA[:, A_KST + o * 512:A_KST + (o + 1) * 512])
                    c.mark(pfx + f"kw{nmm}", "qg")
                    if nmm == 15:
                        c.mark(pfx + "kwA", "qg")
                    nmm += 1
                c.wait("vector", pfx + f"knmm{nmm - 1}")
                c.dof("vector", lambda e: e.reduce_max(
                    out=sumsSB[0:32, 0:1], in_=ar[0:32, 2048:2560], axis=AX.X))
                if tc == 0:
                    c.dof("vector", lambda e: e.tensor_copy(
                        out=kqmax[0:32, 0:1], in_=sumsSB[0:32, 0:1]))
                else:
                    c.dof("vector", lambda e: e.tensor_max(
                        out=kqmax[0:32, 0:1], in0=kqmax[0:32, 0:1],
                        in1=sumsSB[0:32, 0:1]))
                c.mark(pfx + f"kqmax{tc}", "vs")
            c.mark(pfx + "kwB", "qg")
            c.mark(pfx + "k_p2_done", "ts")
            c.wait("vector", pfx + "kqmax3")
            c.dof("vector", lambda e: e.tensor_copy(
                out=sumsSB[0:32, 1:2], in_=kqmax[0:32, 0:1]))
            c.mark(pfx + "knloc", "vs")
            c.wait("gpsimd", pfx + "knloc")
            c.dma("gpsimd", bnd[si], sumsSB[0:32, 1:2])
            c.mark(pfx + "bndw", "qg")
            c.wait("gpsimd", pfx + "bndw")
            c.wait("gpsimd", pfx + "kwB")
            c.wait("gpsimd", pfx + "vwB")
            c.do("gpsimd", lambda e: e.collective_compute(
                "AllGather", ALU.bypass, replica_groups=rg,
                ins=[bnd[si].opt()], outs=[bndg[si].opt()]), inc="cc")
            c.mark(pfx + "cc_bnd", "cc")
            c.do("gpsimd", lambda e: e.collective_compute(
                "AllGather", ALU.bypass, replica_groups=rg,
                ins=[klocA[si].opt()], outs=[kgA[si].opt()]), inc="cc")
            c.mark(pfx + "cc_kA", "cc")
            c.do("gpsimd", lambda e: e.collective_compute(
                "AllGather", ALU.bypass, replica_groups=rg,
                ins=[klocB[si].opt()], outs=[kgB[si].opt()]), inc="cc")
            c.mark(pfx + "cc_kB", "cc")
            c.do("gpsimd", lambda e: e.collective_compute(
                "AllGather", ALU.bypass, replica_groups=rg,
                ins=[vlocA[si].opt()], outs=[vgA[si].opt()]), inc="cc")
            c.mark(pfx + "cc_vA", "cc")
            c.do("gpsimd", lambda e: e.collective_compute(
                "AllGather", ALU.bypass, replica_groups=rg,
                ins=[vlocB[si].opt()], outs=[vgB[si].opt()]), inc="cc")
            c.mark(pfx + "cc_vB", "cc")
            c.wait("gpsimd", pfx + "cc_bnd")
            c.dma("gpsimd", kngT[32:64, 0:4],
                  bndg[si].rearrange("(r h) c -> h (r c)", h=32),
                  allow_slow_non_contiguous=True)
            c.mark(pfx + "bnd_rd", "qg")
            c.wait("vector", pfx + "bnd_rd")
            c.dof("vector", lambda e: e.reduce_max(
                out=kngT[32:64, 0:1], in_=kngT[32:64, 0:4], axis=AX.X))
            c.mark(pfx + "knglob", "vs")

            load_w_slabs(c, pfx + "q", WQ, (pfx + "kwf7", pfx + "k_p2_done"))
            c.wait("vector", pfx + "bias_in")
            c.do("vector", lambda e: e.tensor_copy(
                out=biasQK[:, 0:8], in_=bqkv[:, 0:8]), inc="vs")
            c.mark(pfx + "qbias", "vs")
            c.wait("tensor", pfx + "zT_done")
            nmm = 0
            for tc in range(4):
                nm = nm_tc[tc]
                for o in range(8):
                    par = nmm % 2
                    if nmm >= 2:
                        c.wait("tensor", pfx + f"qev{nmm - 2}")
                    else:
                        c.wait("tensor", pfx + f"kev{30 + nmm}")
                    for m in range(nm):
                        c.wait("tensor", pfx + f"qwf{m}")
                        c.do("tensor", lambda e, o=o, m=m, tc=tc, par=par, nm=nm: e.matmul(
                            ar[:, par * 512:(par + 1) * 512],
                            regA[:, A_WF + m * D + o * 128:A_WF + m * D + (o + 1) * 128],
                            regB[:, m * TOK + tc * 512:m * TOK + (tc + 1) * 512],
                            start=(m == 0), stop=(m == nm - 1)), inc="ts")
                    c.mark(pfx + f"qmm{nmm}", "ts")
                    c.wait("scalar", pfx + f"qmm{nmm}")
                    c.wait("scalar", pfx + "qbias")
                    c.do("scalar", lambda e, o=o, tc=tc, par=par: e.activation(
                        out=QT[:, o * TOK + tc * 512:o * TOK + (tc + 1) * 512],
                        in_=ar[:, par * 512:(par + 1) * 512],
                        func=AF.Identity, bias=biasQK[:, o:o + 1]), inc="ss")
                    c.mark(pfx + f"qev{nmm}", "ss")
                    nmm += 1
                for o in range(8):
                    if o == 0:
                        c.wait("vector", pfx + f"qev{tc * 8 + 7}")
                        c.wait("vector", pfx + ("knmm31" if tc == 0
                                                else f"qnmm{tc - 1}_7"))
                    else:
                        c.wait("vector", pfx + f"qnmm{tc}_{o - 1}")
                    c.do("vector", lambda e, o=o, tc=tc: e.tensor_tensor(
                        out=sqbuf,
                        in0=QT[:, o * TOK + tc * 512:o * TOK + (tc + 1) * 512],
                        in1=QT[:, o * TOK + tc * 512:o * TOK + (tc + 1) * 512],
                        op=ALU.mult), inc="vs")
                    c.mark(pfx + f"qsq{tc}_{o}", "vs")
                    c.wait("tensor", pfx + f"qsq{tc}_{o}")
                    if tc == 0 and o == 0:
                        c.wait("tensor", pfx + "kqmax3")
                    if o == 0 and tc > 0:
                        c.wait("tensor", pfx + f"qqmax{tc - 1}")
                    c.do("tensor", lambda e, o=o, eh=eh: e.matmul(
                        ar[32:64, 2048:2560], eh[:, o * 32:(o + 1) * 32], sqbuf,
                        start=(o == 0), stop=(o == 7),
                        tile_position=(0, 32)), inc="ts")
                    c.mark(pfx + f"qnmm{tc}_{o}", "ts")
                c.wait("vector", pfx + f"qnmm{tc}_7")
                c.dof("vector", lambda e: e.reduce_max(
                    out=sumsSB[32:64, 0:1], in_=ar[32:64, 2048:2560], axis=AX.X))
                if tc == 0:
                    c.dof("vector", lambda e: e.tensor_copy(
                        out=kqmax[32:64, 0:1], in_=sumsSB[32:64, 0:1]))
                else:
                    c.dof("vector", lambda e: e.tensor_max(
                        out=kqmax[32:64, 0:1], in0=kqmax[32:64, 0:1],
                        in1=sumsSB[32:64, 0:1]))
                c.mark(pfx + f"qqmax{tc}", "vs")
            c.mark(pfx + "q_p2_done", "ts")

            c.wait("vector", pfx + "qqmax3")
            c.wait("vector", pfx + "knglob")
            c.dof("vector", lambda e: e.tensor_tensor(
                out=kngT[32:64, 1:2], in0=kqmax[32:64, 0:1],
                in1=kngT[32:64, 0:1], op=ALU.add))
            c.mark(pfx + "cmb", "vs")
            c.wait("tensor", pfx + "cmb")
            c.wait("tensor", pfx + "q_p2_done")
            c.do("tensor", lambda e: e.transpose(
                ar[0:1, 2560:2592], kngT[32:64, 1:2], identf[32:64, 32:64]),
                inc="ts")
            c.mark(pfx + "cmbT", "ts")
            c.wait("vector", pfx + "cmbT")
            c.dof("vector", lambda e: e.reduce_max(
                out=tiny[0:1, 0:1], in_=ar[0:1, 2560:2592], axis=AX.X))
            c.dof("vector", lambda e, scale=scale: e.tensor_scalar_mul(
                out=tiny[0:1, 1:2], in0=tiny[0:1, 0:1], scalar1=-0.5 * scale))
            c.mark(pfx + "c11", "vs")
            c.wait("tensor", pfx + "c11")
            c.do("tensor", lambda e: e.matmul(
                ar[:, 2560:2561], onesf[0:1, :], tiny[0:1, 1:2],
                start=True, stop=True), inc="ts")
            c.mark(pfx + "cbc", "ts")
            c.wait("vector", pfx + "cbc")
            c.do("vector", lambda e: e.tensor_copy(out=negC, in_=ar[:, 2560:2561]), inc="vs")
            c.mark(pfx + "negC", "vs")

            emit_attention(c, si, kind, pfx, scale)

            emit_phase3(c, si, kind, pfx, nm_tc, mout_tc, WO)

        def emit_attention(c, si, kind, pfx, scale):
            isffn = kind == FFN
            ngrp = 4 if isffn else 8
            nktb = 4 if isffn else 2
            ktb_ranks = 1 if isffn else 2
            SLABW = 8192 if isffn else 8256
            if not isffn:
                for gate in ("q_p2_done", "ksq31", "vev31", "kwB", "vwB"):
                    c.wait("vector", pfx + gate)
                for par in range(2):
                    for hh in range(2):
                        for rr in range(2):
                            vr = A_SLAB + par * SLABW + 4096 + hh * 2080 + rr * 1040
                            c.do("vector", lambda e, vr=vr: e.memset(
                                regA[:, vr:vr + 1040]
                                .rearrange("p (j d) -> p j d", d=65)[:, :, 64:65],
                                1.0), inc="vs")
                c.mark(pfx + "vones", "vs")

            def emit_slab_dma(g, ktb):
                    par = ktb % 2
                    base = A_SLAB + par * SLABW
                    seq = g * nktb + ktb
                    if seq == 0:
                        c.wait("sync", pfx + "q_p2_done")
                    elif seq == 1:
                        c.wait("sync", pfx + "ksq31")
                        c.wait("sync", pfx + "vev31")
                        c.wait("sync", pfx + "kwB")
                        c.wait("sync", pfx + "vwB")
                    else:
                        pg, pk = divmod(seq - 2, nktb)
                        c.wait("sync", pfx + f"avdone_g{pg}k{pk}")
                    for half, ccm in (("A", "cc_vA"), ("B", "cc_vB")):
                        c.wait("sync", pfx + ccm)
                        vgh = (vgA if half == "A" else vgB)[si]
                        ho = 0 if half == "A" else 1024
                        for rr in range(ktb_ranks):
                            r = ktb * ktb_ranks + rr
                            for hh in range(2):
                                h = g * 2 + hh
                                if isffn:
                                    c.dma("sync",
                                          regA[:, base + 4096 + hh * 2048 + ho:
                                               base + 4096 + hh * 2048 + ho + 1024]
                                          .rearrange("p (j d) -> p j d", d=128),
                                          vgh[r * HT:(r + 1) * HT, h * 128:(h + 1) * 128]
                                          .rearrange("(j p) d -> p j d", p=128))
                                else:
                                    vr = (base + 4096 + hh * 2080 + rr * 1040 +
                                          (0 if half == "A" else 520))
                                    c.dma("sync",
                                          regA[:, vr:vr + 520]
                                          .rearrange("p (j d) -> p j d", d=65)[:, :, 0:64],
                                          vgh[r * HT:(r + 1) * HT, h * 64:(h + 1) * 64]
                                          .rearrange("(j p) d -> p j d", p=128))
                        c.mark(pfx + f"slabV{half}_g{g}k{ktb}", "qs")
                    for half, ccm in (("A", "cc_kA"), ("B", "cc_kB")):
                        c.wait("sync", pfx + ccm)
                        kgh = (kgA if half == "A" else kgB)[si]
                        ho = 0 if half == "A" else 1024
                        for rr in range(ktb_ranks):
                            r = ktb * ktb_ranks + rr
                            for hh in range(2):
                                h = g * 2 + hh
                                if isffn:
                                    c.dma("sync",
                                          regA[:, base + hh * 2048 + ho:
                                               base + hh * 2048 + ho + 1024],
                                          kgh[r * D + h * 128:r * D + (h + 1) * 128, :])
                                else:
                                    c.dma("sync",
                                          regA[hh * 64:(hh + 1) * 64,
                                               base + rr * 2048 + ho:
                                               base + rr * 2048 + ho + 1024],
                                          kgh[r * D + h * 64:r * D + (h + 1) * 64, :])
                        c.mark(pfx + f"slabK{half}_g{g}k{ktb}", "qs")

            normq = []
            for g in range(ngrp):
                def retire(p, g=g):
                    emit_av(c, pfx, kind, g, *p)
                    if p[11] and p[9] == nktb - 1:
                        emit_normalize_recip(c, pfx, kind, g, p[8])
                        normq.append((g, p[8]))
                emit_slab_dma(g, 0)
                if nktb > 1:
                    emit_slab_dma(g, 1)
                kt_seq = 0
                pends = []
                for ktb in range(nktb):
                    par = ktb % 2
                    base = A_SLAB + par * SLABW
                    for qc in range(4):
                        kts = []
                        for rr in range(ktb_ranks):
                            for j in range(16):
                                if isffn or j < 8 or qc >= j // 4:
                                    kts.append((rr, j))
                        for idx, (rr, j) in enumerate(kts):
                            sl = kt_seq % 2
                            masked = (not isffn) and j >= 8 and qc == j // 4
                            off = 256 if (masked and (j // 2) % 2 == 1) else 0
                            width = 512 - off
                            c.wait("tensor", pfx + f"slabK{'A' if j < 8 else 'B'}_g{g}k{ktb}")
                            c.wait("tensor", pfx + "negC")
                            if kt_seq >= 2:
                                c.wait("tensor", pfx + f"exp{g}_{kt_seq - 2}")
                            if isffn:
                                for hh in range(2):
                                    c.do("tensor", lambda e, hh=hh, j=j, sl=sl, qc=qc, base=base, g=g: e.matmul(
                                        ar[:, sl * 1024 + hh * 512:sl * 1024 + (hh + 1) * 512],
                                        regA[:, base + hh * 2048 + j * 128:base + hh * 2048 + (j + 1) * 128],
                                        QT[:, (g * 2 + hh) * TOK + qc * 512:(g * 2 + hh) * TOK + (qc + 1) * 512],
                                        start=True, stop=True), inc="ts")
                            else:
                                kb = base + rr * 2048 + j * 128
                                qb0 = g * TOK + qc * 512 + off
                                for hh in range(2):
                                    c.do("tensor", lambda e, kb=kb, hh=hh, sl=sl, qb0=qb0, off=off, width=width: e.matmul(
                                        ar[:, sl * 1024 + hh * 512 + off:sl * 1024 + (hh + 1) * 512],
                                        regA[hh * 64:(hh + 1) * 64, kb:kb + 128],
                                        QT[hh * 64:(hh + 1) * 64, qb0:qb0 + width],
                                        start=True, stop=True, tile_position=(hh * 64, 0)), inc="ts")
                            c.mark(pfx + f"qk{g}_{kt_seq}", "ts")
                            if len(pends) >= 2:
                                retire(pends.pop(0))
                            if idx == 2 and normq:
                                emit_normalize_qc(c, pfx, kind, *normq.pop(0))
                            c.wait("scalar", pfx + f"qk{g}_{kt_seq}")
                            c.wait("scalar", pfx + "negC")
                            if kt_seq >= 2:
                                c.wait("scalar", pfx + f"avP{g}_{kt_seq - 2}")
                            c.do("scalar", lambda e, sl=sl: e.activation(
                                out=regA[:, A_P + sl * 1024:A_P + (sl + 1) * 1024],
                                in_=ar[:, sl * 1024:(sl + 1) * 1024],
                                func=AF.Exp, bias=negC, scale=scale), inc="ss")
                            c.mark(pfx + f"exp{g}_{kt_seq}", "ss")
                            if masked:
                                r = ktb * ktb_ranks + rr
                                jk = r if j % 2 == 0 else 7 - r
                                c.wait("gpsimd", pfx + f"exp{g}_{kt_seq}")
                                c.wait("gpsimd", "mask01_ready")
                                for hh in range(2):
                                    c.do("gpsimd", lambda e, sl=sl, hh=hh, off=off, jk=jk: e.tensor_tensor(
                                        out=regA[:, A_P + sl * 1024 + hh * 512 + off:
                                                 A_P + sl * 1024 + hh * 512 + off + 256],
                                        in0=regA[:, A_P + sl * 1024 + hh * 512 + off:
                                                 A_P + sl * 1024 + hh * 512 + off + 256],
                                        in1=mask01[:, jk * 256:(jk + 1) * 256],
                                        op=ALU.mult), inc="ps")
                                c.mark(pfx + f"msk{g}_{kt_seq}", "ps")
                            pends.append((kt_seq, sl, masked, off, width, base, rr, j,
                                          qc, ktb, idx == 0, idx == len(kts) - 1))
                            kt_seq += 1
                    while pends:
                        retire(pends.pop(0))
                    c.mark(pfx + f"avdone_g{g}k{ktb}", "ts")
                    if ktb + 2 < nktb:
                        emit_slab_dma(g, ktb + 2)
            while normq:
                emit_normalize_qc(c, pfx, kind, *normq.pop(0))
            c.mark(pfx + "attn_done", "vs")

        def emit_av(c, pfx, kind, g, kt_seq, sl, masked, off, width, base, rr, j, qc,
                    ktb, is_first, is_last):
            isffn = kind == FFN
            c.wait("tensor", pfx + f"slabV{'A' if j < 8 else 'B'}_g{g}k{ktb}")
            c.wait("tensor", pfx + f"exp{g}_{kt_seq}")
            if masked:
                c.wait("tensor", pfx + f"msk{g}_{kt_seq}")
            if is_first and (ktb > 0 or qc > 0):
                c.wait("tensor", pfx + f"flush{g}_{ktb}_{qc - 1}" if qc > 0
                       else pfx + f"flush{g}_{ktb - 1}_3")
            elif is_first and g > 0:
                c.wait("tensor", pfx + f"flush{g - 1}_{(4 if isffn else 2) - 1}_3")
            if isffn:
                for hh in range(2):
                    c.do("tensor", lambda e, hh=hh, j=j, sl=sl, base=base, st=is_first: e.matmul(
                        ar[:, 2048 + hh * 512:2048 + (hh + 1) * 512],
                        regA[:, base + 4096 + hh * 2048 + j * 128:base + 4096 + hh * 2048 + (j + 1) * 128],
                        regA[:, A_P + sl * 1024 + hh * 512:A_P + sl * 1024 + (hh + 1) * 512],
                        start=st, stop=False), inc="ts")
                for hh in range(2):
                    c.do("tensor", lambda e, hh=hh, sl=sl, st=is_first: e.matmul(
                        ar[hh * 32:(hh + 1) * 32, 3072:3584],
                        ones32,
                        regA[:, A_P + sl * 1024 + hh * 512:A_P + sl * 1024 + (hh + 1) * 512],
                        start=st, stop=False, tile_position=(0, hh * 32)), inc="ts")
            else:
                c.wait("tensor", pfx + "vones")
                vb = base + 4096
                for hh in range(2):
                    lcol = vb + hh * 2080 + rr * 1040 + j * 65
                    c.do("tensor", lambda e, hh=hh, lcol=lcol, sl=sl, off=off, st=is_first: e.matmul(
                        ar[0:65, 2048 + hh * 512 + off:2048 + hh * 512 + 512],
                        regA[:, lcol:lcol + 65],
                        regA[:, A_P + sl * 1024 + hh * 512 + off:A_P + sl * 1024 + (hh + 1) * 512],
                        start=st, stop=False), inc="ts")
            c.mark(pfx + f"avP{g}_{kt_seq}", "ts")
            if is_last:
                c.wait("vector", pfx + f"avP{g}_{kt_seq}")
                first_blk = ktb == 0
                if first_blk and g > 0:
                    c.wait("vector", pfx + f"norm{g - 1}")
                if not first_blk:
                    c.wait("vector", pfx + f"flush{g}_{ktb - 1}_{qc}")
                if isffn:
                    for hh in range(2):
                        src = ar[:, 2048 + hh * 512:2048 + (hh + 1) * 512]
                        dst = AVacc[:, hh * 2048 + qc * 512:hh * 2048 + (qc + 1) * 512]
                        if first_blk:
                            c.do("vector", lambda e, src=src, dst=dst: e.tensor_copy(
                                out=dst, in_=src), inc="vs")
                        else:
                            c.do("vector", lambda e, src=src, dst=dst: e.tensor_tensor(
                                out=dst, in0=src, in1=dst, op=ALU.add), inc="vs")
                else:
                    pairs = [
                        (ar[0:64, 2048:2560], AVacc[0:64, qc * 512:(qc + 1) * 512]),
                        (ar[0:64, 2560:3072], AVacc[64:128, qc * 512:(qc + 1) * 512]),
                        (ar[64:65, 2048:2560], sumsSB[0:1, qc * 512:(qc + 1) * 512]),
                        (ar[64:65, 2560:3072], sumsSB[32:33, qc * 512:(qc + 1) * 512]),
                    ]
                    for src, dst in pairs:
                        if first_blk:
                            c.do("vector", lambda e, src=src, dst=dst: e.tensor_copy(
                                out=dst, in_=src), inc="vs")
                        else:
                            c.do("vector", lambda e, src=src, dst=dst: e.tensor_tensor(
                                out=dst, in0=src, in1=dst, op=ALU.add), inc="vs")
                if isffn:
                    ssrc = ar[0:64, 3072:3584]
                    sdst = sumsSB[0:64, qc * 512:(qc + 1) * 512]
                    if first_blk:
                        c.do("vector", lambda e, ssrc=ssrc, sdst=sdst: e.tensor_copy(
                            out=sdst, in_=ssrc), inc="vs")
                    else:
                        c.do("vector", lambda e, ssrc=ssrc, sdst=sdst: e.tensor_tensor(
                            out=sdst, in0=ssrc, in1=sdst, op=ALU.add), inc="vs")
                c.mark(pfx + f"flush{g}_{ktb}_{qc}", "vs")

        def emit_normalize_recip(c, pfx, kind, g, qc):
            isffn = kind == FFN
            nktb = 4 if isffn else 2
            c.wait("vector", pfx + f"flush{g}_{nktb - 1}_{qc}")
            c.do("vector", lambda e, qc=qc: e.reciprocal(
                out=sumsSB[:, qc * 512:(qc + 1) * 512],
                in_=sumsSB[:, qc * 512:(qc + 1) * 512]), inc="vs")
            c.mark(pfx + f"recip{g}_{qc}", "vs")

        def emit_normalize_qc(c, pfx, kind, g, qc):
            isffn = kind == FFN
            PS = 3584
            if isffn:
                for hh in range(2):
                    c.wait("tensor", pfx + f"recip{g}_{qc}")
                    prev = (pfx + f"nm{g}_{qc}_{hh - 1}" if hh == 1 else
                            (pfx + f"nm{g}_{qc - 1}_1" if qc > 0 else
                             (pfx + f"nm{g - 1}_3_1" if g > 0 else None)))
                    if prev is not None:
                        c.wait("tensor", prev)
                    c.do("tensor", lambda e, hh=hh, qc=qc: e.matmul(
                        ar[:, PS:PS + 512],
                        onesf[hh * 32:hh * 32 + 1, :],
                        sumsSB[hh * 32:hh * 32 + 1, qc * 512:(qc + 1) * 512],
                        start=True, stop=True,
                        tile_position=(hh * 32, 0)), inc="ts")
                    c.mark(pfx + f"nbc{g}_{qc}_{hh}", "ts")
                    c.wait("vector", pfx + f"nbc{g}_{qc}_{hh}")
                    c.do("vector", lambda e, hh=hh, qc=qc: e.tensor_tensor(
                        out=regB[:, (2 * g + hh) * TOK + qc * 512:
                                 (2 * g + hh) * TOK + (qc + 1) * 512],
                        in0=ar[:, PS:PS + 512],
                        in1=AVacc[:, hh * 2048 + qc * 512:hh * 2048 + (qc + 1) * 512],
                        op=ALU.mult), inc="vs")
                    c.mark(pfx + f"nm{g}_{qc}_{hh}", "vs")
            else:
                c.wait("tensor", pfx + f"recip{g}_{qc}")
                prev = (pfx + f"nm{g}_{qc - 1}_1" if qc > 0 else
                        (pfx + f"nm{g - 1}_3_1" if g > 0 else None))
                if prev is not None:
                    c.wait("tensor", prev)
                c.do("tensor", lambda e, qc=qc: e.matmul(
                    ar[0:64, PS:PS + 512],
                    onesf[0:1, 0:64],
                    sumsSB[0:1, qc * 512:(qc + 1) * 512],
                    start=True, stop=True, tile_position=(0, 0)), inc="ts")
                c.do("tensor", lambda e, qc=qc: e.matmul(
                    ar[64:128, PS:PS + 512],
                    onesf[32:33, 0:64],
                    sumsSB[32:33, qc * 512:(qc + 1) * 512],
                    start=True, stop=True, tile_position=(32, 64)), inc="ts")
                c.mark(pfx + f"nbc{g}_{qc}_1", "ts")
                c.wait("vector", pfx + f"nbc{g}_{qc}_1")
                c.do("vector", lambda e, qc=qc: e.tensor_tensor(
                    out=regB[:, g * TOK + qc * 512:g * TOK + (qc + 1) * 512],
                    in0=ar[:, PS:PS + 512],
                    in1=AVacc[:, qc * 512:(qc + 1) * 512],
                    op=ALU.mult), inc="vs")
                c.mark(pfx + f"nm{g}_{qc}_1", "vs")
            if qc == 3:
                c.mark(pfx + f"norm{g}", "vs")

        def emit_phase3(c, si, kind, pfx, nm_tc, mout_tc, WO):
            isffn = kind == FFN
            load_w_slabs(c, pfx + "o", WO,
                         (pfx + "qwf7", pfx + "attn_done"))
            if isffn:
                c.wait("gpsimd", pfx + "vev31")
                c.dma("gpsimd", vbias, fobR)
                c.mark(pfx + "obr_in", "qg")
                c.wait("scalar", pfx + "obr_in")
                c.do("scalar", lambda e: e.mul(out=vbias, in_=vbias, mul=0.5), inc="ss")
                c.mark(pfx + "obhalf", "ss")
            npo = 0
            for tc in range(4):
                mo = mout_tc[tc]
                nmch = nm_tc[tc] if isffn else 8
                noch = (mo + 511) // 512
                for tb in range(tc * 4, tc * 4 + 4):
                    if isffn:
                        c.wait("vector", pfx + "owf7")
                        c.wait("vector", pfx + "obhalf")
                        c.dof("vector", lambda e, tb=tb: e.tensor_scalar(
                            out=wstage, in0=x_res[:, tb * D:(tb + 1) * D],
                            scalar1=small[:, C_MUSAVE + tb:C_MUSAVE + tb + 1],
                            scalar2=small[:, C_R1SAVE + tb:C_R1SAVE + tb + 1],
                            op0=ALU.subtract, op1=ALU.mult))
                        c.dof("vector", lambda e, tb=tb: e.scalar_tensor_tensor(
                            out=x_res[:, tb * D:(tb + 1) * D], in0=wstage, scalar=0.5,
                            in1=x_res[:, tb * D:(tb + 1) * D],
                            op0=ALU.mult, op1=ALU.add))
                        c.dof("vector", lambda e, tb=tb, mo=mo: e.tensor_tensor(
                            out=x_res[:, tb * D:tb * D + mo], in0=vbias[:, 0:mo],
                            in1=x_res[:, tb * D:tb * D + mo], op=ALU.add))
                    for och in range(noch):
                        w = min(512, mo - och * 512)
                        par = npo % 2
                        if npo >= 2:
                            c.wait("tensor", pfx + f"oev{npo - 2}")
                        c.wait("tensor", pfx + "attn_done")
                        for m in range(nmch):
                            c.wait("tensor", pfx + f"owf{m}")
                            c.do("tensor", lambda e, m=m, tb=tb, och=och, par=par, w=w, nmch=nmch: e.matmul(
                                ar[:, par * 512:par * 512 + w],
                                regB[:, m * TOK + tb * 128:m * TOK + (tb + 1) * 128],
                                regA[:, A_WF + m * D + och * 512:A_WF + m * D + och * 512 + w],
                                start=(m == 0), stop=(m == nmch - 1)), inc="ts")
                        c.mark(pfx + f"omm{npo}", "ts")
                        c.wait("vector", pfx + f"omm{npo}")
                        if isffn:
                            c.dof("vector", lambda e, par=par, tb=tb, och=och, w=w: e.scalar_tensor_tensor(
                                out=x_res[:, tb * D + och * 512:tb * D + och * 512 + w],
                                in0=ar[:, par * 512:par * 512 + w], scalar=0.5,
                                in1=x_res[:, tb * D + och * 512:tb * D + och * 512 + w],
                                op0=ALU.mult, op1=ALU.add))
                        else:
                            c.dof("vector", lambda e, par=par, tb=tb, och=och, w=w: e.tensor_tensor(
                                out=x_res[:, tb * D + och * 512:tb * D + och * 512 + w],
                                in0=ar[:, par * 512:par * 512 + w],
                                in1=x_res[:, tb * D + och * 512:tb * D + och * 512 + w],
                                op=ALU.add))
                        c.mark(pfx + f"oev{npo}", "vs")
                        npo += 1
                    c.mark(pfx + f"res{tb}", "vs")
            c.mark("x_ready", "vs")

        def emit_all(c: Ctx):
            c.dma("sync", x_res.rearrange("p (t f) -> p t f", f=D),
                  x_in.rearrange("(t p) f -> p t f", p=128))
            c.mark("x_in_dma", "qs")
            for dst, src in ((ident, ident_c), (identf, identf_c),
                             (ones32, ones32_c), (onesf, onesf_c),
                             (ehF, ehF_c), (ehA, ehA_c), (mask01, maskin)):
                c.dma("gpsimd", dst, src)
            c.mark("consts", "qg")
            c.mark("mask01_ready", "qg")
            c.wait("vector", "x_in_dma")
            c.do("vector", lambda e: e.memset(tiny, 0.0), inc="vs")
            c.mark("x_ready", "vs")


            for si, kind in enumerate(sublayers):
                emit_sublayer(c, si, kind)
                if si in taps:
                    c.wait("sync", "x_ready")
                    c.dma("sync", taps[si].rearrange("(t p) f -> p t f", p=128),
                          x_res.rearrange("p (t f) -> p t f", f=D))
                    c.mark(f"tap{si}", "qs")

            c.wait("sync", "x_ready")
            c.dma("sync", x_out.rearrange("(t p) f -> p t f", p=128),
                  x_res.rearrange("p (t f) -> p t f", f=D))
            if DEBUG_DUMPS:
                dump_srcs = {"kloc0": klocA[0], "vloc0": vlocA[0], "kg0": kgA[0],
                             "vg0": vgA[0], "bndg0": bndg[0], "QT": QT,
                             "sums": sumsSB, "AVacc": AVacc, "attnT": regB,
                             "negC": negC, "x_res_d": x_res}
                for nm in DEBUG_DUMPS:
                    c.dma("sync", dumps_out[nm], dump_srcs[nm])
            c.mark("out_stored", "qs")
            c.wait("sync", "out_stored")

        blk.sync(lambda eng: emit_all(Ctx("sync", eng, sems)))
        blk.tensor(lambda eng: emit_all(Ctx("tensor", eng, sems)))
        blk.vector(lambda eng: emit_all(Ctx("vector", eng, sems)))
        blk.scalar(lambda eng: emit_all(Ctx("scalar", eng, sems)))
        blk.gpsimd(lambda eng: emit_all(Ctx("gpsimd", eng, sems)))

    return nc


def _mask_block(jk, cloc):
    m = np.zeros((128, 256), np.float32)
    for blkcol, tq in enumerate((cloc, 7 - cloc)):
        sl = slice(blkcol * 128, (blkcol + 1) * 128)
        if tq > jk:
            m[:, sl] = 1.0
        elif tq == jk:
            k = np.arange(128)[:, None]
            q = np.arange(128)[None, :]
            m[:, sl] = (k <= q).astype(np.float32)
    return m


def kernel(L0_attn_x, L0_attn_Wq, L0_attn_bq, L0_attn_Wk, L0_attn_bk, L0_attn_Wv,
           L0_attn_bv, L0_attn_Wo, L0_ffn_router_prob, L0_ffn_norm_w, L0_ffn_norm_b,
           L0_ffn_q_w, L0_ffn_q_b, L0_ffn_k_w, L0_ffn_k_b, L0_ffn_v_w, L0_ffn_v_b,
           L0_ffn_o_w, L0_ffn_o_b):
    x = np.ascontiguousarray(np.asarray(L0_attn_x, np.float32))
    B, T, _ = x.shape
    xt = x.reshape(B, 64, 128, D)
    assert np.asarray(L0_ffn_router_prob).shape[-1] == 4

    tileT = lambda v: np.ascontiguousarray(np.asarray(v, np.float32).reshape(8, 128).T)
    rep = lambda v: np.ascontiguousarray(
        np.broadcast_to(np.asarray(v, np.float32), (128, D)))
    tr = lambda w: np.ascontiguousarray(np.asarray(w, np.float32).T)
    natw = lambda w: np.ascontiguousarray(np.asarray(w, np.float32))

    common = dict(
        aWq=natw(L0_attn_Wq), aWk=natw(L0_attn_Wk), aWv=natw(L0_attn_Wv),
        aWo=natw(L0_attn_Wo),
        abqT=tileT(L0_attn_bq), abkT=tileT(L0_attn_bk), abvR=rep(L0_attn_bv),
        fqwT=tr(L0_ffn_q_w), fkwT=tr(L0_ffn_k_w), fvwT=tr(L0_ffn_v_w),
        fowT=tr(L0_ffn_o_w),
        fqbT=tileT(L0_ffn_q_b), fkbT=tileT(L0_ffn_k_b), fvbR=rep(L0_ffn_v_b),
        fnwT=tileT(L0_ffn_norm_w), fnbT=tileT(L0_ffn_norm_b), fobR=rep(L0_ffn_o_b),
    )

    in_maps = []
    for cidx in range(8):
        b, cloc = divmod(cidx, 4)
        tm = tmap(cloc)
        xs = np.ascontiguousarray(xt[b, tm].reshape(TOK, D))
        mk = np.concatenate([_mask_block(jk, cloc) for jk in range(8)], axis=1)
        im = dict(common)
        im["x_in"] = xs
        im["maskin"] = np.ascontiguousarray(mk)
        in_maps.append(im)

    nc = build_program()
    res = run_bass_kernel_spmd(nc, in_maps, core_ids=list(range(8)))

    out = np.empty((B, 64, 128, D), np.float32)
    for cidx in range(8):
        b, cloc = divmod(cidx, 4)
        tm = tmap(cloc)
        out[b, tm] = res.results[cidx]["x_out"].reshape(NTILE, 128, D)
    return out.reshape(B, T, D)


# version 06
# speedup vs baseline: 1.0552x; 1.0552x over previous
import numpy as np
import contextlib
import concourse.bass as bass
import concourse.mybir as mybir
from concourse.bass_utils import run_bass_kernel_spmd
import ml_dtypes

F32 = mybir.dt.float32
BF16 = mybir.dt.bfloat16
AF = mybir.ActivationFunctionType
ALU = mybir.AluOpType
AX = mybir.AxisListType

D = 1024
NTILE = 16
TOK = 2048
EPS = 1e-5
FFN = "ffn"
ATT = "attn"

DEBUG_TAPS = ()
DEBUG_DUMPS = ()


def tmap(cloc):
    return sorted([8 * B + cloc for B in range(8)] + [8 * B + 7 - cloc for B in range(8)])


class Ctx:

    def __init__(self, me, eng, sems):
        self.me = me
        self.eng = eng
        self.sems = sems
        self.cnt = {k: 0 for k in sems}
        self.marks = {}
        self.lastw = {}

    def do(self, engname, fn, inc=None, n=1):
        inst = None
        if self.me == engname:
            inst = fn(self.eng)
        if inc is not None:
            self.cnt[inc] += n
            if inst is not None:
                inst.then_inc(self.sems[inc], n)

    SELF = {"vector": "vs", "scalar": "ss", "tensor": "ts", "gpsimd": "ps"}

    def dof(self, engname, fn):
        sem = Ctx.SELF[engname]
        self.do(engname, fn, inc=sem)
        if self.me == engname:
            self.eng.wait_ge(self.sems[sem], self.cnt[sem])
            self.lastw[sem] = self.cnt[sem]

    def dma(self, engname, out, in_, **kw):
        q = "qs" if engname == "sync" else "qg"
        self.do(engname, lambda e: e.dma_start(out=out, in_=in_, **kw), inc=q, n=16)

    def mark(self, name, sem):
        self.marks[name] = (sem, self.cnt[sem])

    def wait(self, engname, markname):
        sem, val = self.marks[markname]
        if self.me == engname and val > 0 and self.lastw.get(sem, -1) < val:
            self.eng.wait_ge(self.sems[sem], val)
            self.lastw[sem] = val


def build_program(sublayers=(FFN, ATT, FFN), debug_taps=None):
    if debug_taps is None:
        debug_taps = DEBUG_TAPS
    nc = bass.Bass("TRN2", target_bir_lowering=False)

    x_in = nc.dram_tensor("x_in", [TOK, D], F32, kind="ExternalInput").ap()
    x_out = nc.dram_tensor("x_out", [TOK, D], F32, kind="ExternalOutput").ap()
    taps = {i: nc.dram_tensor(f"x_tap{i}", [TOK, D], F32, kind="ExternalOutput").ap()
            for i in debug_taps}
    _dump_shapes = {"kloc0": ([D, TOK // 2], BF16), "vloc0": ([TOK // 2, D], BF16),
                    "kg0": ([4 * D, TOK // 2], BF16), "vg0": ([2 * TOK, D], BF16),
                    "bndg0": ([128, 1], F32), "QT": ([128, 16384], BF16),
                    "sums": ([64, 2048], F32), "AVacc": ([128, 4096], F32),
                    "attnT": ([128, 16384], BF16), "negC": ([128, 1], F32),
                    "x_res_d": ([128, NTILE * D], F32)}
    dumps_out = {nm: nc.dram_tensor(f"dump_{nm}", _dump_shapes[nm][0],
                                    _dump_shapes[nm][1], kind="ExternalOutput").ap()
                 for nm in DEBUG_DUMPS}

    aWq = nc.dram_tensor("aWq", [D, D], F32, kind="ExternalInput").ap()
    aWk = nc.dram_tensor("aWk", [D, D], F32, kind="ExternalInput").ap()
    aWv = nc.dram_tensor("aWv", [D, D], F32, kind="ExternalInput").ap()
    aWo = nc.dram_tensor("aWo", [D, D], F32, kind="ExternalInput").ap()
    abqT = nc.dram_tensor("abqT", [128, 8], F32, kind="ExternalInput").ap()
    abkT = nc.dram_tensor("abkT", [128, 8], F32, kind="ExternalInput").ap()
    abvR = nc.dram_tensor("abvR", [128, D], F32, kind="ExternalInput").ap()
    fqwT = nc.dram_tensor("fqwT", [D, D], F32, kind="ExternalInput").ap()
    fkwT = nc.dram_tensor("fkwT", [D, D], F32, kind="ExternalInput").ap()
    fvwT = nc.dram_tensor("fvwT", [D, D], F32, kind="ExternalInput").ap()
    fowT = nc.dram_tensor("fowT", [D, D], F32, kind="ExternalInput").ap()
    fqbT = nc.dram_tensor("fqbT", [128, 8], F32, kind="ExternalInput").ap()
    fkbT = nc.dram_tensor("fkbT", [128, 8], F32, kind="ExternalInput").ap()
    fvbR = nc.dram_tensor("fvbR", [128, D], F32, kind="ExternalInput").ap()
    fnwT = nc.dram_tensor("fnwT", [128, 8], F32, kind="ExternalInput").ap()
    fnbT = nc.dram_tensor("fnbT", [128, 8], F32, kind="ExternalInput").ap()
    fobR = nc.dram_tensor("fobR", [128, D], F32, kind="ExternalInput").ap()
    maskin = nc.dram_tensor("maskin", [128, 8 * 256], F32, kind="ExternalInput").ap()

    ident_c = nc.inline_tensor(np.eye(128, dtype=ml_dtypes.bfloat16), name="ident_c").ap()
    identf_c = nc.inline_tensor(np.eye(128, dtype=np.float32), name="identf_c").ap()
    o32 = np.zeros((128, 32), dtype=ml_dtypes.bfloat16); o32[:, 0] = 1
    ones32_c = nc.inline_tensor(o32, name="ones32_c").ap()
    onesf_c = nc.inline_tensor(np.ones((128, 128), np.float32), name="onesf_c").ap()
    ehF_np = np.zeros((128, 256), dtype=ml_dtypes.bfloat16)
    for h in range(8):
        ehF_np[:, h * 32 + h] = 1
    ehF_c = nc.inline_tensor(ehF_np, name="ehF_c").ap()
    ehA_np = np.zeros((128, 256), dtype=ml_dtypes.bfloat16)
    for o in range(8):
        ehA_np[0:64, o * 32 + 2 * o] = 1
        ehA_np[64:128, o * 32 + 2 * o + 1] = 1
    ehA_c = nc.inline_tensor(ehA_np, name="ehA_c").ap()

    NS = len(sublayers)
    HT = TOK // 2
    klocA = [nc.dram_tensor(f"klocA{s}", [D, HT], BF16, kind="Internal").ap() for s in range(NS)]
    klocB = [nc.dram_tensor(f"klocB{s}", [D, HT], BF16, kind="Internal").ap() for s in range(NS)]
    vlocA = [nc.dram_tensor(f"vlocA{s}", [HT, D], BF16, kind="Internal").ap() for s in range(NS)]
    vlocB = [nc.dram_tensor(f"vlocB{s}", [HT, D], BF16, kind="Internal").ap() for s in range(NS)]
    kgA = [nc.dram_tensor(f"kgA{s}", [4 * D, HT], BF16, kind="Internal").ap() for s in range(NS)]
    kgB = [nc.dram_tensor(f"kgB{s}", [4 * D, HT], BF16, kind="Internal").ap() for s in range(NS)]
    vgA = [nc.dram_tensor(f"vgA{s}", [4 * HT, D], BF16, kind="Internal").ap() for s in range(NS)]
    vgB = [nc.dram_tensor(f"vgB{s}", [4 * HT, D], BF16, kind="Internal").ap() for s in range(NS)]
    bnd = [nc.dram_tensor(f"bnd{s}", [32, 1], F32, kind="Internal").ap() for s in range(NS)]
    bndg = [nc.dram_tensor(f"bndg{s}", [128, 1], F32, kind="Internal").ap() for s in range(NS)]

    S = lambda n, sh, dt=F32: nc.alloc_sbuf_tensor(n, sh, dt).ap()
    x_res = S("x_res", [128, NTILE * D])
    regA = S("regA", [128, 18560], BF16)
    regB = S("regB", [128, 16384], BF16)
    QT = S("QT", [128, 16384], BF16)
    AVacc = S("AVacc", [128, 4096])
    sumsSB = S("sumsSB", [64, 2048])
    wstage = S("wstage", [128, 1024])
    vbias = S("vbias", [128, D])
    biasQK = S("biasQK", [128, 136])
    bqkv = S("bqkv", [128, 24])
    sqbuf = S("sqbuf", [128, 512], BF16)
    small = S("small", [128, 96])
    kqmax = S("kqmax", [64, 1])
    kngT = S("kngT", [64, 4])
    negC = S("negC", [128, 1])
    tiny = S("tiny", [4, 4])
    ident = S("ident", [128, 128], BF16)
    identf = S("identf", [128, 128])
    ones32 = S("ones32", [128, 32], BF16)
    onesf = S("onesf", [128, 128])
    ehF = S("ehF", [128, 256], BF16)
    ehA = S("ehA", [128, 256], BF16)
    mask01 = S("mask01", [128, 8 * 256], BF16)

    A_WF = 0
    A_KST = 8192
    A_VST = 12288
    A_SLAB = 0
    A_P = 16512

    ar = nc.alloc_psum_tensor("ar", [128, 4096], F32).ap()

    C_LNS, C_MU, C_VAR, C_R1, C_R12, C_T1, C_T2 = 0, 1, 2, 3, 4, 5, 6
    C_MUSAVE, C_R1SAVE = 16, 32

    rg = [[0, 1, 2, 3], [4, 5, 6, 7]]

    with contextlib.ExitStack() as ctx_stack:
        sems = {}
        for name in ("qs", "qg", "cc", "ts", "vs", "ss", "ps"):
            sems[name] = ctx_stack.enter_context(nc.semaphore(name))
        blk = ctx_stack.enter_context(nc.Block())

        def load_w_slabs(c, tag, wt, gates, scale=None):
            sync_gate, dve_gate = gates if gates else (None, None)
            for m in range(8):
                if m >= 1:
                    c.wait("sync", tag + f"wf{m - 1}")
                if m == 0 and sync_gate is not None:
                    c.wait("sync", sync_gate)
                c.dma("sync", wstage, wt[m * 128:(m + 1) * 128, :])
                c.mark(tag + f"wl{m}", "qs")
                c.wait("vector", tag + f"wl{m}")
                if m == 0 and dve_gate is not None:
                    c.wait("vector", dve_gate)
                if scale is None:
                    c.do("vector", lambda e, m=m: e.tensor_copy(
                        out=regA[:, A_WF + m * D:A_WF + (m + 1) * D],
                        in_=wstage), inc="vs")
                else:
                    c.do("vector", lambda e, m=m: e.tensor_scalar_mul(
                        out=regA[:, A_WF + m * D:A_WF + (m + 1) * D],
                        in0=wstage, scalar1=scale), inc="vs")
                c.mark(tag + f"wf{m}", "vs")

        def emit_sublayer(c, si, kind):
            pfx = f"s{si}_"
            isffn = kind == FFN
            scale = (1.0 / 32.0) if isffn else (1.0 / 8.0)
            nm_tc = [8, 4, 2, 1] if isffn else [8, 8, 8, 8]
            mout_tc = [1024, 512, 256, 128] if isffn else [1024] * 4
            WQ, WK, WV, WO = (fqwT, fkwT, fvwT, fowT) if isffn else (aWq, aWk, aWv, aWo)
            BQ, BK, BVR = (fqbT, fkbT, fvbR) if isffn else (abqT, abkT, abvR)

            c.dma("gpsimd", bqkv[:, 0:8], BQ)
            c.dma("gpsimd", bqkv[:, 8:16], BK)
            if isffn:
                c.dma("gpsimd", bqkv[:, 16:24], fnbT)
                c.dma("gpsimd", biasQK[:, 128:136], fnwT)
            c.mark(pfx + "bias_in", "qg")

            c.wait("vector", "x_ready")
            c.wait("vector", "x_ready_p")
            if isffn:
                c.wait("vector", pfx + "bias_in")
            for tb in range(NTILE):
                c.dof("vector", lambda e, tb=tb: e.reduce_sum(
                    out=small[:, tb:tb + 1], in_=x_res[:, tb * D:(tb + 1) * D],
                    axis=AX.X))
            c.dof("vector", lambda e: e.tensor_scalar_mul(
                out=small[:, 16:32], in0=small[:, 0:16], scalar1=1.0 / D))
            for tb in range(NTILE):
                c.dof("vector", lambda e, tb=tb: e.scalar_tensor_tensor(
                    out=wstage, in0=x_res[:, tb * D:(tb + 1) * D],
                    scalar=small[:, 16 + tb:17 + tb],
                    in1=x_res[:, tb * D:(tb + 1) * D],
                    op0=ALU.subtract, op1=ALU.mult,
                    accum_out=small[:, tb:tb + 1]))
            c.dof("vector", lambda e: e.tensor_scalar(
                out=small[:, 0:16], in0=small[:, 0:16],
                scalar1=1.0 / D, scalar2=EPS, op0=ALU.mult, op1=ALU.add))
            c.mark(pfx + "lnv", "vs")
            c.wait("scalar", pfx + "lnv")
            c.dof("scalar", lambda e: e.activation(
                out=small[:, 48:64], in_=small[:, 0:16], func=AF.Ln))
            c.dof("scalar", lambda e: e.activation(
                out=small[:, C_R1SAVE:C_R1SAVE + 16], in_=small[:, 48:64],
                func=AF.Exp, scale=-0.5))
            c.mark(pfx + "r1b", "ss")
            if isffn:
                c.wait("vector", pfx + "r1b")
                c.dof("vector", lambda e: e.reciprocal(
                    out=small[:, 48:64], in_=small[:, 0:16]))
                c.dof("vector", lambda e: e.tensor_scalar(
                    out=small[:, 0:16], in0=small[:, 0:16],
                    scalar1=-EPS, scalar2=None, op0=ALU.add))
                c.dof("vector", lambda e: e.tensor_tensor(
                    out=small[:, 0:16], in0=small[:, 0:16],
                    in1=small[:, 48:64], op=ALU.mult))
                c.dof("vector", lambda e: e.tensor_scalar(
                    out=small[:, 0:16], in0=small[:, 0:16],
                    scalar1=EPS, scalar2=None, op0=ALU.add))
                c.mark(pfx + "vyb", "vs")
                c.wait("scalar", pfx + "vyb")
                c.dof("scalar", lambda e: e.activation(
                    out=small[:, 48:64], in_=small[:, 0:16], func=AF.Ln))
                c.dof("scalar", lambda e: e.activation(
                    out=small[:, 0:16], in_=small[:, 48:64],
                    func=AF.Exp, scale=-0.5))
                c.mark(pfx + "r2b", "ss")
                c.wait("vector", pfx + "r2b")
                c.dof("vector", lambda e: e.tensor_tensor(
                    out=small[:, 0:16], in0=small[:, C_R1SAVE:C_R1SAVE + 16],
                    in1=small[:, 0:16], op=ALU.mult))
                c.dof("vector", lambda e: e.tensor_scalar(
                    out=small[:, 64:80], in0=small[:, C_R1SAVE:C_R1SAVE + 16],
                    scalar1=0.5, scalar2=1.0, op0=ALU.mult, op1=ALU.add))
                c.dof("vector", lambda e: e.tensor_tensor(
                    out=small[:, 80:96], in0=small[:, 16:32],
                    in1=small[:, C_R1SAVE:C_R1SAVE + 16], op=ALU.mult))
                c.dof("vector", lambda e: e.tensor_scalar_mul(
                    out=small[:, 80:96], in0=small[:, 80:96], scalar1=-0.5))
            else:
                c.wait("vector", pfx + "r1b")
            zt_n = 0
            for tb in range(NTILE):
                if tb >= 1:
                    c.wait("vector", pfx + f"ztt{(tb - 1) * 8 + 7}")
                sc2 = (small[:, tb:tb + 1] if isffn
                       else small[:, C_R1SAVE + tb:C_R1SAVE + tb + 1])
                c.dof("vector", lambda e, tb=tb, sc2=sc2: e.tensor_scalar(
                    out=wstage, in0=x_res[:, tb * D:(tb + 1) * D],
                    scalar1=small[:, 16 + tb:17 + tb], scalar2=sc2,
                    op0=ALU.subtract, op1=ALU.mult))
                c.mark(pfx + f"zp{tb}", "vs")
                for mt in range(8):
                    sl = zt_n % 2
                    if zt_n >= 2:
                        c.wait("tensor", pfx + f"zte{zt_n - 2}")
                    c.wait("tensor", pfx + f"zp{tb}")
                    c.do("tensor", lambda e, mt=mt, sl=sl: e.transpose(
                        ar[:, sl * 512:sl * 512 + 128],
                        wstage[:, mt * 128:(mt + 1) * 128], identf), inc="ts")
                    c.mark(pfx + f"ztt{zt_n}", "ts")
                    c.wait("vector", pfx + f"ztt{zt_n}")
                    if isffn:
                        c.do("vector", lambda e, mt=mt, sl=sl, tb=tb: e.tensor_scalar(
                            out=regB[:, mt * TOK + tb * 128:mt * TOK + (tb + 1) * 128],
                            in0=ar[:, sl * 512:sl * 512 + 128],
                            scalar1=biasQK[:, 128 + mt:129 + mt],
                            scalar2=bqkv[:, 16 + mt:17 + mt],
                            op0=ALU.mult, op1=ALU.add), inc="vs")
                    else:
                        c.do("vector", lambda e, mt=mt, sl=sl, tb=tb: e.tensor_copy(
                            out=regB[:, mt * TOK + tb * 128:mt * TOK + (tb + 1) * 128],
                            in_=ar[:, sl * 512:sl * 512 + 128]), inc="vs")
                    c.mark(pfx + f"zte{zt_n}", "vs")
                    zt_n += 1
            c.mark(pfx + "zT_done", "vs")

            load_w_slabs(c, pfx + "v", WV, (pfx + "zT_done", "x_ready"))
            nvm = 0
            for tc in range(4):
                nm = nm_tc[tc]
                if tc == 0:
                    c.wait("gpsimd", "x_ready")
                    c.dma("gpsimd", vbias, BVR)
                    c.mark(pfx + "vbias0", "qg")
                for tb in range(tc * 4, tc * 4 + 4):
                    vslot = A_VST + (tb % 2) * D
                    if tb >= 2:
                        c.wait("vector", pfx + f"vw{tb - 2}")
                    for oh in range(2):
                        par = nvm % 2
                        if nvm >= 2:
                            c.wait("tensor", pfx + f"vev{nvm - 2}")
                        for m in range(nm):
                            c.wait("tensor", pfx + f"vwf{m}")
                            c.do("tensor", lambda e, m=m, tb=tb, oh=oh, par=par, nm=nm: e.matmul(
                                ar[:, 1024 + par * 512:1024 + (par + 1) * 512],
                                regB[:, m * TOK + tb * 128:m * TOK + (tb + 1) * 128],
                                regA[:, A_WF + m * D + oh * 512:A_WF + m * D + (oh + 1) * 512],
                                start=(m == 0), stop=(m == nm - 1)), inc="ts")
                        c.mark(pfx + f"vmm{nvm}", "ts")
                        c.wait("vector", pfx + f"vmm{nvm}")
                        c.wait("vector", pfx + "vbias0")
                        c.do("vector", lambda e, par=par, oh=oh, vslot=vslot: e.tensor_tensor(
                            out=regA[:, vslot + oh * 512:vslot + (oh + 1) * 512],
                            in0=ar[:, 1024 + par * 512:1024 + (par + 1) * 512],
                            in1=vbias[:, oh * 512:(oh + 1) * 512], op=ALU.add), inc="vs")
                        c.mark(pfx + f"vev{nvm}", "vs")
                        nvm += 1
                    c.wait("gpsimd", pfx + f"vev{nvm - 1}")
                    vdst = vlocA[si] if tb < 8 else vlocB[si]
                    c.dma("gpsimd", vdst[(tb % 8) * 128:(tb % 8 + 1) * 128, :],
                          regA[:, vslot:vslot + D])
                    c.mark(pfx + f"vw{tb}", "qg")
                    if tb == 7:
                        c.mark(pfx + "vwA", "qg")
            c.mark(pfx + "vwB", "qg")
            c.mark(pfx + "v_p1_done", "ts")

            load_w_slabs(c, pfx + "k", WK, (pfx + "vwf7", pfx + "v_p1_done"))
            c.wait("vector", pfx + "bias_in")
            c.do("vector", lambda e: e.tensor_copy(
                out=biasQK[:, 8:16], in_=bqkv[:, 8:16]), inc="vs")
            c.mark(pfx + "kbias", "vs")
            c.wait("tensor", pfx + "zT_done")
            eh = ehF if isffn else ehA
            nmm = 0
            for tc in range(4):
                nm = nm_tc[tc]
                for o in range(8):
                    par = nmm % 2
                    if nmm >= 2:
                        c.wait("tensor", pfx + f"kev{nmm - 2}")
                    for m in range(nm):
                        c.wait("tensor", pfx + f"kwf{m}")
                        c.do("tensor", lambda e, o=o, m=m, tc=tc, par=par, nm=nm: e.matmul(
                            ar[:, par * 512:(par + 1) * 512],
                            regA[:, A_WF + m * D + o * 128:A_WF + m * D + (o + 1) * 128],
                            regB[:, m * TOK + tc * 512:m * TOK + (tc + 1) * 512],
                            start=(m == 0), stop=(m == nm - 1)), inc="ts")
                    c.mark(pfx + f"kmm{nmm}", "ts")
                    c.wait("scalar", pfx + f"kmm{nmm}")
                    c.wait("scalar", pfx + "kbias")
                    if nmm >= 8:
                        c.wait("scalar", pfx + f"kw{nmm - 8}")
                        c.wait("scalar", pfx + f"ksq{nmm - 8}")
                    c.do("scalar", lambda e, o=o, par=par: e.activation(
                        out=regA[:, A_KST + o * 512:A_KST + (o + 1) * 512],
                        in_=ar[:, par * 512:(par + 1) * 512],
                        func=AF.Identity, bias=biasQK[:, 8 + o:9 + o]), inc="ss")
                    c.mark(pfx + f"kev{nmm}", "ss")
                    c.wait("vector", pfx + f"kev{nmm}")
                    if nmm > 0:
                        c.wait("vector", pfx + f"knmm{nmm - 1}")
                    c.do("vector", lambda e, o=o: e.tensor_tensor(
                        out=sqbuf,
                        in0=regA[:, A_KST + o * 512:A_KST + (o + 1) * 512],
                        in1=regA[:, A_KST + o * 512:A_KST + (o + 1) * 512],
                        op=ALU.mult), inc="vs")
                    c.mark(pfx + f"ksq{nmm}", "vs")
                    c.wait("tensor", pfx + f"ksq{nmm}")
                    c.wait("tensor", "consts")
                    if o == 0 and tc > 0:
                        c.wait("tensor", pfx + f"kqmax{tc - 1}")
                    c.do("tensor", lambda e, o=o, eh=eh: e.matmul(
                        ar[0:32, 2048:2560], eh[:, o * 32:(o + 1) * 32], sqbuf,
                        start=(o == 0), stop=(o == 7),
                        tile_position=(0, 0)), inc="ts")
                    c.mark(pfx + f"knmm{nmm}", "ts")
                    c.wait("gpsimd", pfx + f"kev{nmm}")
                    kdst = klocA[si] if tc < 2 else klocB[si]
                    c.dma("gpsimd",
                          kdst[o * 128:(o + 1) * 128, (tc % 2) * 512:(tc % 2 + 1) * 512],
                          regA[:, A_KST + o * 512:A_KST + (o + 1) * 512])
                    c.mark(pfx + f"kw{nmm}", "qg")
                    if nmm == 15:
                        c.mark(pfx + "kwA", "qg")
                    nmm += 1
                c.wait("vector", pfx + f"knmm{nmm - 1}")
                c.dof("vector", lambda e: e.reduce_max(
                    out=sumsSB[0:32, 0:1], in_=ar[0:32, 2048:2560], axis=AX.X))
                if tc == 0:
                    c.dof("vector", lambda e: e.tensor_copy(
                        out=kqmax[0:32, 0:1], in_=sumsSB[0:32, 0:1]))
                else:
                    c.dof("vector", lambda e: e.tensor_max(
                        out=kqmax[0:32, 0:1], in0=kqmax[0:32, 0:1],
                        in1=sumsSB[0:32, 0:1]))
                c.mark(pfx + f"kqmax{tc}", "vs")
            c.mark(pfx + "kwB", "qg")
            c.mark(pfx + "k_p2_done", "ts")
            c.wait("vector", pfx + "kqmax3")
            c.dof("vector", lambda e: e.tensor_copy(
                out=sumsSB[0:32, 1:2], in_=kqmax[0:32, 0:1]))
            c.mark(pfx + "knloc", "vs")
            c.wait("gpsimd", pfx + "knloc")
            c.dma("gpsimd", bnd[si], sumsSB[0:32, 1:2])
            c.mark(pfx + "bndw", "qg")
            c.wait("gpsimd", pfx + "bndw")
            c.wait("gpsimd", pfx + "kwB")
            c.wait("gpsimd", pfx + "vwB")
            c.do("gpsimd", lambda e: e.collective_compute(
                "AllGather", ALU.bypass, replica_groups=rg,
                ins=[bnd[si].opt()], outs=[bndg[si].opt()]), inc="cc")
            c.mark(pfx + "cc_bnd", "cc")
            c.do("gpsimd", lambda e: e.collective_compute(
                "AllGather", ALU.bypass, replica_groups=rg,
                ins=[klocA[si].opt()], outs=[kgA[si].opt()]), inc="cc")
            c.mark(pfx + "cc_kA", "cc")
            c.do("gpsimd", lambda e: e.collective_compute(
                "AllGather", ALU.bypass, replica_groups=rg,
                ins=[klocB[si].opt()], outs=[kgB[si].opt()]), inc="cc")
            c.mark(pfx + "cc_kB", "cc")
            c.do("gpsimd", lambda e: e.collective_compute(
                "AllGather", ALU.bypass, replica_groups=rg,
                ins=[vlocA[si].opt()], outs=[vgA[si].opt()]), inc="cc")
            c.mark(pfx + "cc_vA", "cc")
            c.do("gpsimd", lambda e: e.collective_compute(
                "AllGather", ALU.bypass, replica_groups=rg,
                ins=[vlocB[si].opt()], outs=[vgB[si].opt()]), inc="cc")
            c.mark(pfx + "cc_vB", "cc")
            c.wait("gpsimd", pfx + "cc_bnd")
            c.dma("gpsimd", kngT[32:64, 0:4],
                  bndg[si].rearrange("(r h) c -> h (r c)", h=32),
                  allow_slow_non_contiguous=True)
            c.mark(pfx + "bnd_rd", "qg")
            c.wait("vector", pfx + "bnd_rd")
            c.dof("vector", lambda e: e.reduce_max(
                out=kngT[32:64, 0:1], in_=kngT[32:64, 0:4], axis=AX.X))
            c.mark(pfx + "knglob", "vs")

            load_w_slabs(c, pfx + "q", WQ, (pfx + "kwf7", pfx + "k_p2_done"))
            c.wait("vector", pfx + "bias_in")
            c.do("vector", lambda e: e.tensor_copy(
                out=biasQK[:, 0:8], in_=bqkv[:, 0:8]), inc="vs")
            c.mark(pfx + "qbias", "vs")
            c.wait("tensor", pfx + "zT_done")
            nmm = 0
            for tc in range(4):
                nm = nm_tc[tc]
                for o in range(8):
                    par = nmm % 2
                    if nmm >= 2:
                        c.wait("tensor", pfx + f"qev{nmm - 2}")
                    else:
                        c.wait("tensor", pfx + f"kev{30 + nmm}")
                    for m in range(nm):
                        c.wait("tensor", pfx + f"qwf{m}")
                        c.do("tensor", lambda e, o=o, m=m, tc=tc, par=par, nm=nm: e.matmul(
                            ar[:, par * 512:(par + 1) * 512],
                            regA[:, A_WF + m * D + o * 128:A_WF + m * D + (o + 1) * 128],
                            regB[:, m * TOK + tc * 512:m * TOK + (tc + 1) * 512],
                            start=(m == 0), stop=(m == nm - 1)), inc="ts")
                    c.mark(pfx + f"qmm{nmm}", "ts")
                    c.wait("scalar", pfx + f"qmm{nmm}")
                    c.wait("scalar", pfx + "qbias")
                    c.do("scalar", lambda e, o=o, tc=tc, par=par: e.activation(
                        out=QT[:, o * TOK + tc * 512:o * TOK + (tc + 1) * 512],
                        in_=ar[:, par * 512:(par + 1) * 512],
                        func=AF.Identity, bias=biasQK[:, o:o + 1]), inc="ss")
                    c.mark(pfx + f"qev{nmm}", "ss")
                    nmm += 1
                for o in range(8):
                    if o == 0:
                        c.wait("vector", pfx + f"qev{tc * 8 + 7}")
                        c.wait("vector", pfx + ("knmm31" if tc == 0
                                                else f"qnmm{tc - 1}_7"))
                    else:
                        c.wait("vector", pfx + f"qnmm{tc}_{o - 1}")
                    c.do("vector", lambda e, o=o, tc=tc: e.tensor_tensor(
                        out=sqbuf,
                        in0=QT[:, o * TOK + tc * 512:o * TOK + (tc + 1) * 512],
                        in1=QT[:, o * TOK + tc * 512:o * TOK + (tc + 1) * 512],
                        op=ALU.mult), inc="vs")
                    c.mark(pfx + f"qsq{tc}_{o}", "vs")
                    c.wait("tensor", pfx + f"qsq{tc}_{o}")
                    if tc == 0 and o == 0:
                        c.wait("tensor", pfx + "kqmax3")
                    if o == 0 and tc > 0:
                        c.wait("tensor", pfx + f"qqmax{tc - 1}")
                    c.do("tensor", lambda e, o=o, eh=eh: e.matmul(
                        ar[32:64, 2048:2560], eh[:, o * 32:(o + 1) * 32], sqbuf,
                        start=(o == 0), stop=(o == 7),
                        tile_position=(0, 32)), inc="ts")
                    c.mark(pfx + f"qnmm{tc}_{o}", "ts")
                c.wait("vector", pfx + f"qnmm{tc}_7")
                c.dof("vector", lambda e: e.reduce_max(
                    out=sumsSB[32:64, 0:1], in_=ar[32:64, 2048:2560], axis=AX.X))
                if tc == 0:
                    c.dof("vector", lambda e: e.tensor_copy(
                        out=kqmax[32:64, 0:1], in_=sumsSB[32:64, 0:1]))
                else:
                    c.dof("vector", lambda e: e.tensor_max(
                        out=kqmax[32:64, 0:1], in0=kqmax[32:64, 0:1],
                        in1=sumsSB[32:64, 0:1]))
                c.mark(pfx + f"qqmax{tc}", "vs")
            c.mark(pfx + "q_p2_done", "ts")

            c.wait("vector", pfx + "qqmax3")
            c.wait("vector", pfx + "knglob")
            c.dof("vector", lambda e: e.tensor_tensor(
                out=kngT[32:64, 1:2], in0=kqmax[32:64, 0:1],
                in1=kngT[32:64, 0:1], op=ALU.add))
            c.mark(pfx + "cmb", "vs")
            c.wait("tensor", pfx + "cmb")
            c.wait("tensor", pfx + "q_p2_done")
            c.do("tensor", lambda e: e.transpose(
                ar[0:1, 2560:2592], kngT[32:64, 1:2], identf[32:64, 32:64]),
                inc="ts")
            c.mark(pfx + "cmbT", "ts")
            c.wait("vector", pfx + "cmbT")
            c.dof("vector", lambda e: e.reduce_max(
                out=tiny[0:1, 0:1], in_=ar[0:1, 2560:2592], axis=AX.X))
            c.dof("vector", lambda e, scale=scale: e.tensor_scalar_mul(
                out=tiny[0:1, 1:2], in0=tiny[0:1, 0:1], scalar1=-0.5 * scale))
            c.mark(pfx + "c11", "vs")
            c.wait("tensor", pfx + "c11")
            c.do("tensor", lambda e: e.matmul(
                ar[:, 2560:2561], onesf[0:1, :], tiny[0:1, 1:2],
                start=True, stop=True), inc="ts")
            c.mark(pfx + "cbc", "ts")
            c.wait("vector", pfx + "cbc")
            c.do("vector", lambda e: e.tensor_copy(out=negC, in_=ar[:, 2560:2561]), inc="vs")
            c.mark(pfx + "negC", "vs")

            emit_attention(c, si, kind, pfx, scale)

            emit_phase3(c, si, kind, pfx, nm_tc, mout_tc, WO)

        def emit_attention(c, si, kind, pfx, scale):
            isffn = kind == FFN
            ngrp = 4 if isffn else 8
            nktb = 4 if isffn else 2
            ktb_ranks = 1 if isffn else 2
            SLABW = 8192 if isffn else 8256
            if not isffn:
                for gate in ("q_p2_done", "ksq31", "vev31", "kwB", "vwB"):
                    c.wait("vector", pfx + gate)
                for par in range(2):
                    for hh in range(2):
                        for rr in range(2):
                            vr = A_SLAB + par * SLABW + 4096 + hh * 2080 + rr * 1040
                            c.do("vector", lambda e, vr=vr: e.memset(
                                regA[:, vr:vr + 1040]
                                .rearrange("p (j d) -> p j d", d=65)[:, :, 64:65],
                                1.0), inc="vs")
                c.mark(pfx + "vones", "vs")

            def emit_slab_dma(g, ktb):
                    par = ktb % 2
                    base = A_SLAB + par * SLABW
                    seq = g * nktb + ktb
                    if seq == 0:
                        c.wait("sync", pfx + "q_p2_done")
                    elif seq == 1:
                        c.wait("sync", pfx + "ksq31")
                        c.wait("sync", pfx + "vev31")
                        c.wait("sync", pfx + "kwB")
                        c.wait("sync", pfx + "vwB")
                    else:
                        pg, pk = divmod(seq - 2, nktb)
                        c.wait("sync", pfx + f"avdone_g{pg}k{pk}")
                    for half, ccm in (("A", "cc_vA"), ("B", "cc_vB")):
                        c.wait("sync", pfx + ccm)
                        vgh = (vgA if half == "A" else vgB)[si]
                        ho = 0 if half == "A" else 1024
                        for rr in range(ktb_ranks):
                            r = ktb * ktb_ranks + rr
                            for hh in range(2):
                                h = g * 2 + hh
                                if isffn:
                                    c.dma("sync",
                                          regA[:, base + 4096 + hh * 2048 + ho:
                                               base + 4096 + hh * 2048 + ho + 1024]
                                          .rearrange("p (j d) -> p j d", d=128),
                                          vgh[r * HT:(r + 1) * HT, h * 128:(h + 1) * 128]
                                          .rearrange("(j p) d -> p j d", p=128))
                                else:
                                    vr = (base + 4096 + hh * 2080 + rr * 1040 +
                                          (0 if half == "A" else 520))
                                    c.dma("sync",
                                          regA[:, vr:vr + 520]
                                          .rearrange("p (j d) -> p j d", d=65)[:, :, 0:64],
                                          vgh[r * HT:(r + 1) * HT, h * 64:(h + 1) * 64]
                                          .rearrange("(j p) d -> p j d", p=128))
                        c.mark(pfx + f"slabV{half}_g{g}k{ktb}", "qs")
                    for half, ccm in (("A", "cc_kA"), ("B", "cc_kB")):
                        c.wait("sync", pfx + ccm)
                        kgh = (kgA if half == "A" else kgB)[si]
                        ho = 0 if half == "A" else 1024
                        for rr in range(ktb_ranks):
                            r = ktb * ktb_ranks + rr
                            for hh in range(2):
                                h = g * 2 + hh
                                if isffn:
                                    c.dma("sync",
                                          regA[:, base + hh * 2048 + ho:
                                               base + hh * 2048 + ho + 1024],
                                          kgh[r * D + h * 128:r * D + (h + 1) * 128, :])
                                else:
                                    c.dma("sync",
                                          regA[hh * 64:(hh + 1) * 64,
                                               base + rr * 2048 + ho:
                                               base + rr * 2048 + ho + 1024],
                                          kgh[r * D + h * 64:r * D + (h + 1) * 64, :])
                        c.mark(pfx + f"slabK{half}_g{g}k{ktb}", "qs")

            normq = []
            for g in range(ngrp):
                def retire(p, g=g):
                    emit_av(c, pfx, kind, g, *p)
                    if p[11] and p[9] == nktb - 1:
                        emit_normalize_recip(c, pfx, kind, g, p[8])
                        normq.append((g, p[8]))
                emit_slab_dma(g, 0)
                if nktb > 1:
                    emit_slab_dma(g, 1)
                kt_seq = 0
                pends = []
                for ktb in range(nktb):
                    par = ktb % 2
                    base = A_SLAB + par * SLABW
                    for qc in range(4):
                        kts = []
                        for rr in range(ktb_ranks):
                            for j in range(16):
                                if isffn or j < 8 or qc >= j // 4:
                                    kts.append((rr, j))
                        for idx, (rr, j) in enumerate(kts):
                            sl = kt_seq % 2
                            masked = (not isffn) and j >= 8 and qc == j // 4
                            off = 256 if (masked and (j // 2) % 2 == 1) else 0
                            width = 512 - off
                            c.wait("tensor", pfx + f"slabK{'A' if j < 8 else 'B'}_g{g}k{ktb}")
                            c.wait("tensor", pfx + "negC")
                            if kt_seq >= 2:
                                c.wait("tensor", pfx + f"exp{g}_{kt_seq - 2}")
                            if isffn:
                                for hh in range(2):
                                    c.do("tensor", lambda e, hh=hh, j=j, sl=sl, qc=qc, base=base, g=g: e.matmul(
                                        ar[:, sl * 1024 + hh * 512:sl * 1024 + (hh + 1) * 512],
                                        regA[:, base + hh * 2048 + j * 128:base + hh * 2048 + (j + 1) * 128],
                                        QT[:, (g * 2 + hh) * TOK + qc * 512:(g * 2 + hh) * TOK + (qc + 1) * 512],
                                        start=True, stop=True), inc="ts")
                            else:
                                kb = base + rr * 2048 + j * 128
                                qb0 = g * TOK + qc * 512 + off
                                for hh in range(2):
                                    c.do("tensor", lambda e, kb=kb, hh=hh, sl=sl, qb0=qb0, off=off, width=width: e.matmul(
                                        ar[:, sl * 1024 + hh * 512 + off:sl * 1024 + (hh + 1) * 512],
                                        regA[hh * 64:(hh + 1) * 64, kb:kb + 128],
                                        QT[hh * 64:(hh + 1) * 64, qb0:qb0 + width],
                                        start=True, stop=True, tile_position=(hh * 64, 0)), inc="ts")
                            c.mark(pfx + f"qk{g}_{kt_seq}", "ts")
                            if len(pends) >= 2:
                                retire(pends.pop(0))
                            if idx == 2 and normq:
                                emit_normalize_qc(c, pfx, kind, *normq.pop(0))
                            c.wait("scalar", pfx + f"qk{g}_{kt_seq}")
                            c.wait("scalar", pfx + "negC")
                            if kt_seq >= 2:
                                c.wait("scalar", pfx + f"avP{g}_{kt_seq - 2}")
                            c.do("scalar", lambda e, sl=sl: e.activation(
                                out=regA[:, A_P + sl * 1024:A_P + (sl + 1) * 1024],
                                in_=ar[:, sl * 1024:(sl + 1) * 1024],
                                func=AF.Exp, bias=negC, scale=scale), inc="ss")
                            c.mark(pfx + f"exp{g}_{kt_seq}", "ss")
                            if masked:
                                r = ktb * ktb_ranks + rr
                                jk = r if j % 2 == 0 else 7 - r
                                c.wait("gpsimd", pfx + f"exp{g}_{kt_seq}")
                                c.wait("gpsimd", "mask01_ready")
                                for hh in range(2):
                                    c.do("gpsimd", lambda e, sl=sl, hh=hh, off=off, jk=jk: e.tensor_tensor(
                                        out=regA[:, A_P + sl * 1024 + hh * 512 + off:
                                                 A_P + sl * 1024 + hh * 512 + off + 256],
                                        in0=regA[:, A_P + sl * 1024 + hh * 512 + off:
                                                 A_P + sl * 1024 + hh * 512 + off + 256],
                                        in1=mask01[:, jk * 256:(jk + 1) * 256],
                                        op=ALU.mult), inc="ps")
                                c.mark(pfx + f"msk{g}_{kt_seq}", "ps")
                            pends.append((kt_seq, sl, masked, off, width, base, rr, j,
                                          qc, ktb, idx == 0, idx == len(kts) - 1))
                            kt_seq += 1
                    while pends:
                        retire(pends.pop(0))
                    c.mark(pfx + f"avdone_g{g}k{ktb}", "ts")
                    if ktb + 2 < nktb:
                        emit_slab_dma(g, ktb + 2)
            while normq:
                emit_normalize_qc(c, pfx, kind, *normq.pop(0))
            c.mark(pfx + "attn_done", "vs")

        def emit_av(c, pfx, kind, g, kt_seq, sl, masked, off, width, base, rr, j, qc,
                    ktb, is_first, is_last):
            isffn = kind == FFN
            c.wait("tensor", pfx + f"slabV{'A' if j < 8 else 'B'}_g{g}k{ktb}")
            c.wait("tensor", pfx + f"exp{g}_{kt_seq}")
            if masked:
                c.wait("tensor", pfx + f"msk{g}_{kt_seq}")
            if is_first and (ktb > 0 or qc > 0):
                c.wait("tensor", pfx + f"flush{g}_{ktb}_{qc - 1}" if qc > 0
                       else pfx + f"flush{g}_{ktb - 1}_3")
            elif is_first and g > 0:
                c.wait("tensor", pfx + f"flush{g - 1}_{(4 if isffn else 2) - 1}_3")
            if isffn:
                for hh in range(2):
                    c.do("tensor", lambda e, hh=hh, j=j, sl=sl, base=base, st=is_first: e.matmul(
                        ar[:, 2048 + hh * 512:2048 + (hh + 1) * 512],
                        regA[:, base + 4096 + hh * 2048 + j * 128:base + 4096 + hh * 2048 + (j + 1) * 128],
                        regA[:, A_P + sl * 1024 + hh * 512:A_P + sl * 1024 + (hh + 1) * 512],
                        start=st, stop=False), inc="ts")
                for hh in range(2):
                    c.do("tensor", lambda e, hh=hh, sl=sl, st=is_first: e.matmul(
                        ar[hh * 32:(hh + 1) * 32, 3072:3584],
                        ones32,
                        regA[:, A_P + sl * 1024 + hh * 512:A_P + sl * 1024 + (hh + 1) * 512],
                        start=st, stop=False, tile_position=(0, hh * 32)), inc="ts")
            else:
                c.wait("tensor", pfx + "vones")
                vb = base + 4096
                for hh in range(2):
                    lcol = vb + hh * 2080 + rr * 1040 + j * 65
                    c.do("tensor", lambda e, hh=hh, lcol=lcol, sl=sl, off=off, st=is_first: e.matmul(
                        ar[0:65, 2048 + hh * 512 + off:2048 + hh * 512 + 512],
                        regA[:, lcol:lcol + 65],
                        regA[:, A_P + sl * 1024 + hh * 512 + off:A_P + sl * 1024 + (hh + 1) * 512],
                        start=st, stop=False), inc="ts")
            c.mark(pfx + f"avP{g}_{kt_seq}", "ts")
            if is_last:
                c.wait("vector", pfx + f"avP{g}_{kt_seq}")
                first_blk = ktb == 0
                if first_blk and g > 0:
                    c.wait("vector", pfx + f"norm{g - 1}")
                if not first_blk:
                    c.wait("vector", pfx + f"flush{g}_{ktb - 1}_{qc}")
                if isffn:
                    for hh in range(2):
                        src = ar[:, 2048 + hh * 512:2048 + (hh + 1) * 512]
                        dst = AVacc[:, hh * 2048 + qc * 512:hh * 2048 + (qc + 1) * 512]
                        if first_blk:
                            c.do("vector", lambda e, src=src, dst=dst: e.tensor_copy(
                                out=dst, in_=src), inc="vs")
                        else:
                            c.do("vector", lambda e, src=src, dst=dst: e.tensor_tensor(
                                out=dst, in0=src, in1=dst, op=ALU.add), inc="vs")
                else:
                    pairs = [
                        (ar[0:64, 2048:2560], AVacc[0:64, qc * 512:(qc + 1) * 512]),
                        (ar[0:64, 2560:3072], AVacc[64:128, qc * 512:(qc + 1) * 512]),
                        (ar[64:65, 2048:2560], sumsSB[0:1, qc * 512:(qc + 1) * 512]),
                        (ar[64:65, 2560:3072], sumsSB[32:33, qc * 512:(qc + 1) * 512]),
                    ]
                    for src, dst in pairs:
                        if first_blk:
                            c.do("vector", lambda e, src=src, dst=dst: e.tensor_copy(
                                out=dst, in_=src), inc="vs")
                        else:
                            c.do("vector", lambda e, src=src, dst=dst: e.tensor_tensor(
                                out=dst, in0=src, in1=dst, op=ALU.add), inc="vs")
                if isffn:
                    ssrc = ar[0:64, 3072:3584]
                    sdst = sumsSB[0:64, qc * 512:(qc + 1) * 512]
                    if first_blk:
                        c.do("vector", lambda e, ssrc=ssrc, sdst=sdst: e.tensor_copy(
                            out=sdst, in_=ssrc), inc="vs")
                    else:
                        c.do("vector", lambda e, ssrc=ssrc, sdst=sdst: e.tensor_tensor(
                            out=sdst, in0=ssrc, in1=sdst, op=ALU.add), inc="vs")
                c.mark(pfx + f"flush{g}_{ktb}_{qc}", "vs")

        def emit_normalize_recip(c, pfx, kind, g, qc):
            isffn = kind == FFN
            nktb = 4 if isffn else 2
            c.wait("vector", pfx + f"flush{g}_{nktb - 1}_{qc}")
            c.do("vector", lambda e, qc=qc: e.reciprocal(
                out=sumsSB[:, qc * 512:(qc + 1) * 512],
                in_=sumsSB[:, qc * 512:(qc + 1) * 512]), inc="vs")
            c.mark(pfx + f"recip{g}_{qc}", "vs")

        def emit_normalize_qc(c, pfx, kind, g, qc):
            isffn = kind == FFN
            PS = 3584
            if isffn:
                for hh in range(2):
                    c.wait("tensor", pfx + f"recip{g}_{qc}")
                    prev = (pfx + f"nm{g}_{qc}_{hh - 1}" if hh == 1 else
                            (pfx + f"nm{g}_{qc - 1}_1" if qc > 0 else
                             (pfx + f"nm{g - 1}_3_1" if g > 0 else None)))
                    if prev is not None:
                        c.wait("tensor", prev)
                    c.do("tensor", lambda e, hh=hh, qc=qc: e.matmul(
                        ar[:, PS:PS + 512],
                        onesf[hh * 32:hh * 32 + 1, :],
                        sumsSB[hh * 32:hh * 32 + 1, qc * 512:(qc + 1) * 512],
                        start=True, stop=True,
                        tile_position=(hh * 32, 0)), inc="ts")
                    c.mark(pfx + f"nbc{g}_{qc}_{hh}", "ts")
                    c.wait("vector", pfx + f"nbc{g}_{qc}_{hh}")
                    c.do("vector", lambda e, hh=hh, qc=qc: e.tensor_tensor(
                        out=regB[:, (2 * g + hh) * TOK + qc * 512:
                                 (2 * g + hh) * TOK + (qc + 1) * 512],
                        in0=ar[:, PS:PS + 512],
                        in1=AVacc[:, hh * 2048 + qc * 512:hh * 2048 + (qc + 1) * 512],
                        op=ALU.mult), inc="vs")
                    c.mark(pfx + f"nm{g}_{qc}_{hh}", "vs")
            else:
                c.wait("tensor", pfx + f"recip{g}_{qc}")
                prev = (pfx + f"nm{g}_{qc - 1}_1" if qc > 0 else
                        (pfx + f"nm{g - 1}_3_1" if g > 0 else None))
                if prev is not None:
                    c.wait("tensor", prev)
                c.do("tensor", lambda e, qc=qc: e.matmul(
                    ar[0:64, PS:PS + 512],
                    onesf[0:1, 0:64],
                    sumsSB[0:1, qc * 512:(qc + 1) * 512],
                    start=True, stop=True, tile_position=(0, 0)), inc="ts")
                c.do("tensor", lambda e, qc=qc: e.matmul(
                    ar[64:128, PS:PS + 512],
                    onesf[32:33, 0:64],
                    sumsSB[32:33, qc * 512:(qc + 1) * 512],
                    start=True, stop=True, tile_position=(32, 64)), inc="ts")
                c.mark(pfx + f"nbc{g}_{qc}_1", "ts")
                c.wait("vector", pfx + f"nbc{g}_{qc}_1")
                c.do("vector", lambda e, qc=qc: e.tensor_tensor(
                    out=regB[:, g * TOK + qc * 512:g * TOK + (qc + 1) * 512],
                    in0=ar[:, PS:PS + 512],
                    in1=AVacc[:, qc * 512:(qc + 1) * 512],
                    op=ALU.mult), inc="vs")
                c.mark(pfx + f"nm{g}_{qc}_1", "vs")
            if qc == 3:
                c.mark(pfx + f"norm{g}", "vs")

        def emit_phase3(c, si, kind, pfx, nm_tc, mout_tc, WO):
            isffn = kind == FFN
            load_w_slabs(c, pfx + "o", WO,
                         (pfx + "qwf7", pfx + "attn_done"),
                         scale=(0.5 if isffn else None))
            if isffn:
                c.wait("gpsimd", pfx + "vev31")
                c.dma("gpsimd", vbias, fobR)
                c.mark(pfx + "obr_in", "qg")
                c.wait("scalar", pfx + "obr_in")
                c.do("scalar", lambda e: e.mul(out=vbias, in_=vbias, mul=0.5), inc="ss")
                c.mark(pfx + "obhalf", "ss")
            npo = 0
            for tc in range(4):
                mo = mout_tc[tc]
                nmch = nm_tc[tc] if isffn else 8
                noch = (mo + 511) // 512
                for tb in range(tc * 4, tc * 4 + 4):
                    if isffn:
                        c.wait("vector", pfx + "owf7")
                        c.wait("vector", pfx + "obhalf")
                        c.dof("vector", lambda e, tb=tb: e.tensor_scalar(
                            out=x_res[:, tb * D:(tb + 1) * D],
                            in0=x_res[:, tb * D:(tb + 1) * D],
                            scalar1=small[:, 64 + tb:65 + tb],
                            scalar2=small[:, 80 + tb:81 + tb],
                            op0=ALU.mult, op1=ALU.add))
                        c.dof("vector", lambda e, tb=tb, mo=mo: e.tensor_tensor(
                            out=x_res[:, tb * D:tb * D + mo], in0=vbias[:, 0:mo],
                            in1=x_res[:, tb * D:tb * D + mo], op=ALU.add))
                        c.mark(pfx + f"pre{tb}", "vs")
                    for och in range(noch):
                        w = min(512, mo - och * 512)
                        par = npo % 2
                        if npo >= 2:
                            c.wait("tensor", pfx + f"oev{npo - 2}")
                        c.wait("tensor", pfx + "attn_done")
                        for m in range(nmch):
                            c.wait("tensor", pfx + f"owf{m}")
                            c.do("tensor", lambda e, m=m, tb=tb, och=och, par=par, w=w, nmch=nmch: e.matmul(
                                ar[:, par * 512:par * 512 + w],
                                regB[:, m * TOK + tb * 128:m * TOK + (tb + 1) * 128],
                                regA[:, A_WF + m * D + och * 512:A_WF + m * D + och * 512 + w],
                                start=(m == 0), stop=(m == nmch - 1)), inc="ts")
                        c.mark(pfx + f"omm{npo}", "ts")
                        c.wait("vector", pfx + f"omm{npo}")
                        c.do("vector", lambda e, par=par, tb=tb, och=och, w=w: e.tensor_tensor(
                            out=x_res[:, tb * D + och * 512:tb * D + och * 512 + w],
                            in0=ar[:, par * 512:par * 512 + w],
                            in1=x_res[:, tb * D + och * 512:tb * D + och * 512 + w],
                            op=ALU.add), inc="vs")
                        c.mark(pfx + f"oev{npo}", "vs")
                        npo += 1
            c.mark("x_ready", "vs")
            c.mark("x_ready_p", "ps")

        def emit_all(c: Ctx):
            c.dma("sync", x_res.rearrange("p (t f) -> p t f", f=D),
                  x_in.rearrange("(t p) f -> p t f", p=128))
            c.mark("x_in_dma", "qs")
            for dst, src in ((ident, ident_c), (identf, identf_c),
                             (ones32, ones32_c), (onesf, onesf_c),
                             (ehF, ehF_c), (ehA, ehA_c), (mask01, maskin)):
                c.dma("gpsimd", dst, src)
            c.mark("consts", "qg")
            c.mark("mask01_ready", "qg")
            c.wait("vector", "x_in_dma")
            c.do("vector", lambda e: e.memset(tiny, 0.0), inc="vs")
            c.mark("x_ready", "vs")
            c.mark("x_ready_p", "ps")


            for si, kind in enumerate(sublayers):
                emit_sublayer(c, si, kind)
                if si in taps:
                    c.wait("sync", "x_ready")
                    c.dma("sync", taps[si].rearrange("(t p) f -> p t f", p=128),
                          x_res.rearrange("p (t f) -> p t f", f=D))
                    c.mark(f"tap{si}", "qs")

            c.wait("sync", "x_ready")
            c.wait("sync", "x_ready_p")
            c.dma("sync", x_out.rearrange("(t p) f -> p t f", p=128),
                  x_res.rearrange("p (t f) -> p t f", f=D))
            if DEBUG_DUMPS:
                dump_srcs = {"kloc0": klocA[0], "vloc0": vlocA[0], "kg0": kgA[0],
                             "vg0": vgA[0], "bndg0": bndg[0], "QT": QT,
                             "sums": sumsSB, "AVacc": AVacc, "attnT": regB,
                             "negC": negC, "x_res_d": x_res}
                for nm in DEBUG_DUMPS:
                    c.dma("sync", dumps_out[nm], dump_srcs[nm])
            c.mark("out_stored", "qs")
            c.wait("sync", "out_stored")

        blk.sync(lambda eng: emit_all(Ctx("sync", eng, sems)))
        blk.tensor(lambda eng: emit_all(Ctx("tensor", eng, sems)))
        blk.vector(lambda eng: emit_all(Ctx("vector", eng, sems)))
        blk.scalar(lambda eng: emit_all(Ctx("scalar", eng, sems)))
        blk.gpsimd(lambda eng: emit_all(Ctx("gpsimd", eng, sems)))

    return nc


def _mask_block(jk, cloc):
    m = np.zeros((128, 256), np.float32)
    for blkcol, tq in enumerate((cloc, 7 - cloc)):
        sl = slice(blkcol * 128, (blkcol + 1) * 128)
        if tq > jk:
            m[:, sl] = 1.0
        elif tq == jk:
            k = np.arange(128)[:, None]
            q = np.arange(128)[None, :]
            m[:, sl] = (k <= q).astype(np.float32)
    return m


def kernel(L0_attn_x, L0_attn_Wq, L0_attn_bq, L0_attn_Wk, L0_attn_bk, L0_attn_Wv,
           L0_attn_bv, L0_attn_Wo, L0_ffn_router_prob, L0_ffn_norm_w, L0_ffn_norm_b,
           L0_ffn_q_w, L0_ffn_q_b, L0_ffn_k_w, L0_ffn_k_b, L0_ffn_v_w, L0_ffn_v_b,
           L0_ffn_o_w, L0_ffn_o_b):
    x = np.ascontiguousarray(np.asarray(L0_attn_x, np.float32))
    B, T, _ = x.shape
    xt = x.reshape(B, 64, 128, D)
    assert np.asarray(L0_ffn_router_prob).shape[-1] == 4

    tileT = lambda v: np.ascontiguousarray(np.asarray(v, np.float32).reshape(8, 128).T)
    rep = lambda v: np.ascontiguousarray(
        np.broadcast_to(np.asarray(v, np.float32), (128, D)))
    tr = lambda w: np.ascontiguousarray(np.asarray(w, np.float32).T)
    natw = lambda w: np.ascontiguousarray(np.asarray(w, np.float32))

    common = dict(
        aWq=natw(L0_attn_Wq), aWk=natw(L0_attn_Wk), aWv=natw(L0_attn_Wv),
        aWo=natw(L0_attn_Wo),
        abqT=tileT(L0_attn_bq), abkT=tileT(L0_attn_bk), abvR=rep(L0_attn_bv),
        fqwT=tr(L0_ffn_q_w), fkwT=tr(L0_ffn_k_w), fvwT=tr(L0_ffn_v_w),
        fowT=tr(L0_ffn_o_w),
        fqbT=tileT(L0_ffn_q_b), fkbT=tileT(L0_ffn_k_b), fvbR=rep(L0_ffn_v_b),
        fnwT=tileT(L0_ffn_norm_w), fnbT=tileT(L0_ffn_norm_b), fobR=rep(L0_ffn_o_b),
    )

    in_maps = []
    for cidx in range(8):
        b, cloc = divmod(cidx, 4)
        tm = tmap(cloc)
        xs = np.ascontiguousarray(xt[b, tm].reshape(TOK, D))
        mk = np.concatenate([_mask_block(jk, cloc) for jk in range(8)], axis=1)
        im = dict(common)
        im["x_in"] = xs
        im["maskin"] = np.ascontiguousarray(mk)
        in_maps.append(im)

    nc = build_program()
    res = run_bass_kernel_spmd(nc, in_maps, core_ids=list(range(8)))

    out = np.empty((B, 64, 128, D), np.float32)
    for cidx in range(8):
        b, cloc = divmod(cidx, 4)
        tm = tmap(cloc)
        out[b, tm] = res.results[cidx]["x_out"].reshape(NTILE, 128, D)
    return out.reshape(B, T, D)


# version 08
# speedup vs baseline: 1.0687x; 1.0127x over previous
import numpy as np
import contextlib
import concourse.bass as bass
import concourse.mybir as mybir
from concourse.bass_utils import run_bass_kernel_spmd
import ml_dtypes

F32 = mybir.dt.float32
BF16 = mybir.dt.bfloat16
AF = mybir.ActivationFunctionType
ALU = mybir.AluOpType
AX = mybir.AxisListType

D = 1024
NTILE = 16
TOK = 2048
EPS = 1e-5
FFN = "ffn"
ATT = "attn"

DEBUG_TAPS = ()
DEBUG_DUMPS = ()


def tmap(cloc):
    return sorted([8 * B + cloc for B in range(8)] + [8 * B + 7 - cloc for B in range(8)])


class Ctx:

    def __init__(self, me, eng, sems):
        self.me = me
        self.eng = eng
        self.sems = sems
        self.cnt = {k: 0 for k in sems}
        self.marks = {}
        self.lastw = {}

    def do(self, engname, fn, inc=None, n=1):
        inst = None
        if self.me == engname:
            inst = fn(self.eng)
        if inc is not None:
            self.cnt[inc] += n
            if inst is not None:
                inst.then_inc(self.sems[inc], n)

    SELF = {"vector": "vs", "scalar": "ss", "tensor": "ts", "gpsimd": "ps"}

    def dof(self, engname, fn):
        sem = Ctx.SELF[engname]
        self.do(engname, fn, inc=sem)
        if self.me == engname:
            self.eng.wait_ge(self.sems[sem], self.cnt[sem])
            self.lastw[sem] = self.cnt[sem]

    def dma(self, engname, out, in_, **kw):
        q = "qs" if engname == "sync" else "qg"
        self.do(engname, lambda e: e.dma_start(out=out, in_=in_, **kw), inc=q, n=16)

    def mark(self, name, sem):
        self.marks[name] = (sem, self.cnt[sem])

    def wait(self, engname, markname):
        sem, val = self.marks[markname]
        if self.me == engname and val > 0 and self.lastw.get(sem, -1) < val:
            self.eng.wait_ge(self.sems[sem], val)
            self.lastw[sem] = val


def build_program(sublayers=(FFN, ATT, FFN), debug_taps=None):
    if debug_taps is None:
        debug_taps = DEBUG_TAPS
    nc = bass.Bass("TRN2", target_bir_lowering=False)

    x_in = nc.dram_tensor("x_in", [TOK, D], F32, kind="ExternalInput").ap()
    x_out = nc.dram_tensor("x_out", [TOK, D], F32, kind="ExternalOutput").ap()
    taps = {i: nc.dram_tensor(f"x_tap{i}", [TOK, D], F32, kind="ExternalOutput").ap()
            for i in debug_taps}
    _dump_shapes = {"kloc0": ([D, TOK // 2], BF16), "vloc0": ([TOK // 2, D], BF16),
                    "kg0": ([4 * D, TOK // 2], BF16), "vg0": ([2 * TOK, D], BF16),
                    "bndg0": ([128, 1], F32), "QT": ([128, 16384], BF16),
                    "sums": ([64, 2048], F32), "AVacc": ([128, 4096], F32),
                    "attnT": ([128, 16384], BF16), "negC": ([128, 1], F32),
                    "x_res_d": ([128, NTILE * D], F32)}
    dumps_out = {nm: nc.dram_tensor(f"dump_{nm}", _dump_shapes[nm][0],
                                    _dump_shapes[nm][1], kind="ExternalOutput").ap()
                 for nm in DEBUG_DUMPS}

    aWq = nc.dram_tensor("aWq", [D, D], F32, kind="ExternalInput").ap()
    aWk = nc.dram_tensor("aWk", [D, D], F32, kind="ExternalInput").ap()
    aWv = nc.dram_tensor("aWv", [D, D], F32, kind="ExternalInput").ap()
    aWo = nc.dram_tensor("aWo", [D, D], F32, kind="ExternalInput").ap()
    abqT = nc.dram_tensor("abqT", [128, 8], F32, kind="ExternalInput").ap()
    abkT = nc.dram_tensor("abkT", [128, 8], F32, kind="ExternalInput").ap()
    abvR = nc.dram_tensor("abvR", [128, D], F32, kind="ExternalInput").ap()
    fqwT = nc.dram_tensor("fqwT", [D, D], F32, kind="ExternalInput").ap()
    fkwT = nc.dram_tensor("fkwT", [D, D], F32, kind="ExternalInput").ap()
    fvwT = nc.dram_tensor("fvwT", [D, D], F32, kind="ExternalInput").ap()
    fowT = nc.dram_tensor("fowT", [D, D], F32, kind="ExternalInput").ap()
    fqbT = nc.dram_tensor("fqbT", [128, 8], F32, kind="ExternalInput").ap()
    fkbT = nc.dram_tensor("fkbT", [128, 8], F32, kind="ExternalInput").ap()
    fvbR = nc.dram_tensor("fvbR", [128, D], F32, kind="ExternalInput").ap()
    fnwT = nc.dram_tensor("fnwT", [128, 8], F32, kind="ExternalInput").ap()
    fnbT = nc.dram_tensor("fnbT", [128, 8], F32, kind="ExternalInput").ap()
    fobR = nc.dram_tensor("fobR", [128, D], F32, kind="ExternalInput").ap()
    maskin = nc.dram_tensor("maskin", [128, 8 * 256], F32, kind="ExternalInput").ap()

    ident_c = nc.inline_tensor(np.eye(128, dtype=ml_dtypes.bfloat16), name="ident_c").ap()
    identf_c = nc.inline_tensor(np.eye(128, dtype=np.float32), name="identf_c").ap()
    o32 = np.zeros((128, 32), dtype=ml_dtypes.bfloat16); o32[:, 0] = 1
    ones32_c = nc.inline_tensor(o32, name="ones32_c").ap()
    onesf_c = nc.inline_tensor(np.ones((128, 128), np.float32), name="onesf_c").ap()
    ehF_np = np.zeros((128, 256), dtype=ml_dtypes.bfloat16)
    for h in range(8):
        ehF_np[:, h * 32 + h] = 1
    ehF_c = nc.inline_tensor(ehF_np, name="ehF_c").ap()
    ehA_np = np.zeros((128, 256), dtype=ml_dtypes.bfloat16)
    for o in range(8):
        ehA_np[0:64, o * 32 + 2 * o] = 1
        ehA_np[64:128, o * 32 + 2 * o + 1] = 1
    ehA_c = nc.inline_tensor(ehA_np, name="ehA_c").ap()

    NS = len(sublayers)
    HT = TOK // 2
    klocA = [nc.dram_tensor(f"klocA{s}", [D, HT], BF16, kind="Internal").ap() for s in range(NS)]
    klocB = [nc.dram_tensor(f"klocB{s}", [D, HT], BF16, kind="Internal").ap() for s in range(NS)]
    vlocA = [nc.dram_tensor(f"vlocA{s}", [HT, D], BF16, kind="Internal").ap() for s in range(NS)]
    vlocB = [nc.dram_tensor(f"vlocB{s}", [HT, D], BF16, kind="Internal").ap() for s in range(NS)]
    kgA = [nc.dram_tensor(f"kgA{s}", [4 * D, HT], BF16, kind="Internal").ap() for s in range(NS)]
    kgB = [nc.dram_tensor(f"kgB{s}", [4 * D, HT], BF16, kind="Internal").ap() for s in range(NS)]
    vgA = [nc.dram_tensor(f"vgA{s}", [4 * HT, D], BF16, kind="Internal").ap() for s in range(NS)]
    vgB = [nc.dram_tensor(f"vgB{s}", [4 * HT, D], BF16, kind="Internal").ap() for s in range(NS)]
    bnd = [nc.dram_tensor(f"bnd{s}", [32, 1], F32, kind="Internal").ap() for s in range(NS)]
    bndg = [nc.dram_tensor(f"bndg{s}", [128, 1], F32, kind="Internal").ap() for s in range(NS)]

    S = lambda n, sh, dt=F32: nc.alloc_sbuf_tensor(n, sh, dt).ap()
    x_res = S("x_res", [128, NTILE * D])
    regA = S("regA", [128, 18560], BF16)
    regB = S("regB", [128, 16384], BF16)
    QT = S("QT", [128, 16384], BF16)
    AVacc = S("AVacc", [128, 4096])
    sumsSB = S("sumsSB", [64, 2048])
    wstage = S("wstage", [128, 1024])
    vbias = S("vbias", [128, D])
    biasQK = S("biasQK", [128, 136])
    bqkv = S("bqkv", [128, 24])
    sqbuf = S("sqbuf", [128, 512], BF16)
    small = S("small", [128, 96])
    kqmax = S("kqmax", [64, 1])
    kngT = S("kngT", [64, 4])
    negC = S("negC", [128, 1])
    tiny = S("tiny", [4, 4])
    ident = S("ident", [128, 128], BF16)
    identf = S("identf", [128, 128])
    ones32 = S("ones32", [128, 32], BF16)
    onesf = S("onesf", [128, 128])
    ehF = S("ehF", [128, 256], BF16)
    ehA = S("ehA", [128, 256], BF16)
    mask01 = S("mask01", [128, 8 * 256], BF16)

    A_WF = 0
    A_KST = 8192
    A_VST = 12288
    A_SLAB = 0
    A_P = 16512

    ar = nc.alloc_psum_tensor("ar", [128, 4096], F32).ap()

    C_LNS, C_MU, C_VAR, C_R1, C_R12, C_T1, C_T2 = 0, 1, 2, 3, 4, 5, 6
    C_MUSAVE, C_R1SAVE = 16, 32

    rg = [[0, 1, 2, 3], [4, 5, 6, 7]]

    with contextlib.ExitStack() as ctx_stack:
        sems = {}
        for name in ("qs", "qg", "cc", "ts", "vs", "ss", "ps"):
            sems[name] = ctx_stack.enter_context(nc.semaphore(name))
        blk = ctx_stack.enter_context(nc.Block())

        def load_w_slabs(c, tag, wt, gates, scale=None, dstT=None):
            sync_gate, dve_gate = gates if gates else (None, None)
            if dstT is None:
                dstT, dbase = regA, A_WF
            else:
                dstT, dbase = dstT, 0
            for m in range(8):
                if m >= 1:
                    c.wait("sync", tag + f"wf{m - 1}")
                if m == 0 and sync_gate is not None:
                    c.wait("sync", sync_gate)
                c.dma("sync", wstage, wt[m * 128:(m + 1) * 128, :])
                c.mark(tag + f"wl{m}", "qs")
                c.wait("vector", tag + f"wl{m}")
                if m == 0 and dve_gate is not None:
                    c.wait("vector", dve_gate)
                if scale is None:
                    c.do("vector", lambda e, m=m, dstT=dstT, dbase=dbase: e.tensor_copy(
                        out=dstT[:, dbase + m * D:dbase + (m + 1) * D],
                        in_=wstage), inc="vs")
                else:
                    c.do("vector", lambda e, m=m, dstT=dstT, dbase=dbase: e.tensor_scalar_mul(
                        out=dstT[:, dbase + m * D:dbase + (m + 1) * D],
                        in0=wstage, scalar1=scale), inc="vs")
                c.mark(tag + f"wf{m}", "vs")

        def emit_sublayer(c, si, kind):
            pfx = f"s{si}_"
            isffn = kind == FFN
            scale = (1.0 / 32.0) if isffn else (1.0 / 8.0)
            nm_tc = [8, 4, 2, 1] if isffn else [8, 8, 8, 8]
            mout_tc = [1024, 512, 256, 128] if isffn else [1024] * 4
            WQ, WK, WV, WO = (fqwT, fkwT, fvwT, fowT) if isffn else (aWq, aWk, aWv, aWo)
            BQ, BK, BVR = (fqbT, fkbT, fvbR) if isffn else (abqT, abkT, abvR)

            c.dma("gpsimd", bqkv[:, 0:8], BQ)
            c.dma("gpsimd", bqkv[:, 8:16], BK)
            if isffn:
                c.dma("gpsimd", bqkv[:, 16:24], fnbT)
                c.dma("gpsimd", biasQK[:, 128:136], fnwT)
            c.mark(pfx + "bias_in", "qg")

            c.wait("vector", "x_ready")
            c.wait("vector", "x_ready_p")
            if isffn:
                c.wait("vector", pfx + "bias_in")
            for tb in range(NTILE):
                c.dof("vector", lambda e, tb=tb: e.reduce_sum(
                    out=small[:, tb:tb + 1], in_=x_res[:, tb * D:(tb + 1) * D],
                    axis=AX.X))
            c.dof("vector", lambda e: e.tensor_scalar_mul(
                out=small[:, 16:32], in0=small[:, 0:16], scalar1=1.0 / D))
            for tb in range(NTILE):
                c.dof("vector", lambda e, tb=tb: e.scalar_tensor_tensor(
                    out=wstage, in0=x_res[:, tb * D:(tb + 1) * D],
                    scalar=small[:, 16 + tb:17 + tb],
                    in1=x_res[:, tb * D:(tb + 1) * D],
                    op0=ALU.subtract, op1=ALU.mult,
                    accum_out=small[:, tb:tb + 1]))
            c.dof("vector", lambda e: e.tensor_scalar(
                out=small[:, 0:16], in0=small[:, 0:16],
                scalar1=1.0 / D, scalar2=EPS, op0=ALU.mult, op1=ALU.add))
            c.mark(pfx + "lnv", "vs")
            c.wait("scalar", pfx + "lnv")
            c.dof("scalar", lambda e: e.activation(
                out=small[:, 48:64], in_=small[:, 0:16], func=AF.Ln))
            c.dof("scalar", lambda e: e.activation(
                out=small[:, C_R1SAVE:C_R1SAVE + 16], in_=small[:, 48:64],
                func=AF.Exp, scale=-0.5))
            c.mark(pfx + "r1b", "ss")
            if isffn:
                c.wait("vector", pfx + "r1b")
                c.dof("vector", lambda e: e.reciprocal(
                    out=small[:, 48:64], in_=small[:, 0:16]))
                c.dof("vector", lambda e: e.tensor_scalar(
                    out=small[:, 0:16], in0=small[:, 0:16],
                    scalar1=-EPS, scalar2=None, op0=ALU.add))
                c.dof("vector", lambda e: e.tensor_tensor(
                    out=small[:, 0:16], in0=small[:, 0:16],
                    in1=small[:, 48:64], op=ALU.mult))
                c.dof("vector", lambda e: e.tensor_scalar(
                    out=small[:, 0:16], in0=small[:, 0:16],
                    scalar1=EPS, scalar2=None, op0=ALU.add))
                c.mark(pfx + "vyb", "vs")
                c.wait("scalar", pfx + "vyb")
                c.dof("scalar", lambda e: e.activation(
                    out=small[:, 48:64], in_=small[:, 0:16], func=AF.Ln))
                c.dof("scalar", lambda e: e.activation(
                    out=small[:, 0:16], in_=small[:, 48:64],
                    func=AF.Exp, scale=-0.5))
                c.mark(pfx + "r2b", "ss")
                c.wait("vector", pfx + "r2b")
                c.dof("vector", lambda e: e.tensor_tensor(
                    out=small[:, 0:16], in0=small[:, C_R1SAVE:C_R1SAVE + 16],
                    in1=small[:, 0:16], op=ALU.mult))
                c.dof("vector", lambda e: e.tensor_scalar(
                    out=small[:, 64:80], in0=small[:, C_R1SAVE:C_R1SAVE + 16],
                    scalar1=0.5, scalar2=1.0, op0=ALU.mult, op1=ALU.add))
                c.dof("vector", lambda e: e.tensor_tensor(
                    out=small[:, 80:96], in0=small[:, 16:32],
                    in1=small[:, C_R1SAVE:C_R1SAVE + 16], op=ALU.mult))
                c.dof("vector", lambda e: e.tensor_scalar_mul(
                    out=small[:, 80:96], in0=small[:, 80:96], scalar1=-0.5))
            else:
                c.wait("vector", pfx + "r1b")
            zt_n = 0
            for tb in range(NTILE):
                if tb >= 1:
                    c.wait("vector", pfx + f"ztt{(tb - 1) * 8 + 7}")
                sc2 = (small[:, tb:tb + 1] if isffn
                       else small[:, C_R1SAVE + tb:C_R1SAVE + tb + 1])
                c.dof("vector", lambda e, tb=tb, sc2=sc2: e.tensor_scalar(
                    out=wstage, in0=x_res[:, tb * D:(tb + 1) * D],
                    scalar1=small[:, 16 + tb:17 + tb], scalar2=sc2,
                    op0=ALU.subtract, op1=ALU.mult))
                c.mark(pfx + f"zp{tb}", "vs")
                for mt in range(8):
                    sl = zt_n % 2
                    if zt_n >= 2:
                        c.wait("tensor", pfx + f"zte{zt_n - 2}")
                    c.wait("tensor", pfx + f"zp{tb}")
                    c.do("tensor", lambda e, mt=mt, sl=sl: e.transpose(
                        ar[:, sl * 512:sl * 512 + 128],
                        wstage[:, mt * 128:(mt + 1) * 128], identf), inc="ts")
                    c.mark(pfx + f"ztt{zt_n}", "ts")
                    c.wait("vector", pfx + f"ztt{zt_n}")
                    if isffn:
                        c.do("vector", lambda e, mt=mt, sl=sl, tb=tb: e.tensor_scalar(
                            out=regB[:, mt * TOK + tb * 128:mt * TOK + (tb + 1) * 128],
                            in0=ar[:, sl * 512:sl * 512 + 128],
                            scalar1=biasQK[:, 128 + mt:129 + mt],
                            scalar2=bqkv[:, 16 + mt:17 + mt],
                            op0=ALU.mult, op1=ALU.add), inc="vs")
                    else:
                        c.do("vector", lambda e, mt=mt, sl=sl, tb=tb: e.tensor_copy(
                            out=regB[:, mt * TOK + tb * 128:mt * TOK + (tb + 1) * 128],
                            in_=ar[:, sl * 512:sl * 512 + 128]), inc="vs")
                    c.mark(pfx + f"zte{zt_n}", "vs")
                    zt_n += 1
            c.mark(pfx + "zT_done", "vs")

            load_w_slabs(c, pfx + "v", WV, (pfx + "zT_done", "x_ready"))
            nvm = 0
            for tc in range(4):
                nm = nm_tc[tc]
                if tc == 0:
                    c.wait("gpsimd", "x_ready")
                    c.dma("gpsimd", vbias, BVR)
                    c.mark(pfx + "vbias0", "qg")
                for tb in range(tc * 4, tc * 4 + 4):
                    vslot = A_VST + (tb % 2) * D
                    if tb >= 2:
                        c.wait("vector", pfx + f"vw{tb - 2}")
                    for oh in range(2):
                        par = nvm % 2
                        if nvm >= 2:
                            c.wait("tensor", pfx + f"vev{nvm - 2}")
                        for m in range(nm):
                            c.wait("tensor", pfx + f"vwf{m}")
                            c.do("tensor", lambda e, m=m, tb=tb, oh=oh, par=par, nm=nm: e.matmul(
                                ar[:, 1024 + par * 512:1024 + (par + 1) * 512],
                                regB[:, m * TOK + tb * 128:m * TOK + (tb + 1) * 128],
                                regA[:, A_WF + m * D + oh * 512:A_WF + m * D + (oh + 1) * 512],
                                start=(m == 0), stop=(m == nm - 1)), inc="ts")
                        c.mark(pfx + f"vmm{nvm}", "ts")
                        c.wait("vector", pfx + f"vmm{nvm}")
                        c.wait("vector", pfx + "vbias0")
                        c.do("vector", lambda e, par=par, oh=oh, vslot=vslot: e.tensor_tensor(
                            out=regA[:, vslot + oh * 512:vslot + (oh + 1) * 512],
                            in0=ar[:, 1024 + par * 512:1024 + (par + 1) * 512],
                            in1=vbias[:, oh * 512:(oh + 1) * 512], op=ALU.add), inc="vs")
                        c.mark(pfx + f"vev{nvm}", "vs")
                        nvm += 1
                    c.wait("gpsimd", pfx + f"vev{nvm - 1}")
                    vdst = vlocA[si] if tb < 8 else vlocB[si]
                    c.dma("gpsimd", vdst[(tb % 8) * 128:(tb % 8 + 1) * 128, :],
                          regA[:, vslot:vslot + D])
                    c.mark(pfx + f"vw{tb}", "qg")
                    if tb == 7:
                        c.mark(pfx + "vwA", "qg")
            c.mark(pfx + "vwB", "qg")
            c.mark(pfx + "v_p1_done", "ts")

            load_w_slabs(c, pfx + "k", WK, (pfx + "vwf7", pfx + "v_p1_done"))
            c.wait("vector", pfx + "bias_in")
            c.do("vector", lambda e: e.tensor_copy(
                out=biasQK[:, 8:16], in_=bqkv[:, 8:16]), inc="vs")
            c.mark(pfx + "kbias", "vs")
            c.wait("tensor", pfx + "zT_done")
            eh = ehF if isffn else ehA
            nmm = 0
            for tc in range(4):
                nm = nm_tc[tc]
                for o in range(8):
                    par = nmm % 2
                    if nmm >= 2:
                        c.wait("tensor", pfx + f"kev{nmm - 2}")
                    for m in range(nm):
                        c.wait("tensor", pfx + f"kwf{m}")
                        c.do("tensor", lambda e, o=o, m=m, tc=tc, par=par, nm=nm: e.matmul(
                            ar[:, par * 512:(par + 1) * 512],
                            regA[:, A_WF + m * D + o * 128:A_WF + m * D + (o + 1) * 128],
                            regB[:, m * TOK + tc * 512:m * TOK + (tc + 1) * 512],
                            start=(m == 0), stop=(m == nm - 1)), inc="ts")
                    c.mark(pfx + f"kmm{nmm}", "ts")
                    c.wait("scalar", pfx + f"kmm{nmm}")
                    c.wait("scalar", pfx + "kbias")
                    if nmm >= 8:
                        c.wait("scalar", pfx + f"kw{nmm - 8}")
                        c.wait("scalar", pfx + f"ksq{nmm - 8}")
                    c.do("scalar", lambda e, o=o, par=par: e.activation(
                        out=regA[:, A_KST + o * 512:A_KST + (o + 1) * 512],
                        in_=ar[:, par * 512:(par + 1) * 512],
                        func=AF.Identity, bias=biasQK[:, 8 + o:9 + o]), inc="ss")
                    c.mark(pfx + f"kev{nmm}", "ss")
                    c.wait("vector", pfx + f"kev{nmm}")
                    if nmm > 0:
                        c.wait("vector", pfx + f"knmm{nmm - 1}")
                    c.do("vector", lambda e, o=o: e.tensor_tensor(
                        out=sqbuf,
                        in0=regA[:, A_KST + o * 512:A_KST + (o + 1) * 512],
                        in1=regA[:, A_KST + o * 512:A_KST + (o + 1) * 512],
                        op=ALU.mult), inc="vs")
                    c.mark(pfx + f"ksq{nmm}", "vs")
                    c.wait("tensor", pfx + f"ksq{nmm}")
                    c.wait("tensor", "consts")
                    if o == 0 and tc > 0:
                        c.wait("tensor", pfx + f"kqmax{tc - 1}")
                    c.do("tensor", lambda e, o=o, eh=eh: e.matmul(
                        ar[0:32, 2048:2560], eh[:, o * 32:(o + 1) * 32], sqbuf,
                        start=(o == 0), stop=(o == 7),
                        tile_position=(0, 0)), inc="ts")
                    c.mark(pfx + f"knmm{nmm}", "ts")
                    c.wait("gpsimd", pfx + f"kev{nmm}")
                    kdst = klocA[si] if tc < 2 else klocB[si]
                    c.dma("gpsimd",
                          kdst[o * 128:(o + 1) * 128, (tc % 2) * 512:(tc % 2 + 1) * 512],
                          regA[:, A_KST + o * 512:A_KST + (o + 1) * 512])
                    c.mark(pfx + f"kw{nmm}", "qg")
                    if nmm == 15:
                        c.mark(pfx + "kwA", "qg")
                    nmm += 1
                c.wait("vector", pfx + f"knmm{nmm - 1}")
                c.dof("vector", lambda e: e.reduce_max(
                    out=sumsSB[0:32, 0:1], in_=ar[0:32, 2048:2560], axis=AX.X))
                if tc == 0:
                    c.dof("vector", lambda e: e.tensor_copy(
                        out=kqmax[0:32, 0:1], in_=sumsSB[0:32, 0:1]))
                else:
                    c.dof("vector", lambda e: e.tensor_max(
                        out=kqmax[0:32, 0:1], in0=kqmax[0:32, 0:1],
                        in1=sumsSB[0:32, 0:1]))
                c.mark(pfx + f"kqmax{tc}", "vs")
            c.mark(pfx + "kwB", "qg")
            c.mark(pfx + "k_p2_done", "ts")
            c.wait("vector", pfx + "kqmax3")
            c.dof("vector", lambda e: e.tensor_copy(
                out=sumsSB[0:32, 1:2], in_=kqmax[0:32, 0:1]))
            c.mark(pfx + "knloc", "vs")
            c.wait("gpsimd", pfx + "knloc")
            c.dma("gpsimd", bnd[si], sumsSB[0:32, 1:2])
            c.mark(pfx + "bndw", "qg")
            c.wait("gpsimd", pfx + "bndw")
            c.wait("gpsimd", pfx + "kwB")
            c.wait("gpsimd", pfx + "vwB")
            c.do("gpsimd", lambda e: e.collective_compute(
                "AllGather", ALU.bypass, replica_groups=rg,
                ins=[bnd[si].opt()], outs=[bndg[si].opt()]), inc="cc")
            c.mark(pfx + "cc_bnd", "cc")
            c.do("gpsimd", lambda e: e.collective_compute(
                "AllGather", ALU.bypass, replica_groups=rg,
                ins=[klocA[si].opt()], outs=[kgA[si].opt()]), inc="cc")
            c.mark(pfx + "cc_kA", "cc")
            c.do("gpsimd", lambda e: e.collective_compute(
                "AllGather", ALU.bypass, replica_groups=rg,
                ins=[klocB[si].opt()], outs=[kgB[si].opt()]), inc="cc")
            c.mark(pfx + "cc_kB", "cc")
            c.do("gpsimd", lambda e: e.collective_compute(
                "AllGather", ALU.bypass, replica_groups=rg,
                ins=[vlocA[si].opt()], outs=[vgA[si].opt()]), inc="cc")
            c.mark(pfx + "cc_vA", "cc")
            c.do("gpsimd", lambda e: e.collective_compute(
                "AllGather", ALU.bypass, replica_groups=rg,
                ins=[vlocB[si].opt()], outs=[vgB[si].opt()]), inc="cc")
            c.mark(pfx + "cc_vB", "cc")
            c.wait("gpsimd", pfx + "cc_bnd")
            c.dma("gpsimd", kngT[32:64, 0:4],
                  bndg[si].rearrange("(r h) c -> h (r c)", h=32),
                  allow_slow_non_contiguous=True)
            c.mark(pfx + "bnd_rd", "qg")
            c.wait("vector", pfx + "bnd_rd")
            c.dof("vector", lambda e: e.reduce_max(
                out=kngT[32:64, 0:1], in_=kngT[32:64, 0:4], axis=AX.X))
            c.mark(pfx + "knglob", "vs")

            load_w_slabs(c, pfx + "q", WQ, (pfx + "kwf7", pfx + "k_p2_done"))
            c.wait("vector", pfx + "bias_in")
            c.do("vector", lambda e: e.tensor_copy(
                out=biasQK[:, 0:8], in_=bqkv[:, 0:8]), inc="vs")
            c.mark(pfx + "qbias", "vs")
            c.wait("tensor", pfx + "zT_done")
            nmm = 0
            for tc in range(4):
                nm = nm_tc[tc]
                for o in range(8):
                    par = nmm % 2
                    if nmm >= 2:
                        c.wait("tensor", pfx + f"qev{nmm - 2}")
                    else:
                        c.wait("tensor", pfx + f"kev{30 + nmm}")
                    for m in range(nm):
                        c.wait("tensor", pfx + f"qwf{m}")
                        c.do("tensor", lambda e, o=o, m=m, tc=tc, par=par, nm=nm: e.matmul(
                            ar[:, par * 512:(par + 1) * 512],
                            regA[:, A_WF + m * D + o * 128:A_WF + m * D + (o + 1) * 128],
                            regB[:, m * TOK + tc * 512:m * TOK + (tc + 1) * 512],
                            start=(m == 0), stop=(m == nm - 1)), inc="ts")
                    c.mark(pfx + f"qmm{nmm}", "ts")
                    c.wait("scalar", pfx + f"qmm{nmm}")
                    c.wait("scalar", pfx + "qbias")
                    c.do("scalar", lambda e, o=o, tc=tc, par=par: e.activation(
                        out=QT[:, o * TOK + tc * 512:o * TOK + (tc + 1) * 512],
                        in_=ar[:, par * 512:(par + 1) * 512],
                        func=AF.Identity, bias=biasQK[:, o:o + 1]), inc="ss")
                    c.mark(pfx + f"qev{nmm}", "ss")
                    nmm += 1
                for o in range(8):
                    if o == 0:
                        c.wait("vector", pfx + f"qev{tc * 8 + 7}")
                        c.wait("vector", pfx + ("knmm31" if tc == 0
                                                else f"qnmm{tc - 1}_7"))
                    else:
                        c.wait("vector", pfx + f"qnmm{tc}_{o - 1}")
                    c.do("vector", lambda e, o=o, tc=tc: e.tensor_tensor(
                        out=sqbuf,
                        in0=QT[:, o * TOK + tc * 512:o * TOK + (tc + 1) * 512],
                        in1=QT[:, o * TOK + tc * 512:o * TOK + (tc + 1) * 512],
                        op=ALU.mult), inc="vs")
                    c.mark(pfx + f"qsq{tc}_{o}", "vs")
                    c.wait("tensor", pfx + f"qsq{tc}_{o}")
                    if tc == 0 and o == 0:
                        c.wait("tensor", pfx + "kqmax3")
                    if o == 0 and tc > 0:
                        c.wait("tensor", pfx + f"qqmax{tc - 1}")
                    c.do("tensor", lambda e, o=o, eh=eh: e.matmul(
                        ar[32:64, 2048:2560], eh[:, o * 32:(o + 1) * 32], sqbuf,
                        start=(o == 0), stop=(o == 7),
                        tile_position=(0, 32)), inc="ts")
                    c.mark(pfx + f"qnmm{tc}_{o}", "ts")
                c.wait("vector", pfx + f"qnmm{tc}_7")
                c.dof("vector", lambda e: e.reduce_max(
                    out=sumsSB[32:64, 0:1], in_=ar[32:64, 2048:2560], axis=AX.X))
                if tc == 0:
                    c.dof("vector", lambda e: e.tensor_copy(
                        out=kqmax[32:64, 0:1], in_=sumsSB[32:64, 0:1]))
                else:
                    c.dof("vector", lambda e: e.tensor_max(
                        out=kqmax[32:64, 0:1], in0=kqmax[32:64, 0:1],
                        in1=sumsSB[32:64, 0:1]))
                c.mark(pfx + f"qqmax{tc}", "vs")
            c.mark(pfx + "q_p2_done", "ts")

            c.wait("vector", pfx + "qqmax3")
            c.wait("vector", pfx + "knglob")
            c.dof("vector", lambda e: e.tensor_tensor(
                out=kngT[32:64, 1:2], in0=kqmax[32:64, 0:1],
                in1=kngT[32:64, 0:1], op=ALU.add))
            c.mark(pfx + "cmb", "vs")
            c.wait("tensor", pfx + "cmb")
            c.wait("tensor", pfx + "q_p2_done")
            c.do("tensor", lambda e: e.transpose(
                ar[0:1, 2560:2592], kngT[32:64, 1:2], identf[32:64, 32:64]),
                inc="ts")
            c.mark(pfx + "cmbT", "ts")
            c.wait("vector", pfx + "cmbT")
            c.dof("vector", lambda e: e.reduce_max(
                out=tiny[0:1, 0:1], in_=ar[0:1, 2560:2592], axis=AX.X))
            c.dof("vector", lambda e, scale=scale: e.tensor_scalar_mul(
                out=tiny[0:1, 1:2], in0=tiny[0:1, 0:1], scalar1=-0.5 * scale))
            c.mark(pfx + "c11", "vs")
            c.wait("tensor", pfx + "c11")
            c.do("tensor", lambda e: e.matmul(
                ar[:, 2560:2561], onesf[0:1, :], tiny[0:1, 1:2],
                start=True, stop=True), inc="ts")
            c.mark(pfx + "cbc", "ts")
            c.wait("vector", pfx + "cbc")
            c.do("vector", lambda e: e.tensor_copy(out=negC, in_=ar[:, 2560:2561]), inc="vs")
            c.mark(pfx + "negC", "vs")

            emit_attention(c, si, kind, pfx, scale)

            emit_phase3(c, si, kind, pfx, nm_tc, mout_tc, WO)

        def emit_attention(c, si, kind, pfx, scale):
            isffn = kind == FFN
            ngrp = 4 if isffn else 8
            nktb = 4 if isffn else 2
            ktb_ranks = 1 if isffn else 2
            SLABW = 8192 if isffn else 8256
            if not isffn:
                for gate in ("q_p2_done", "ksq31", "vev31", "kwB", "vwB"):
                    c.wait("vector", pfx + gate)
                for par in range(2):
                    for hh in range(2):
                        for rr in range(2):
                            vr = A_SLAB + par * SLABW + 4096 + hh * 2080 + rr * 1040
                            c.do("vector", lambda e, vr=vr: e.memset(
                                regA[:, vr:vr + 1040]
                                .rearrange("p (j d) -> p j d", d=65)[:, :, 64:65],
                                1.0), inc="vs")
                c.mark(pfx + "vones", "vs")

            def emit_slab_dma(g, ktb):
                    par = ktb % 2
                    base = A_SLAB + par * SLABW
                    seq = g * nktb + ktb
                    if seq == 0:
                        c.wait("sync", pfx + "q_p2_done")
                    elif seq == 1:
                        c.wait("sync", pfx + "ksq31")
                        c.wait("sync", pfx + "vev31")
                        c.wait("sync", pfx + "kwB")
                        c.wait("sync", pfx + "vwB")
                    else:
                        pg, pk = divmod(seq - 2, nktb)
                        c.wait("sync", pfx + f"avdone_g{pg}k{pk}")
                    for half, ccm in (("A", "cc_vA"), ("B", "cc_vB")):
                        c.wait("sync", pfx + ccm)
                        vgh = (vgA if half == "A" else vgB)[si]
                        ho = 0 if half == "A" else 1024
                        for rr in range(ktb_ranks):
                            r = ktb * ktb_ranks + rr
                            for hh in range(2):
                                h = g * 2 + hh
                                if isffn:
                                    c.dma("sync",
                                          regA[:, base + 4096 + hh * 2048 + ho:
                                               base + 4096 + hh * 2048 + ho + 1024]
                                          .rearrange("p (j d) -> p j d", d=128),
                                          vgh[r * HT:(r + 1) * HT, h * 128:(h + 1) * 128]
                                          .rearrange("(j p) d -> p j d", p=128))
                                else:
                                    vr = (base + 4096 + hh * 2080 + rr * 1040 +
                                          (0 if half == "A" else 520))
                                    c.dma("sync",
                                          regA[:, vr:vr + 520]
                                          .rearrange("p (j d) -> p j d", d=65)[:, :, 0:64],
                                          vgh[r * HT:(r + 1) * HT, h * 64:(h + 1) * 64]
                                          .rearrange("(j p) d -> p j d", p=128))
                        c.mark(pfx + f"slabV{half}_g{g}k{ktb}", "qs")
                    for half, ccm in (("A", "cc_kA"), ("B", "cc_kB")):
                        c.wait("sync", pfx + ccm)
                        kgh = (kgA if half == "A" else kgB)[si]
                        ho = 0 if half == "A" else 1024
                        for rr in range(ktb_ranks):
                            r = ktb * ktb_ranks + rr
                            for hh in range(2):
                                h = g * 2 + hh
                                if isffn:
                                    c.dma("sync",
                                          regA[:, base + hh * 2048 + ho:
                                               base + hh * 2048 + ho + 1024],
                                          kgh[r * D + h * 128:r * D + (h + 1) * 128, :])
                                else:
                                    c.dma("sync",
                                          regA[hh * 64:(hh + 1) * 64,
                                               base + rr * 2048 + ho:
                                               base + rr * 2048 + ho + 1024],
                                          kgh[r * D + h * 64:r * D + (h + 1) * 64, :])
                        c.mark(pfx + f"slabK{half}_g{g}k{ktb}", "qs")

            normq = []
            for g in range(ngrp):
                if g == (2 if isffn else 4):
                    c.mark(pfx + "qtfree", "ts")
                def retire(p, g=g):
                    emit_av(c, pfx, kind, g, *p)
                    if p[11] and p[9] == nktb - 1:
                        emit_normalize_recip(c, pfx, kind, g, p[8])
                        normq.append((g, p[8]))
                emit_slab_dma(g, 0)
                if nktb > 1:
                    emit_slab_dma(g, 1)
                kt_seq = 0
                pends = []
                for ktb in range(nktb):
                    par = ktb % 2
                    base = A_SLAB + par * SLABW
                    for qc in range(4):
                        kts = []
                        for rr in range(ktb_ranks):
                            for j in range(16):
                                if isffn or j < 8 or qc >= j // 4:
                                    kts.append((rr, j))
                        for idx, (rr, j) in enumerate(kts):
                            sl = kt_seq % 2
                            masked = (not isffn) and j >= 8 and qc == j // 4
                            off = 256 if (masked and (j // 2) % 2 == 1) else 0
                            width = 512 - off
                            c.wait("tensor", pfx + f"slabK{'A' if j < 8 else 'B'}_g{g}k{ktb}")
                            c.wait("tensor", pfx + "negC")
                            if kt_seq >= 2:
                                c.wait("tensor", pfx + f"exp{g}_{kt_seq - 2}")
                            if isffn:
                                for hh in range(2):
                                    c.do("tensor", lambda e, hh=hh, j=j, sl=sl, qc=qc, base=base, g=g: e.matmul(
                                        ar[:, sl * 1024 + hh * 512:sl * 1024 + (hh + 1) * 512],
                                        regA[:, base + hh * 2048 + j * 128:base + hh * 2048 + (j + 1) * 128],
                                        QT[:, (g * 2 + hh) * TOK + qc * 512:(g * 2 + hh) * TOK + (qc + 1) * 512],
                                        start=True, stop=True), inc="ts")
                            else:
                                kb = base + rr * 2048 + j * 128
                                qb0 = g * TOK + qc * 512 + off
                                for hh in range(2):
                                    c.do("tensor", lambda e, kb=kb, hh=hh, sl=sl, qb0=qb0, off=off, width=width: e.matmul(
                                        ar[:, sl * 1024 + hh * 512 + off:sl * 1024 + (hh + 1) * 512],
                                        regA[hh * 64:(hh + 1) * 64, kb:kb + 128],
                                        QT[hh * 64:(hh + 1) * 64, qb0:qb0 + width],
                                        start=True, stop=True, tile_position=(hh * 64, 0)), inc="ts")
                            c.mark(pfx + f"qk{g}_{kt_seq}", "ts")
                            if len(pends) >= 2:
                                retire(pends.pop(0))
                            if idx == 2 and normq:
                                emit_normalize_qc(c, pfx, kind, *normq.pop(0))
                            c.wait("scalar", pfx + f"qk{g}_{kt_seq}")
                            c.wait("scalar", pfx + "negC")
                            if kt_seq >= 2:
                                c.wait("scalar", pfx + f"avP{g}_{kt_seq - 2}")
                            c.do("scalar", lambda e, sl=sl: e.activation(
                                out=regA[:, A_P + sl * 1024:A_P + (sl + 1) * 1024],
                                in_=ar[:, sl * 1024:(sl + 1) * 1024],
                                func=AF.Exp, bias=negC, scale=scale), inc="ss")
                            c.mark(pfx + f"exp{g}_{kt_seq}", "ss")
                            if masked:
                                r = ktb * ktb_ranks + rr
                                jk = r if j % 2 == 0 else 7 - r
                                c.wait("gpsimd", pfx + f"exp{g}_{kt_seq}")
                                c.wait("gpsimd", "mask01_ready")
                                for hh in range(2):
                                    c.do("gpsimd", lambda e, sl=sl, hh=hh, off=off, jk=jk: e.tensor_tensor(
                                        out=regA[:, A_P + sl * 1024 + hh * 512 + off:
                                                 A_P + sl * 1024 + hh * 512 + off + 256],
                                        in0=regA[:, A_P + sl * 1024 + hh * 512 + off:
                                                 A_P + sl * 1024 + hh * 512 + off + 256],
                                        in1=mask01[:, jk * 256:(jk + 1) * 256],
                                        op=ALU.mult), inc="ps")
                                c.mark(pfx + f"msk{g}_{kt_seq}", "ps")
                            pends.append((kt_seq, sl, masked, off, width, base, rr, j,
                                          qc, ktb, idx == 0, idx == len(kts) - 1))
                            kt_seq += 1
                    while pends:
                        retire(pends.pop(0))
                    c.mark(pfx + f"avdone_g{g}k{ktb}", "ts")
                    if ktb + 2 < nktb:
                        emit_slab_dma(g, ktb + 2)
            while normq:
                emit_normalize_qc(c, pfx, kind, *normq.pop(0))
            c.mark(pfx + "attn_done", "vs")

        def emit_av(c, pfx, kind, g, kt_seq, sl, masked, off, width, base, rr, j, qc,
                    ktb, is_first, is_last):
            isffn = kind == FFN
            c.wait("tensor", pfx + f"slabV{'A' if j < 8 else 'B'}_g{g}k{ktb}")
            c.wait("tensor", pfx + f"exp{g}_{kt_seq}")
            if masked:
                c.wait("tensor", pfx + f"msk{g}_{kt_seq}")
            if isffn and is_first and (ktb > 0 or qc > 0):
                c.wait("tensor", pfx + f"flush{g}_{ktb}_{qc - 1}" if qc > 0
                       else pfx + f"flush{g}_{ktb - 1}_3")
            elif isffn and is_first and g > 0:
                c.wait("tensor", pfx + f"flush{g - 1}_3_3")
            if isffn:
                for hh in range(2):
                    c.do("tensor", lambda e, hh=hh, j=j, sl=sl, base=base, st=is_first: e.matmul(
                        ar[:, 2048 + hh * 512:2048 + (hh + 1) * 512],
                        regA[:, base + 4096 + hh * 2048 + j * 128:base + 4096 + hh * 2048 + (j + 1) * 128],
                        regA[:, A_P + sl * 1024 + hh * 512:A_P + sl * 1024 + (hh + 1) * 512],
                        start=st, stop=False), inc="ts")
                for hh in range(2):
                    c.do("tensor", lambda e, hh=hh, sl=sl, st=is_first: e.matmul(
                        ar[hh * 32:(hh + 1) * 32, 3072:3584],
                        ones32,
                        regA[:, A_P + sl * 1024 + hh * 512:A_P + sl * 1024 + (hh + 1) * 512],
                        start=st, stop=False, tile_position=(0, hh * 32)), inc="ts")
            else:
                c.wait("tensor", pfx + "vones")
                if is_first:
                    pred = ktb * 4 + qc - 2
                    pg2 = g
                    if pred < 0:
                        pg2, pred = g - 1, (4 if isffn else 2) * 4 + pred
                    if pg2 >= 0:
                        pk2, pq2 = divmod(pred, 4)
                        if pk2 == (4 if isffn else 2) - 1:
                            c.wait("tensor", pfx + f"nm{pg2}_{pq2}_1")
                        else:
                            c.wait("tensor", pfx + f"flush{pg2}_{pk2}_{pq2}")
                vb = base + 4096
                pa = 2048 + (qc % 2) * 1024
                for hh in range(2):
                    lcol = vb + hh * 2080 + rr * 1040 + j * 65
                    c.do("tensor", lambda e, hh=hh, lcol=lcol, sl=sl, off=off, st=is_first, pa=pa: e.matmul(
                        ar[0:65, pa + hh * 512 + off:pa + hh * 512 + 512],
                        regA[:, lcol:lcol + 65],
                        regA[:, A_P + sl * 1024 + hh * 512 + off:A_P + sl * 1024 + (hh + 1) * 512],
                        start=st, stop=False), inc="ts")
            c.mark(pfx + f"avP{g}_{kt_seq}", "ts")
            if is_last:
                c.wait("vector", pfx + f"avP{g}_{kt_seq}")
                first_blk = ktb == 0
                if first_blk and g > 0:
                    c.wait("vector", pfx + f"norm{g - 1}")
                if not first_blk:
                    c.wait("vector", pfx + f"flush{g}_{ktb - 1}_{qc}")
                if isffn:
                    for hh in range(2):
                        src = ar[:, 2048 + hh * 512:2048 + (hh + 1) * 512]
                        dst = AVacc[:, hh * 2048 + qc * 512:hh * 2048 + (qc + 1) * 512]
                        if first_blk:
                            c.do("vector", lambda e, src=src, dst=dst: e.tensor_copy(
                                out=dst, in_=src), inc="vs")
                        else:
                            c.do("vector", lambda e, src=src, dst=dst: e.tensor_tensor(
                                out=dst, in0=src, in1=dst, op=ALU.add), inc="vs")
                else:
                    pa = 2048 + (qc % 2) * 1024
                    pairs = [
                        (ar[0:64, pa:pa + 512], AVacc[0:64, qc * 512:(qc + 1) * 512]),
                        (ar[0:64, pa + 512:pa + 1024], AVacc[64:128, qc * 512:(qc + 1) * 512]),
                        (ar[64:65, pa:pa + 512], sumsSB[0:1, qc * 512:(qc + 1) * 512]),
                        (ar[64:65, pa + 512:pa + 1024], sumsSB[32:33, qc * 512:(qc + 1) * 512]),
                    ]
                    for src, dst in pairs:
                        if first_blk:
                            c.do("vector", lambda e, src=src, dst=dst: e.tensor_copy(
                                out=dst, in_=src), inc="vs")
                        else:
                            c.do("vector", lambda e, src=src, dst=dst: e.tensor_tensor(
                                out=dst, in0=src, in1=dst, op=ALU.add), inc="vs")
                if isffn:
                    ssrc = ar[0:64, 3072:3584]
                    sdst = sumsSB[0:64, qc * 512:(qc + 1) * 512]
                    if first_blk:
                        c.do("vector", lambda e, ssrc=ssrc, sdst=sdst: e.tensor_copy(
                            out=sdst, in_=ssrc), inc="vs")
                    else:
                        c.do("vector", lambda e, ssrc=ssrc, sdst=sdst: e.tensor_tensor(
                            out=sdst, in0=ssrc, in1=sdst, op=ALU.add), inc="vs")
                c.mark(pfx + f"flush{g}_{ktb}_{qc}", "vs")

        def emit_normalize_recip(c, pfx, kind, g, qc):
            isffn = kind == FFN
            nktb = 4 if isffn else 2
            c.wait("vector", pfx + f"flush{g}_{nktb - 1}_{qc}")
            c.do("vector", lambda e, qc=qc: e.reciprocal(
                out=sumsSB[:, qc * 512:(qc + 1) * 512],
                in_=sumsSB[:, qc * 512:(qc + 1) * 512]), inc="vs")
            c.mark(pfx + f"recip{g}_{qc}", "vs")

        def emit_normalize_qc(c, pfx, kind, g, qc):
            isffn = kind == FFN
            PS = 3584
            if isffn:
                for hh in range(2):
                    c.wait("tensor", pfx + f"recip{g}_{qc}")
                    prev = (pfx + f"nm{g}_{qc}_{hh - 1}" if hh == 1 else
                            (pfx + f"nm{g}_{qc - 1}_1" if qc > 0 else
                             (pfx + f"nm{g - 1}_3_1" if g > 0 else None)))
                    if prev is not None:
                        c.wait("tensor", prev)
                    c.do("tensor", lambda e, hh=hh, qc=qc: e.matmul(
                        ar[:, PS:PS + 512],
                        onesf[hh * 32:hh * 32 + 1, :],
                        sumsSB[hh * 32:hh * 32 + 1, qc * 512:(qc + 1) * 512],
                        start=True, stop=True,
                        tile_position=(hh * 32, 0)), inc="ts")
                    c.mark(pfx + f"nbc{g}_{qc}_{hh}", "ts")
                    c.wait("vector", pfx + f"nbc{g}_{qc}_{hh}")
                    c.do("vector", lambda e, hh=hh, qc=qc: e.tensor_tensor(
                        out=regB[:, (2 * g + hh) * TOK + qc * 512:
                                 (2 * g + hh) * TOK + (qc + 1) * 512],
                        in0=ar[:, PS:PS + 512],
                        in1=AVacc[:, hh * 2048 + qc * 512:hh * 2048 + (qc + 1) * 512],
                        op=ALU.mult), inc="vs")
                    c.mark(pfx + f"nm{g}_{qc}_{hh}", "vs")
            else:
                PSA = 2048 + (qc % 2) * 1024
                c.wait("tensor", pfx + f"recip{g}_{qc}")
                prev = (pfx + f"nm{g}_{qc - 1}_1" if qc > 0 else
                        (pfx + f"nm{g - 1}_3_1" if g > 0 else None))
                if prev is not None:
                    c.wait("tensor", prev)
                c.do("tensor", lambda e, qc=qc, PSA=PSA: e.matmul(
                    ar[0:64, PSA:PSA + 512],
                    onesf[0:1, 0:64],
                    sumsSB[0:1, qc * 512:(qc + 1) * 512],
                    start=True, stop=True, tile_position=(0, 0)), inc="ts")
                c.do("tensor", lambda e, qc=qc, PSA=PSA: e.matmul(
                    ar[64:128, PSA:PSA + 512],
                    onesf[32:33, 0:64],
                    sumsSB[32:33, qc * 512:(qc + 1) * 512],
                    start=True, stop=True, tile_position=(32, 64)), inc="ts")
                c.mark(pfx + f"nbc{g}_{qc}_1", "ts")
                c.wait("vector", pfx + f"nbc{g}_{qc}_1")
                c.do("vector", lambda e, qc=qc, PSA=PSA: e.tensor_tensor(
                    out=regB[:, g * TOK + qc * 512:g * TOK + (qc + 1) * 512],
                    in0=ar[:, PSA:PSA + 512],
                    in1=AVacc[:, qc * 512:(qc + 1) * 512],
                    op=ALU.mult), inc="vs")
                c.mark(pfx + f"nm{g}_{qc}_1", "vs")
            if qc == 3:
                c.mark(pfx + f"norm{g}", "vs")

        def emit_phase3(c, si, kind, pfx, nm_tc, mout_tc, WO):
            isffn = kind == FFN
            load_w_slabs(c, pfx + "o", WO,
                         (pfx + "qwf7", pfx + "qtfree"),
                         scale=(0.5 if isffn else None), dstT=QT)
            if isffn:
                c.wait("gpsimd", pfx + "vev31")
                c.dma("gpsimd", vbias, fobR)
                c.mark(pfx + "obr_in", "qg")
                c.wait("scalar", pfx + "obr_in")
                c.do("scalar", lambda e: e.mul(out=vbias, in_=vbias, mul=0.5), inc="ss")
                c.mark(pfx + "obhalf", "ss")
            npo = 0
            for tc in range(4):
                mo = mout_tc[tc]
                nmch = nm_tc[tc] if isffn else 8
                noch = (mo + 511) // 512
                for tb in range(tc * 4, tc * 4 + 4):
                    if isffn:
                        c.wait("vector", pfx + "owf7")
                        c.wait("vector", pfx + "obhalf")
                        c.dof("vector", lambda e, tb=tb: e.tensor_scalar(
                            out=x_res[:, tb * D:(tb + 1) * D],
                            in0=x_res[:, tb * D:(tb + 1) * D],
                            scalar1=small[:, 64 + tb:65 + tb],
                            scalar2=small[:, 80 + tb:81 + tb],
                            op0=ALU.mult, op1=ALU.add))
                        c.dof("vector", lambda e, tb=tb, mo=mo: e.tensor_tensor(
                            out=x_res[:, tb * D:tb * D + mo], in0=vbias[:, 0:mo],
                            in1=x_res[:, tb * D:tb * D + mo], op=ALU.add))
                        c.mark(pfx + f"pre{tb}", "vs")
                    for och in range(noch):
                        w = min(512, mo - och * 512)
                        par = npo % 2
                        if npo >= 2:
                            c.wait("tensor", pfx + f"oev{npo - 2}")
                        c.wait("tensor", pfx + "attn_done")
                        for m in range(nmch):
                            c.wait("tensor", pfx + f"owf{m}")
                            c.do("tensor", lambda e, m=m, tb=tb, och=och, par=par, w=w, nmch=nmch: e.matmul(
                                ar[:, par * 512:par * 512 + w],
                                regB[:, m * TOK + tb * 128:m * TOK + (tb + 1) * 128],
                                QT[:, m * D + och * 512:m * D + och * 512 + w],
                                start=(m == 0), stop=(m == nmch - 1)), inc="ts")
                        c.mark(pfx + f"omm{npo}", "ts")
                        c.wait("vector", pfx + f"omm{npo}")
                        c.do("vector", lambda e, par=par, tb=tb, och=och, w=w: e.tensor_tensor(
                            out=x_res[:, tb * D + och * 512:tb * D + och * 512 + w],
                            in0=ar[:, par * 512:par * 512 + w],
                            in1=x_res[:, tb * D + och * 512:tb * D + och * 512 + w],
                            op=ALU.add), inc="vs")
                        c.mark(pfx + f"oev{npo}", "vs")
                        npo += 1
                    c.mark(pfx + f"res{tb}", "vs")
            c.mark("x_ready", "vs")
            c.mark("x_ready_p", "ps")

        def emit_all(c: Ctx):
            c.dma("sync", x_res.rearrange("p (t f) -> p t f", f=D),
                  x_in.rearrange("(t p) f -> p t f", p=128))
            c.mark("x_in_dma", "qs")
            for dst, src in ((ident, ident_c), (identf, identf_c),
                             (ones32, ones32_c), (onesf, onesf_c),
                             (ehF, ehF_c), (ehA, ehA_c), (mask01, maskin)):
                c.dma("gpsimd", dst, src)
            c.mark("consts", "qg")
            c.mark("mask01_ready", "qg")
            c.wait("vector", "x_in_dma")
            c.do("vector", lambda e: e.memset(tiny, 0.0), inc="vs")
            c.mark("x_ready", "vs")
            c.mark("x_ready_p", "ps")


            for si, kind in enumerate(sublayers):
                emit_sublayer(c, si, kind)
                if si in taps:
                    c.wait("sync", "x_ready")
                    c.dma("sync", taps[si].rearrange("(t p) f -> p t f", p=128),
                          x_res.rearrange("p (t f) -> p t f", f=D))
                    c.mark(f"tap{si}", "qs")

            lastp = f"s{len(sublayers) - 1}_" if sublayers else ""
            xo_r = x_out.rearrange("(t p) f -> p t f", p=128)
            for tb in range(NTILE):
                if sublayers:
                    c.wait("sync", lastp + f"res{tb}")
                c.dma("sync", xo_r[:, tb:tb + 1, :],
                      x_res[:, tb * D:(tb + 1) * D].rearrange(
                          "p (t f) -> p t f", f=D))
            if DEBUG_DUMPS:
                dump_srcs = {"kloc0": klocA[0], "vloc0": vlocA[0], "kg0": kgA[0],
                             "vg0": vgA[0], "bndg0": bndg[0], "QT": QT,
                             "sums": sumsSB, "AVacc": AVacc, "attnT": regB,
                             "negC": negC, "x_res_d": x_res}
                for nm in DEBUG_DUMPS:
                    c.dma("sync", dumps_out[nm], dump_srcs[nm])
            c.mark("out_stored", "qs")
            c.wait("sync", "out_stored")

        blk.sync(lambda eng: emit_all(Ctx("sync", eng, sems)))
        blk.tensor(lambda eng: emit_all(Ctx("tensor", eng, sems)))
        blk.vector(lambda eng: emit_all(Ctx("vector", eng, sems)))
        blk.scalar(lambda eng: emit_all(Ctx("scalar", eng, sems)))
        blk.gpsimd(lambda eng: emit_all(Ctx("gpsimd", eng, sems)))

    return nc


def _mask_block(jk, cloc):
    m = np.zeros((128, 256), np.float32)
    for blkcol, tq in enumerate((cloc, 7 - cloc)):
        sl = slice(blkcol * 128, (blkcol + 1) * 128)
        if tq > jk:
            m[:, sl] = 1.0
        elif tq == jk:
            k = np.arange(128)[:, None]
            q = np.arange(128)[None, :]
            m[:, sl] = (k <= q).astype(np.float32)
    return m


def kernel(L0_attn_x, L0_attn_Wq, L0_attn_bq, L0_attn_Wk, L0_attn_bk, L0_attn_Wv,
           L0_attn_bv, L0_attn_Wo, L0_ffn_router_prob, L0_ffn_norm_w, L0_ffn_norm_b,
           L0_ffn_q_w, L0_ffn_q_b, L0_ffn_k_w, L0_ffn_k_b, L0_ffn_v_w, L0_ffn_v_b,
           L0_ffn_o_w, L0_ffn_o_b):
    x = np.ascontiguousarray(np.asarray(L0_attn_x, np.float32))
    B, T, _ = x.shape
    xt = x.reshape(B, 64, 128, D)
    assert np.asarray(L0_ffn_router_prob).shape[-1] == 4

    tileT = lambda v: np.ascontiguousarray(np.asarray(v, np.float32).reshape(8, 128).T)
    rep = lambda v: np.ascontiguousarray(
        np.broadcast_to(np.asarray(v, np.float32), (128, D)))
    tr = lambda w: np.ascontiguousarray(np.asarray(w, np.float32).T)
    natw = lambda w: np.ascontiguousarray(np.asarray(w, np.float32))

    common = dict(
        aWq=natw(L0_attn_Wq), aWk=natw(L0_attn_Wk), aWv=natw(L0_attn_Wv),
        aWo=natw(L0_attn_Wo),
        abqT=tileT(L0_attn_bq), abkT=tileT(L0_attn_bk), abvR=rep(L0_attn_bv),
        fqwT=tr(L0_ffn_q_w), fkwT=tr(L0_ffn_k_w), fvwT=tr(L0_ffn_v_w),
        fowT=tr(L0_ffn_o_w),
        fqbT=tileT(L0_ffn_q_b), fkbT=tileT(L0_ffn_k_b), fvbR=rep(L0_ffn_v_b),
        fnwT=tileT(L0_ffn_norm_w), fnbT=tileT(L0_ffn_norm_b), fobR=rep(L0_ffn_o_b),
    )

    in_maps = []
    for cidx in range(8):
        b, cloc = divmod(cidx, 4)
        tm = tmap(cloc)
        xs = np.ascontiguousarray(xt[b, tm].reshape(TOK, D))
        mk = np.concatenate([_mask_block(jk, cloc) for jk in range(8)], axis=1)
        im = dict(common)
        im["x_in"] = xs
        im["maskin"] = np.ascontiguousarray(mk)
        in_maps.append(im)

    nc = build_program()
    res = run_bass_kernel_spmd(nc, in_maps, core_ids=list(range(8)))

    out = np.empty((B, 64, 128, D), np.float32)
    for cidx in range(8):
        b, cloc = divmod(cidx, 4)
        tm = tmap(cloc)
        out[b, tm] = res.results[cidx]["x_out"].reshape(NTILE, 128, D)
    return out.reshape(B, T, D)


# version 09 (final)
# speedup vs baseline: 1.0772x; 1.0080x over previous
import numpy as np
import contextlib
import concourse.bass as bass
import concourse.mybir as mybir
from concourse.bass_utils import run_bass_kernel_spmd
import ml_dtypes

F32 = mybir.dt.float32
BF16 = mybir.dt.bfloat16
AF = mybir.ActivationFunctionType
ALU = mybir.AluOpType
AX = mybir.AxisListType

D = 1024
NTILE = 16
TOK = 2048
EPS = 1e-5
FFN = "ffn"
ATT = "attn"

DEBUG_TAPS = ()
DEBUG_DUMPS = ()


def tmap(cloc):
    return sorted([8 * B + cloc for B in range(8)] + [8 * B + 7 - cloc for B in range(8)])


class Ctx:

    def __init__(self, me, eng, sems):
        self.me = me
        self.eng = eng
        self.sems = sems
        self.cnt = {k: 0 for k in sems}
        self.marks = {}
        self.lastw = {}

    def do(self, engname, fn, inc=None, n=1):
        inst = None
        if self.me == engname:
            inst = fn(self.eng)
        if inc is not None:
            self.cnt[inc] += n
            if inst is not None:
                inst.then_inc(self.sems[inc], n)

    SELF = {"vector": "vs", "scalar": "ss", "tensor": "ts", "gpsimd": "ps"}

    def dof(self, engname, fn):
        sem = Ctx.SELF[engname]
        self.do(engname, fn, inc=sem)
        if self.me == engname:
            self.eng.wait_ge(self.sems[sem], self.cnt[sem])
            self.lastw[sem] = self.cnt[sem]

    def dma(self, engname, out, in_, **kw):
        q = "qs" if engname == "sync" else "qg"
        self.do(engname, lambda e: e.dma_start(out=out, in_=in_, **kw), inc=q, n=16)

    def mark(self, name, sem):
        self.marks[name] = (sem, self.cnt[sem])

    def wait(self, engname, markname):
        sem, val = self.marks[markname]
        if self.me == engname and val > 0 and self.lastw.get(sem, -1) < val:
            self.eng.wait_ge(self.sems[sem], val)
            self.lastw[sem] = val


def build_program(sublayers=(FFN, ATT, FFN), debug_taps=None):
    if debug_taps is None:
        debug_taps = DEBUG_TAPS
    nc = bass.Bass("TRN2", target_bir_lowering=False)

    x_in = nc.dram_tensor("x_in", [TOK, D], F32, kind="ExternalInput").ap()
    x_out = nc.dram_tensor("x_out", [TOK, D], F32, kind="ExternalOutput").ap()
    taps = {i: nc.dram_tensor(f"x_tap{i}", [TOK, D], F32, kind="ExternalOutput").ap()
            for i in debug_taps}
    _dump_shapes = {"kloc0": ([D, TOK // 2], BF16), "vloc0": ([TOK // 2, D], BF16),
                    "kg0": ([4 * D, TOK // 2], BF16), "vg0": ([2 * TOK, D], BF16),
                    "bndg0": ([128, 1], F32), "QT": ([128, 16384], BF16),
                    "sums": ([64, 2048], F32), "AVacc": ([128, 4096], F32),
                    "attnT": ([128, 16384], BF16), "negC": ([128, 1], F32),
                    "x_res_d": ([128, NTILE * D], F32)}
    dumps_out = {nm: nc.dram_tensor(f"dump_{nm}", _dump_shapes[nm][0],
                                    _dump_shapes[nm][1], kind="ExternalOutput").ap()
                 for nm in DEBUG_DUMPS}

    aWq = nc.dram_tensor("aWq", [D, D], F32, kind="ExternalInput").ap()
    aWk = nc.dram_tensor("aWk", [D, D], F32, kind="ExternalInput").ap()
    aWv = nc.dram_tensor("aWv", [D, D], F32, kind="ExternalInput").ap()
    aWo = nc.dram_tensor("aWo", [D, D], F32, kind="ExternalInput").ap()
    abqT = nc.dram_tensor("abqT", [128, 8], F32, kind="ExternalInput").ap()
    abkT = nc.dram_tensor("abkT", [128, 8], F32, kind="ExternalInput").ap()
    abvR = nc.dram_tensor("abvR", [128, D], F32, kind="ExternalInput").ap()
    fqwT = nc.dram_tensor("fqwT", [D, D], F32, kind="ExternalInput").ap()
    fkwT = nc.dram_tensor("fkwT", [D, D], F32, kind="ExternalInput").ap()
    fvwT = nc.dram_tensor("fvwT", [D, D], F32, kind="ExternalInput").ap()
    fowT = nc.dram_tensor("fowT", [D, D], F32, kind="ExternalInput").ap()
    fqbT = nc.dram_tensor("fqbT", [128, 8], F32, kind="ExternalInput").ap()
    fkbT = nc.dram_tensor("fkbT", [128, 8], F32, kind="ExternalInput").ap()
    fvbR = nc.dram_tensor("fvbR", [128, D], F32, kind="ExternalInput").ap()
    fnwT = nc.dram_tensor("fnwT", [128, 8], F32, kind="ExternalInput").ap()
    fnbT = nc.dram_tensor("fnbT", [128, 8], F32, kind="ExternalInput").ap()
    fobR = nc.dram_tensor("fobR", [128, D], F32, kind="ExternalInput").ap()
    maskin = nc.dram_tensor("maskin", [128, 8 * 256], F32, kind="ExternalInput").ap()

    ident_c = nc.inline_tensor(np.eye(128, dtype=ml_dtypes.bfloat16), name="ident_c").ap()
    identf_c = nc.inline_tensor(np.eye(128, dtype=np.float32), name="identf_c").ap()
    o32 = np.zeros((128, 32), dtype=ml_dtypes.bfloat16); o32[:, 0] = 1
    ones32_c = nc.inline_tensor(o32, name="ones32_c").ap()
    onesf_c = nc.inline_tensor(np.ones((128, 128), np.float32), name="onesf_c").ap()
    ehF_np = np.zeros((128, 256), dtype=ml_dtypes.bfloat16)
    for h in range(8):
        ehF_np[:, h * 32 + h] = 1
    ehF_c = nc.inline_tensor(ehF_np, name="ehF_c").ap()
    ehA_np = np.zeros((128, 256), dtype=ml_dtypes.bfloat16)
    for o in range(8):
        ehA_np[0:64, o * 32 + 2 * o] = 1
        ehA_np[64:128, o * 32 + 2 * o + 1] = 1
    ehA_c = nc.inline_tensor(ehA_np, name="ehA_c").ap()

    NS = len(sublayers)
    HT = TOK // 2
    klocA = [nc.dram_tensor(f"klocA{s}", [D, HT], BF16, kind="Internal").ap() for s in range(NS)]
    klocB = [nc.dram_tensor(f"klocB{s}", [D, HT], BF16, kind="Internal").ap() for s in range(NS)]
    vlocA = [nc.dram_tensor(f"vlocA{s}", [HT, D], BF16, kind="Internal").ap() for s in range(NS)]
    vlocB = [nc.dram_tensor(f"vlocB{s}", [HT, D], BF16, kind="Internal").ap() for s in range(NS)]
    kgA = [nc.dram_tensor(f"kgA{s}", [4 * D, HT], BF16, kind="Internal").ap() for s in range(NS)]
    kgB = [nc.dram_tensor(f"kgB{s}", [4 * D, HT], BF16, kind="Internal").ap() for s in range(NS)]
    vgA = [nc.dram_tensor(f"vgA{s}", [4 * HT, D], BF16, kind="Internal").ap() for s in range(NS)]
    vgB = [nc.dram_tensor(f"vgB{s}", [4 * HT, D], BF16, kind="Internal").ap() for s in range(NS)]
    bnd = [nc.dram_tensor(f"bnd{s}", [32, 1], F32, kind="Internal").ap() for s in range(NS)]
    bndg = [nc.dram_tensor(f"bndg{s}", [128, 1], F32, kind="Internal").ap() for s in range(NS)]

    S = lambda n, sh, dt=F32: nc.alloc_sbuf_tensor(n, sh, dt).ap()
    x_res = S("x_res", [128, NTILE * D])
    regA = S("regA", [128, 18560], BF16)
    regB = S("regB", [128, 16384], BF16)
    QT = S("QT", [128, 16384], BF16)
    AVacc = S("AVacc", [128, 4096])
    sumsSB = S("sumsSB", [64, 2048])
    wstage = S("wstage", [128, 1024])
    vbias = S("vbias", [128, D])
    biasQK = S("biasQK", [128, 136])
    bqkv = S("bqkv", [128, 24])
    sqbuf = S("sqbuf", [128, 512], BF16)
    small = S("small", [128, 96])
    kqmax = S("kqmax", [64, 1])
    kngT = S("kngT", [64, 4])
    negC = S("negC", [128, 1])
    tiny = S("tiny", [4, 4])
    ident = S("ident", [128, 128], BF16)
    identf = S("identf", [128, 128])
    ones32 = S("ones32", [128, 32], BF16)
    onesf = S("onesf", [128, 128])
    ehF = S("ehF", [128, 256], BF16)
    ehA = S("ehA", [128, 256], BF16)
    mask01 = S("mask01", [128, 8 * 256], BF16)

    A_WF = 0
    A_KST = 8192
    A_VST = 12288
    A_SLAB = 0
    A_P = 16512

    ar = nc.alloc_psum_tensor("ar", [128, 4096], F32).ap()

    C_LNS, C_MU, C_VAR, C_R1, C_R12, C_T1, C_T2 = 0, 1, 2, 3, 4, 5, 6
    C_MUSAVE, C_R1SAVE = 16, 32

    rg = [[0, 1, 2, 3], [4, 5, 6, 7]]

    with contextlib.ExitStack() as ctx_stack:
        sems = {}
        for name in ("qs", "qg", "cc", "ts", "vs", "ss", "ps"):
            sems[name] = ctx_stack.enter_context(nc.semaphore(name))
        blk = ctx_stack.enter_context(nc.Block())

        def load_w_slabs(c, tag, wt, gates, scale=None, dstT=None):
            sync_gate, dve_gate = gates if gates else (None, None)
            if dstT is None:
                dstT, dbase = regA, A_WF
            else:
                dstT, dbase = dstT, 0
            for m in range(8):
                if m >= 1:
                    c.wait("sync", tag + f"wf{m - 1}")
                if m == 0 and sync_gate is not None:
                    c.wait("sync", sync_gate)
                c.dma("sync", wstage, wt[m * 128:(m + 1) * 128, :])
                c.mark(tag + f"wl{m}", "qs")
                c.wait("vector", tag + f"wl{m}")
                if m == 0 and dve_gate is not None:
                    c.wait("vector", dve_gate)
                if scale is None:
                    c.do("vector", lambda e, m=m, dstT=dstT, dbase=dbase: e.tensor_copy(
                        out=dstT[:, dbase + m * D:dbase + (m + 1) * D],
                        in_=wstage), inc="vs")
                else:
                    c.do("vector", lambda e, m=m, dstT=dstT, dbase=dbase: e.tensor_scalar_mul(
                        out=dstT[:, dbase + m * D:dbase + (m + 1) * D],
                        in0=wstage, scalar1=scale), inc="vs")
                c.mark(tag + f"wf{m}", "vs")

        def emit_sublayer(c, si, kind):
            pfx = f"s{si}_"
            isffn = kind == FFN
            scale = (1.0 / 32.0) if isffn else (1.0 / 8.0)
            nm_tc = [8, 4, 2, 1] if isffn else [8, 8, 8, 8]
            mout_tc = [1024, 512, 256, 128] if isffn else [1024] * 4
            WQ, WK, WV, WO = (fqwT, fkwT, fvwT, fowT) if isffn else (aWq, aWk, aWv, aWo)
            BQ, BK, BVR = (fqbT, fkbT, fvbR) if isffn else (abqT, abkT, abvR)

            c.dma("gpsimd", bqkv[:, 0:8], BQ)
            c.dma("gpsimd", bqkv[:, 8:16], BK)
            if isffn:
                c.dma("gpsimd", bqkv[:, 16:24], fnbT)
                c.dma("gpsimd", biasQK[:, 128:136], fnwT)
            c.mark(pfx + "bias_in", "qg")

            c.wait("vector", "x_ready")
            c.wait("vector", "x_ready_p")
            if isffn:
                c.wait("vector", pfx + "bias_in")
            for tb in range(NTILE):
                c.dof("vector", lambda e, tb=tb: e.reduce_sum(
                    out=small[:, tb:tb + 1], in_=x_res[:, tb * D:(tb + 1) * D],
                    axis=AX.X))
            c.dof("vector", lambda e: e.tensor_scalar_mul(
                out=small[:, 16:32], in0=small[:, 0:16], scalar1=1.0 / D))
            for tb in range(NTILE):
                c.dof("vector", lambda e, tb=tb: e.scalar_tensor_tensor(
                    out=wstage, in0=x_res[:, tb * D:(tb + 1) * D],
                    scalar=small[:, 16 + tb:17 + tb],
                    in1=x_res[:, tb * D:(tb + 1) * D],
                    op0=ALU.subtract, op1=ALU.mult,
                    accum_out=small[:, tb:tb + 1]))
            c.dof("vector", lambda e: e.tensor_scalar(
                out=small[:, 0:16], in0=small[:, 0:16],
                scalar1=1.0 / D, scalar2=EPS, op0=ALU.mult, op1=ALU.add))
            c.mark(pfx + "lnv", "vs")
            c.wait("scalar", pfx + "lnv")
            c.dof("scalar", lambda e: e.activation(
                out=small[:, 48:64], in_=small[:, 0:16], func=AF.Ln))
            c.dof("scalar", lambda e: e.activation(
                out=small[:, C_R1SAVE:C_R1SAVE + 16], in_=small[:, 48:64],
                func=AF.Exp, scale=-0.5))
            c.mark(pfx + "r1b", "ss")
            if isffn:
                c.wait("vector", pfx + "r1b")
                c.dof("vector", lambda e: e.reciprocal(
                    out=small[:, 48:64], in_=small[:, 0:16]))
                c.dof("vector", lambda e: e.tensor_scalar(
                    out=small[:, 0:16], in0=small[:, 0:16],
                    scalar1=-EPS, scalar2=None, op0=ALU.add))
                c.dof("vector", lambda e: e.tensor_tensor(
                    out=small[:, 0:16], in0=small[:, 0:16],
                    in1=small[:, 48:64], op=ALU.mult))
                c.dof("vector", lambda e: e.tensor_scalar(
                    out=small[:, 0:16], in0=small[:, 0:16],
                    scalar1=EPS, scalar2=None, op0=ALU.add))
                c.mark(pfx + "vyb", "vs")
                c.wait("scalar", pfx + "vyb")
                c.dof("scalar", lambda e: e.activation(
                    out=small[:, 48:64], in_=small[:, 0:16], func=AF.Ln))
                c.dof("scalar", lambda e: e.activation(
                    out=small[:, 0:16], in_=small[:, 48:64],
                    func=AF.Exp, scale=-0.5))
                c.mark(pfx + "r2b", "ss")
                c.wait("vector", pfx + "r2b")
                c.dof("vector", lambda e: e.tensor_tensor(
                    out=small[:, 0:16], in0=small[:, C_R1SAVE:C_R1SAVE + 16],
                    in1=small[:, 0:16], op=ALU.mult))
                c.dof("vector", lambda e: e.tensor_scalar(
                    out=small[:, 64:80], in0=small[:, C_R1SAVE:C_R1SAVE + 16],
                    scalar1=0.5, scalar2=1.0, op0=ALU.mult, op1=ALU.add))
                c.dof("vector", lambda e: e.tensor_tensor(
                    out=small[:, 80:96], in0=small[:, 16:32],
                    in1=small[:, C_R1SAVE:C_R1SAVE + 16], op=ALU.mult))
                c.dof("vector", lambda e: e.tensor_scalar_mul(
                    out=small[:, 80:96], in0=small[:, 80:96], scalar1=-0.5))
            else:
                c.wait("vector", pfx + "r1b")
            zt_n = 0
            for tb in range(NTILE):
                if tb >= 1:
                    c.wait("vector", pfx + f"ztt{(tb - 1) * 8 + 7}")
                sc2 = (small[:, tb:tb + 1] if isffn
                       else small[:, C_R1SAVE + tb:C_R1SAVE + tb + 1])
                c.dof("vector", lambda e, tb=tb, sc2=sc2: e.tensor_scalar(
                    out=wstage, in0=x_res[:, tb * D:(tb + 1) * D],
                    scalar1=small[:, 16 + tb:17 + tb], scalar2=sc2,
                    op0=ALU.subtract, op1=ALU.mult))
                c.mark(pfx + f"zp{tb}", "vs")
                for mt in range(8):
                    sl = zt_n % 2
                    if zt_n >= 2:
                        c.wait("tensor", pfx + f"zte{zt_n - 2}")
                    c.wait("tensor", pfx + f"zp{tb}")
                    c.do("tensor", lambda e, mt=mt, sl=sl: e.transpose(
                        ar[:, sl * 512:sl * 512 + 128],
                        wstage[:, mt * 128:(mt + 1) * 128], identf), inc="ts")
                    c.mark(pfx + f"ztt{zt_n}", "ts")
                    c.wait("vector", pfx + f"ztt{zt_n}")
                    if isffn:
                        c.do("vector", lambda e, mt=mt, sl=sl, tb=tb: e.tensor_scalar(
                            out=regB[:, mt * TOK + tb * 128:mt * TOK + (tb + 1) * 128],
                            in0=ar[:, sl * 512:sl * 512 + 128],
                            scalar1=biasQK[:, 128 + mt:129 + mt],
                            scalar2=bqkv[:, 16 + mt:17 + mt],
                            op0=ALU.mult, op1=ALU.add), inc="vs")
                    else:
                        c.do("vector", lambda e, mt=mt, sl=sl, tb=tb: e.tensor_copy(
                            out=regB[:, mt * TOK + tb * 128:mt * TOK + (tb + 1) * 128],
                            in_=ar[:, sl * 512:sl * 512 + 128]), inc="vs")
                    c.mark(pfx + f"zte{zt_n}", "vs")
                    zt_n += 1
            c.mark(pfx + "zT_done", "vs")

            load_w_slabs(c, pfx + "v", WV, (pfx + "zT_done", "x_ready"))
            nvm = 0
            for tc in range(4):
                nm = nm_tc[tc]
                if tc == 0:
                    c.wait("gpsimd", "x_ready")
                    c.dma("gpsimd", vbias, BVR)
                    c.mark(pfx + "vbias0", "qg")
                for tb in range(tc * 4, tc * 4 + 4):
                    vslot = A_VST + (tb % 2) * D
                    if tb >= 2:
                        c.wait("vector", pfx + f"vw{tb - 2}")
                    for oh in range(2):
                        par = nvm % 2
                        if nvm >= 2:
                            c.wait("tensor", pfx + f"vev{nvm - 2}")
                        for m in range(nm):
                            c.wait("tensor", pfx + f"vwf{m}")
                            c.do("tensor", lambda e, m=m, tb=tb, oh=oh, par=par, nm=nm: e.matmul(
                                ar[:, 1024 + par * 512:1024 + (par + 1) * 512],
                                regB[:, m * TOK + tb * 128:m * TOK + (tb + 1) * 128],
                                regA[:, A_WF + m * D + oh * 512:A_WF + m * D + (oh + 1) * 512],
                                start=(m == 0), stop=(m == nm - 1)), inc="ts")
                        c.mark(pfx + f"vmm{nvm}", "ts")
                        c.wait("vector", pfx + f"vmm{nvm}")
                        c.wait("vector", pfx + "vbias0")
                        c.do("vector", lambda e, par=par, oh=oh, vslot=vslot: e.tensor_tensor(
                            out=regA[:, vslot + oh * 512:vslot + (oh + 1) * 512],
                            in0=ar[:, 1024 + par * 512:1024 + (par + 1) * 512],
                            in1=vbias[:, oh * 512:(oh + 1) * 512], op=ALU.add), inc="vs")
                        c.mark(pfx + f"vev{nvm}", "vs")
                        nvm += 1
                    c.wait("gpsimd", pfx + f"vev{nvm - 1}")
                    vdst = vlocA[si] if tb < 8 else vlocB[si]
                    c.dma("gpsimd", vdst[(tb % 8) * 128:(tb % 8 + 1) * 128, :],
                          regA[:, vslot:vslot + D])
                    c.mark(pfx + f"vw{tb}", "qg")
                    if tb == 7:
                        c.mark(pfx + "vwA", "qg")
            c.mark(pfx + "vwB", "qg")
            c.mark(pfx + "v_p1_done", "ts")

            load_w_slabs(c, pfx + "k", WK, (pfx + "vwf7", pfx + "v_p1_done"))
            c.wait("vector", pfx + "bias_in")
            c.do("vector", lambda e: e.tensor_copy(
                out=biasQK[:, 8:16], in_=bqkv[:, 8:16]), inc="vs")
            c.mark(pfx + "kbias", "vs")
            c.wait("tensor", pfx + "zT_done")
            eh = ehF if isffn else ehA
            nmm = 0
            for tc in range(4):
                nm = nm_tc[tc]
                for o in range(8):
                    par = nmm % 2
                    if nmm >= 2:
                        c.wait("tensor", pfx + f"kev{nmm - 2}")
                    for m in range(nm):
                        c.wait("tensor", pfx + f"kwf{m}")
                        c.do("tensor", lambda e, o=o, m=m, tc=tc, par=par, nm=nm: e.matmul(
                            ar[:, par * 512:(par + 1) * 512],
                            regA[:, A_WF + m * D + o * 128:A_WF + m * D + (o + 1) * 128],
                            regB[:, m * TOK + tc * 512:m * TOK + (tc + 1) * 512],
                            start=(m == 0), stop=(m == nm - 1)), inc="ts")
                    c.mark(pfx + f"kmm{nmm}", "ts")
                    c.wait("scalar", pfx + f"kmm{nmm}")
                    c.wait("scalar", pfx + "kbias")
                    if nmm >= 8:
                        c.wait("scalar", pfx + f"kw{nmm - 8}")
                        c.wait("scalar", pfx + f"ksq{nmm - 8}")
                    c.do("scalar", lambda e, o=o, par=par: e.activation(
                        out=regA[:, A_KST + o * 512:A_KST + (o + 1) * 512],
                        in_=ar[:, par * 512:(par + 1) * 512],
                        func=AF.Identity, bias=biasQK[:, 8 + o:9 + o]), inc="ss")
                    c.mark(pfx + f"kev{nmm}", "ss")
                    c.wait("vector", pfx + f"kev{nmm}")
                    if nmm > 0:
                        c.wait("vector", pfx + f"knmm{nmm - 1}")
                    c.do("vector", lambda e, o=o: e.tensor_tensor(
                        out=sqbuf,
                        in0=regA[:, A_KST + o * 512:A_KST + (o + 1) * 512],
                        in1=regA[:, A_KST + o * 512:A_KST + (o + 1) * 512],
                        op=ALU.mult), inc="vs")
                    c.mark(pfx + f"ksq{nmm}", "vs")
                    c.wait("tensor", pfx + f"ksq{nmm}")
                    c.wait("tensor", "consts")
                    if o == 0 and tc > 0:
                        c.wait("tensor", pfx + f"kqmax{tc - 1}")
                    c.do("tensor", lambda e, o=o, eh=eh: e.matmul(
                        ar[0:32, 2048:2560], eh[:, o * 32:(o + 1) * 32], sqbuf,
                        start=(o == 0), stop=(o == 7),
                        tile_position=(0, 0)), inc="ts")
                    c.mark(pfx + f"knmm{nmm}", "ts")
                    c.wait("gpsimd", pfx + f"kev{nmm}")
                    kdst = klocA[si] if tc < 2 else klocB[si]
                    c.dma("gpsimd",
                          kdst[o * 128:(o + 1) * 128, (tc % 2) * 512:(tc % 2 + 1) * 512],
                          regA[:, A_KST + o * 512:A_KST + (o + 1) * 512])
                    c.mark(pfx + f"kw{nmm}", "qg")
                    if nmm == 15:
                        c.mark(pfx + "kwA", "qg")
                    nmm += 1
                c.wait("vector", pfx + f"knmm{nmm - 1}")
                c.dof("vector", lambda e: e.reduce_max(
                    out=sumsSB[0:32, 0:1], in_=ar[0:32, 2048:2560], axis=AX.X))
                if tc == 0:
                    c.dof("vector", lambda e: e.tensor_copy(
                        out=kqmax[0:32, 0:1], in_=sumsSB[0:32, 0:1]))
                else:
                    c.dof("vector", lambda e: e.tensor_max(
                        out=kqmax[0:32, 0:1], in0=kqmax[0:32, 0:1],
                        in1=sumsSB[0:32, 0:1]))
                c.mark(pfx + f"kqmax{tc}", "vs")
            c.mark(pfx + "kwB", "qg")
            c.mark(pfx + "k_p2_done", "ts")
            c.wait("vector", pfx + "kqmax3")
            c.dof("vector", lambda e: e.tensor_copy(
                out=sumsSB[0:32, 1:2], in_=kqmax[0:32, 0:1]))
            c.mark(pfx + "knloc", "vs")
            c.wait("gpsimd", pfx + "knloc")
            c.dma("gpsimd", bnd[si], sumsSB[0:32, 1:2])
            c.mark(pfx + "bndw", "qg")
            c.wait("gpsimd", pfx + "bndw")
            c.wait("gpsimd", pfx + "kwB")
            c.wait("gpsimd", pfx + "vwB")
            c.do("gpsimd", lambda e: e.collective_compute(
                "AllGather", ALU.bypass, replica_groups=rg,
                ins=[bnd[si].opt()], outs=[bndg[si].opt()]), inc="cc")
            c.mark(pfx + "cc_bnd", "cc")
            c.do("gpsimd", lambda e: e.collective_compute(
                "AllGather", ALU.bypass, replica_groups=rg,
                ins=[klocA[si].opt()], outs=[kgA[si].opt()]), inc="cc")
            c.mark(pfx + "cc_kA", "cc")
            c.do("gpsimd", lambda e: e.collective_compute(
                "AllGather", ALU.bypass, replica_groups=rg,
                ins=[klocB[si].opt()], outs=[kgB[si].opt()]), inc="cc")
            c.mark(pfx + "cc_kB", "cc")
            c.do("gpsimd", lambda e: e.collective_compute(
                "AllGather", ALU.bypass, replica_groups=rg,
                ins=[vlocA[si].opt()], outs=[vgA[si].opt()]), inc="cc")
            c.mark(pfx + "cc_vA", "cc")
            c.do("gpsimd", lambda e: e.collective_compute(
                "AllGather", ALU.bypass, replica_groups=rg,
                ins=[vlocB[si].opt()], outs=[vgB[si].opt()]), inc="cc")
            c.mark(pfx + "cc_vB", "cc")
            c.wait("gpsimd", pfx + "cc_bnd")
            c.dma("gpsimd", kngT[32:64, 0:4],
                  bndg[si].rearrange("(r h) c -> h (r c)", h=32),
                  allow_slow_non_contiguous=True)
            c.mark(pfx + "bnd_rd", "qg")
            c.wait("vector", pfx + "bnd_rd")
            c.dof("vector", lambda e: e.reduce_max(
                out=kngT[32:64, 0:1], in_=kngT[32:64, 0:4], axis=AX.X))
            c.mark(pfx + "knglob", "vs")

            load_w_slabs(c, pfx + "q", WQ, (pfx + "kwf7", pfx + "k_p2_done"))
            c.wait("vector", pfx + "bias_in")
            c.do("vector", lambda e: e.tensor_copy(
                out=biasQK[:, 0:8], in_=bqkv[:, 0:8]), inc="vs")
            c.mark(pfx + "qbias", "vs")
            c.wait("tensor", pfx + "zT_done")
            nmm = 0
            for tc in range(4):
                nm = nm_tc[tc]
                for o in range(8):
                    par = nmm % 2
                    if nmm >= 2:
                        c.wait("tensor", pfx + f"qev{nmm - 2}")
                    else:
                        c.wait("tensor", pfx + f"kev{30 + nmm}")
                    for m in range(nm):
                        c.wait("tensor", pfx + f"qwf{m}")
                        c.do("tensor", lambda e, o=o, m=m, tc=tc, par=par, nm=nm: e.matmul(
                            ar[:, par * 512:(par + 1) * 512],
                            regA[:, A_WF + m * D + o * 128:A_WF + m * D + (o + 1) * 128],
                            regB[:, m * TOK + tc * 512:m * TOK + (tc + 1) * 512],
                            start=(m == 0), stop=(m == nm - 1)), inc="ts")
                    c.mark(pfx + f"qmm{nmm}", "ts")
                    c.wait("scalar", pfx + f"qmm{nmm}")
                    c.wait("scalar", pfx + "qbias")
                    c.do("scalar", lambda e, o=o, tc=tc, par=par: e.activation(
                        out=QT[:, o * TOK + tc * 512:o * TOK + (tc + 1) * 512],
                        in_=ar[:, par * 512:(par + 1) * 512],
                        func=AF.Identity, bias=biasQK[:, o:o + 1]), inc="ss")
                    c.mark(pfx + f"qev{nmm}", "ss")
                    nmm += 1
                for o in range(8):
                    if o == 0:
                        c.wait("vector", pfx + f"qev{tc * 8 + 7}")
                        c.wait("vector", pfx + ("knmm31" if tc == 0
                                                else f"qnmm{tc - 1}_7"))
                    else:
                        c.wait("vector", pfx + f"qnmm{tc}_{o - 1}")
                    c.do("vector", lambda e, o=o, tc=tc: e.tensor_tensor(
                        out=sqbuf,
                        in0=QT[:, o * TOK + tc * 512:o * TOK + (tc + 1) * 512],
                        in1=QT[:, o * TOK + tc * 512:o * TOK + (tc + 1) * 512],
                        op=ALU.mult), inc="vs")
                    c.mark(pfx + f"qsq{tc}_{o}", "vs")
                    c.wait("tensor", pfx + f"qsq{tc}_{o}")
                    if tc == 0 and o == 0:
                        c.wait("tensor", pfx + "kqmax3")
                    if o == 0 and tc > 0:
                        c.wait("tensor", pfx + f"qqmax{tc - 1}")
                    c.do("tensor", lambda e, o=o, eh=eh: e.matmul(
                        ar[32:64, 2048:2560], eh[:, o * 32:(o + 1) * 32], sqbuf,
                        start=(o == 0), stop=(o == 7),
                        tile_position=(0, 32)), inc="ts")
                    c.mark(pfx + f"qnmm{tc}_{o}", "ts")
                c.wait("vector", pfx + f"qnmm{tc}_7")
                c.dof("vector", lambda e: e.reduce_max(
                    out=sumsSB[32:64, 0:1], in_=ar[32:64, 2048:2560], axis=AX.X))
                if tc == 0:
                    c.dof("vector", lambda e: e.tensor_copy(
                        out=kqmax[32:64, 0:1], in_=sumsSB[32:64, 0:1]))
                else:
                    c.dof("vector", lambda e: e.tensor_max(
                        out=kqmax[32:64, 0:1], in0=kqmax[32:64, 0:1],
                        in1=sumsSB[32:64, 0:1]))
                c.mark(pfx + f"qqmax{tc}", "vs")
            c.mark(pfx + "q_p2_done", "ts")

            c.wait("vector", pfx + "qqmax3")
            c.wait("vector", pfx + "knglob")
            c.dof("vector", lambda e: e.tensor_tensor(
                out=kngT[32:64, 1:2], in0=kqmax[32:64, 0:1],
                in1=kngT[32:64, 0:1], op=ALU.add))
            c.mark(pfx + "cmb", "vs")
            c.wait("tensor", pfx + "cmb")
            c.wait("tensor", pfx + "q_p2_done")
            c.do("tensor", lambda e: e.transpose(
                ar[0:1, 2560:2592], kngT[32:64, 1:2], identf[32:64, 32:64]),
                inc="ts")
            c.mark(pfx + "cmbT", "ts")
            c.wait("vector", pfx + "cmbT")
            c.dof("vector", lambda e: e.reduce_max(
                out=tiny[0:1, 0:1], in_=ar[0:1, 2560:2592], axis=AX.X))
            c.dof("vector", lambda e, scale=scale: e.tensor_scalar_mul(
                out=tiny[0:1, 1:2], in0=tiny[0:1, 0:1], scalar1=-0.5 * scale))
            c.mark(pfx + "c11", "vs")
            c.wait("tensor", pfx + "c11")
            c.do("tensor", lambda e: e.matmul(
                ar[:, 2560:2561], onesf[0:1, :], tiny[0:1, 1:2],
                start=True, stop=True), inc="ts")
            c.mark(pfx + "cbc", "ts")
            c.wait("vector", pfx + "cbc")
            c.do("vector", lambda e: e.tensor_copy(out=negC, in_=ar[:, 2560:2561]), inc="vs")
            c.mark(pfx + "negC", "vs")

            emit_attention(c, si, kind, pfx, scale)

            emit_phase3(c, si, kind, pfx, nm_tc, mout_tc, WO)

        def emit_attention(c, si, kind, pfx, scale):
            isffn = kind == FFN
            ngrp = 4 if isffn else 8
            nktb = 4 if isffn else 2
            ktb_ranks = 1 if isffn else 2
            SLABW = 8192 if isffn else 8256
            if not isffn:
                for gate in ("q_p2_done", "ksq31", "vev31", "kwB", "vwB"):
                    c.wait("vector", pfx + gate)
                for par in range(2):
                    for hh in range(2):
                        for rr in range(2):
                            vr = A_SLAB + par * SLABW + 4096 + hh * 2080 + rr * 1040
                            c.do("vector", lambda e, vr=vr: e.memset(
                                regA[:, vr:vr + 1040]
                                .rearrange("p (j d) -> p j d", d=65)[:, :, 64:65],
                                1.0), inc="vs")
                c.mark(pfx + "vones", "vs")

            def emit_slab_dma(g, ktb):
                    par = ktb % 2
                    base = A_SLAB + par * SLABW
                    seq = g * nktb + ktb
                    if seq == 0:
                        c.wait("sync", pfx + "q_p2_done")
                    elif seq == 1:
                        c.wait("sync", pfx + "ksq31")
                        c.wait("sync", pfx + "vev31")
                        c.wait("sync", pfx + "kwB")
                        c.wait("sync", pfx + "vwB")
                    else:
                        pg, pk = divmod(seq - 2, nktb)
                        c.wait("sync", pfx + f"avdone_g{pg}k{pk}")
                    for half, ccm in (("A", "cc_vA"), ("B", "cc_vB")):
                        c.wait("sync", pfx + ccm)
                        vgh = (vgA if half == "A" else vgB)[si]
                        ho = 0 if half == "A" else 1024
                        for rr in range(ktb_ranks):
                            r = ktb * ktb_ranks + rr
                            for hh in range(2):
                                h = g * 2 + hh
                                if isffn:
                                    c.dma("sync",
                                          regA[:, base + 4096 + hh * 2048 + ho:
                                               base + 4096 + hh * 2048 + ho + 1024]
                                          .rearrange("p (j d) -> p j d", d=128),
                                          vgh[r * HT:(r + 1) * HT, h * 128:(h + 1) * 128]
                                          .rearrange("(j p) d -> p j d", p=128))
                                else:
                                    vr = (base + 4096 + hh * 2080 + rr * 1040 +
                                          (0 if half == "A" else 520))
                                    c.dma("sync",
                                          regA[:, vr:vr + 520]
                                          .rearrange("p (j d) -> p j d", d=65)[:, :, 0:64],
                                          vgh[r * HT:(r + 1) * HT, h * 64:(h + 1) * 64]
                                          .rearrange("(j p) d -> p j d", p=128))
                        c.mark(pfx + f"slabV{half}_g{g}k{ktb}", "qs")
                    for half, ccm in (("A", "cc_kA"), ("B", "cc_kB")):
                        c.wait("sync", pfx + ccm)
                        kgh = (kgA if half == "A" else kgB)[si]
                        ho = 0 if half == "A" else 1024
                        for rr in range(ktb_ranks):
                            r = ktb * ktb_ranks + rr
                            for hh in range(2):
                                h = g * 2 + hh
                                if isffn:
                                    c.dma("sync",
                                          regA[:, base + hh * 2048 + ho:
                                               base + hh * 2048 + ho + 1024],
                                          kgh[r * D + h * 128:r * D + (h + 1) * 128, :])
                                else:
                                    c.dma("sync",
                                          regA[hh * 64:(hh + 1) * 64,
                                               base + rr * 2048 + ho:
                                               base + rr * 2048 + ho + 1024],
                                          kgh[r * D + h * 64:r * D + (h + 1) * 64, :])
                        c.mark(pfx + f"slabK{half}_g{g}k{ktb}", "qs")

            normq = []
            for g in range(ngrp):
                if g == (2 if isffn else 4):
                    c.mark(pfx + "qtfree", "ts")
                def retire(p, g=g):
                    emit_av(c, pfx, kind, g, *p)
                    if p[11] and p[9] == nktb - 1:
                        emit_normalize_recip(c, pfx, kind, g, p[8])
                        normq.append((g, p[8]))
                emit_slab_dma(g, 0)
                if nktb > 1:
                    emit_slab_dma(g, 1)
                kt_seq = 0
                pends = []
                for ktb in range(nktb):
                    par = ktb % 2
                    base = A_SLAB + par * SLABW
                    for qc in range(4):
                        kts = []
                        for rr in range(ktb_ranks):
                            for j in range(16):
                                if isffn or j < 8 or qc >= j // 4:
                                    kts.append((rr, j))
                        for idx, (rr, j) in enumerate(kts):
                            sl = kt_seq % 2
                            masked = (not isffn) and j >= 8 and qc == j // 4
                            off = 256 if (masked and (j // 2) % 2 == 1) else 0
                            width = 512 - off
                            c.wait("tensor", pfx + f"slabK{'A' if j < 8 else 'B'}_g{g}k{ktb}")
                            c.wait("tensor", pfx + "negC")
                            if kt_seq >= 2:
                                c.wait("tensor", pfx + f"exp{g}_{kt_seq - 2}")
                            if isffn:
                                for hh in range(2):
                                    c.do("tensor", lambda e, hh=hh, j=j, sl=sl, qc=qc, base=base, g=g: e.matmul(
                                        ar[:, sl * 1024 + hh * 512:sl * 1024 + (hh + 1) * 512],
                                        regA[:, base + hh * 2048 + j * 128:base + hh * 2048 + (j + 1) * 128],
                                        QT[:, (g * 2 + hh) * TOK + qc * 512:(g * 2 + hh) * TOK + (qc + 1) * 512],
                                        start=True, stop=True), inc="ts")
                            else:
                                kb = base + rr * 2048 + j * 128
                                qb0 = g * TOK + qc * 512 + off
                                for hh in range(2):
                                    c.do("tensor", lambda e, kb=kb, hh=hh, sl=sl, qb0=qb0, off=off, width=width: e.matmul(
                                        ar[:, sl * 1024 + hh * 512 + off:sl * 1024 + (hh + 1) * 512],
                                        regA[hh * 64:(hh + 1) * 64, kb:kb + 128],
                                        QT[hh * 64:(hh + 1) * 64, qb0:qb0 + width],
                                        start=True, stop=True, tile_position=(hh * 64, 0)), inc="ts")
                            c.mark(pfx + f"qk{g}_{kt_seq}", "ts")
                            if len(pends) >= 2:
                                retire(pends.pop(0))
                            if idx == 2 and normq:
                                emit_normalize_qc(c, pfx, kind, *normq.pop(0))
                            c.wait("scalar", pfx + f"qk{g}_{kt_seq}")
                            c.wait("scalar", pfx + "negC")
                            if kt_seq >= 2:
                                c.wait("scalar", pfx + f"avP{g}_{kt_seq - 2}")
                            c.do("scalar", lambda e, sl=sl: e.activation(
                                out=regA[:, A_P + sl * 1024:A_P + (sl + 1) * 1024],
                                in_=ar[:, sl * 1024:(sl + 1) * 1024],
                                func=AF.Exp, bias=negC, scale=scale), inc="ss")
                            c.mark(pfx + f"exp{g}_{kt_seq}", "ss")
                            if masked:
                                r = ktb * ktb_ranks + rr
                                jk = r if j % 2 == 0 else 7 - r
                                c.wait("gpsimd", pfx + f"exp{g}_{kt_seq}")
                                c.wait("gpsimd", "mask01_ready")
                                for hh in range(2):
                                    c.do("gpsimd", lambda e, sl=sl, hh=hh, off=off, jk=jk: e.tensor_tensor(
                                        out=regA[:, A_P + sl * 1024 + hh * 512 + off:
                                                 A_P + sl * 1024 + hh * 512 + off + 256],
                                        in0=regA[:, A_P + sl * 1024 + hh * 512 + off:
                                                 A_P + sl * 1024 + hh * 512 + off + 256],
                                        in1=mask01[:, jk * 256:(jk + 1) * 256],
                                        op=ALU.mult), inc="ps")
                                c.mark(pfx + f"msk{g}_{kt_seq}", "ps")
                            pends.append((kt_seq, sl, masked, off, width, base, rr, j,
                                          qc, ktb, idx == 0, idx == len(kts) - 1))
                            kt_seq += 1
                    while pends:
                        retire(pends.pop(0))
                    c.mark(pfx + f"avdone_g{g}k{ktb}", "ts")
                    if ktb + 2 < nktb:
                        emit_slab_dma(g, ktb + 2)
            while normq:
                emit_normalize_qc(c, pfx, kind, *normq.pop(0))
            c.mark(pfx + "attn_done", "vs")

        def emit_av(c, pfx, kind, g, kt_seq, sl, masked, off, width, base, rr, j, qc,
                    ktb, is_first, is_last):
            isffn = kind == FFN
            c.wait("tensor", pfx + f"slabV{'A' if j < 8 else 'B'}_g{g}k{ktb}")
            c.wait("tensor", pfx + f"exp{g}_{kt_seq}")
            if masked:
                c.wait("tensor", pfx + f"msk{g}_{kt_seq}")
            if isffn and is_first and (ktb > 0 or qc > 0):
                c.wait("tensor", pfx + f"flushA{g}_{ktb}_{qc - 1}" if qc > 0
                       else pfx + f"flushA{g}_{ktb - 1}_3")
            elif isffn and is_first and g > 0:
                c.wait("tensor", pfx + f"flushA{g - 1}_3_3")
            if isffn:
                for hh in range(2):
                    c.do("tensor", lambda e, hh=hh, j=j, sl=sl, base=base, st=is_first: e.matmul(
                        ar[:, 2048 + hh * 512:2048 + (hh + 1) * 512],
                        regA[:, base + 4096 + hh * 2048 + j * 128:base + 4096 + hh * 2048 + (j + 1) * 128],
                        regA[:, A_P + sl * 1024 + hh * 512:A_P + sl * 1024 + (hh + 1) * 512],
                        start=st, stop=False), inc="ts")
                if is_first and (ktb > 0 or qc > 0):
                    c.wait("tensor", pfx + f"flush{g}_{ktb}_{qc - 1}" if qc > 0
                           else pfx + f"flush{g}_{ktb - 1}_3")
                elif is_first and g > 0:
                    c.wait("tensor", pfx + f"flush{g - 1}_3_3")
                for hh in range(2):
                    c.do("tensor", lambda e, hh=hh, sl=sl, st=is_first: e.matmul(
                        ar[hh * 32:(hh + 1) * 32, 3072:3584],
                        ones32,
                        regA[:, A_P + sl * 1024 + hh * 512:A_P + sl * 1024 + (hh + 1) * 512],
                        start=st, stop=False, tile_position=(0, hh * 32)), inc="ts")
            else:
                c.wait("tensor", pfx + "vones")
                if is_first:
                    pred = ktb * 4 + qc - 2
                    pg2 = g
                    if pred < 0:
                        pg2, pred = g - 1, (4 if isffn else 2) * 4 + pred
                    if pg2 >= 0:
                        pk2, pq2 = divmod(pred, 4)
                        if pk2 == (4 if isffn else 2) - 1:
                            c.wait("tensor", pfx + f"nm{pg2}_{pq2}_1")
                        else:
                            c.wait("tensor", pfx + f"flush{pg2}_{pk2}_{pq2}")
                vb = base + 4096
                pa = 2048 + (qc % 2) * 1024
                for hh in range(2):
                    lcol = vb + hh * 2080 + rr * 1040 + j * 65
                    c.do("tensor", lambda e, hh=hh, lcol=lcol, sl=sl, off=off, st=is_first, pa=pa: e.matmul(
                        ar[0:65, pa + hh * 512 + off:pa + hh * 512 + 512],
                        regA[:, lcol:lcol + 65],
                        regA[:, A_P + sl * 1024 + hh * 512 + off:A_P + sl * 1024 + (hh + 1) * 512],
                        start=st, stop=False), inc="ts")
            c.mark(pfx + f"avP{g}_{kt_seq}", "ts")
            if is_last:
                c.wait("vector", pfx + f"avP{g}_{kt_seq}")
                first_blk = ktb == 0
                if first_blk and g > 0:
                    c.wait("vector", pfx + f"norm{g - 1}")
                if not first_blk:
                    c.wait("vector", pfx + f"flush{g}_{ktb - 1}_{qc}")
                if isffn:
                    for hh in range(2):
                        src = ar[:, 2048 + hh * 512:2048 + (hh + 1) * 512]
                        dst = AVacc[:, hh * 2048 + qc * 512:hh * 2048 + (qc + 1) * 512]
                        if first_blk:
                            c.do("vector", lambda e, src=src, dst=dst: e.tensor_copy(
                                out=dst, in_=src), inc="vs")
                        else:
                            c.do("vector", lambda e, src=src, dst=dst: e.tensor_tensor(
                                out=dst, in0=src, in1=dst, op=ALU.add), inc="vs")
                    c.mark(pfx + f"flushA{g}_{ktb}_{qc}", "vs")
                else:
                    pa = 2048 + (qc % 2) * 1024
                    pairs = [
                        (ar[0:64, pa:pa + 512], AVacc[0:64, qc * 512:(qc + 1) * 512]),
                        (ar[0:64, pa + 512:pa + 1024], AVacc[64:128, qc * 512:(qc + 1) * 512]),
                        (ar[64:65, pa:pa + 512], sumsSB[0:1, qc * 512:(qc + 1) * 512]),
                        (ar[64:65, pa + 512:pa + 1024], sumsSB[32:33, qc * 512:(qc + 1) * 512]),
                    ]
                    for src, dst in pairs:
                        if first_blk:
                            c.do("vector", lambda e, src=src, dst=dst: e.tensor_copy(
                                out=dst, in_=src), inc="vs")
                        else:
                            c.do("vector", lambda e, src=src, dst=dst: e.tensor_tensor(
                                out=dst, in0=src, in1=dst, op=ALU.add), inc="vs")
                if isffn:
                    ssrc = ar[0:64, 3072:3584]
                    sdst = sumsSB[0:64, qc * 512:(qc + 1) * 512]
                    if first_blk:
                        c.do("vector", lambda e, ssrc=ssrc, sdst=sdst: e.tensor_copy(
                            out=sdst, in_=ssrc), inc="vs")
                    else:
                        c.do("vector", lambda e, ssrc=ssrc, sdst=sdst: e.tensor_tensor(
                            out=sdst, in0=ssrc, in1=sdst, op=ALU.add), inc="vs")
                c.mark(pfx + f"flush{g}_{ktb}_{qc}", "vs")

        def emit_normalize_recip(c, pfx, kind, g, qc):
            isffn = kind == FFN
            nktb = 4 if isffn else 2
            c.wait("vector", pfx + f"flush{g}_{nktb - 1}_{qc}")
            c.do("vector", lambda e, qc=qc: e.reciprocal(
                out=sumsSB[:, qc * 512:(qc + 1) * 512],
                in_=sumsSB[:, qc * 512:(qc + 1) * 512]), inc="vs")
            c.mark(pfx + f"recip{g}_{qc}", "vs")

        def emit_normalize_qc(c, pfx, kind, g, qc):
            isffn = kind == FFN
            PS = 3584
            if isffn:
                for hh in range(2):
                    c.wait("tensor", pfx + f"recip{g}_{qc}")
                    prev = (pfx + f"nm{g}_{qc}_{hh - 1}" if hh == 1 else
                            (pfx + f"nm{g}_{qc - 1}_1" if qc > 0 else
                             (pfx + f"nm{g - 1}_3_1" if g > 0 else None)))
                    if prev is not None:
                        c.wait("tensor", prev)
                    c.do("tensor", lambda e, hh=hh, qc=qc: e.matmul(
                        ar[:, PS:PS + 512],
                        onesf[hh * 32:hh * 32 + 1, :],
                        sumsSB[hh * 32:hh * 32 + 1, qc * 512:(qc + 1) * 512],
                        start=True, stop=True,
                        tile_position=(hh * 32, 0)), inc="ts")
                    c.mark(pfx + f"nbc{g}_{qc}_{hh}", "ts")
                    c.wait("vector", pfx + f"nbc{g}_{qc}_{hh}")
                    c.do("vector", lambda e, hh=hh, qc=qc: e.tensor_tensor(
                        out=regB[:, (2 * g + hh) * TOK + qc * 512:
                                 (2 * g + hh) * TOK + (qc + 1) * 512],
                        in0=ar[:, PS:PS + 512],
                        in1=AVacc[:, hh * 2048 + qc * 512:hh * 2048 + (qc + 1) * 512],
                        op=ALU.mult), inc="vs")
                    c.mark(pfx + f"nm{g}_{qc}_{hh}", "vs")
            else:
                PSA = 2048 + (qc % 2) * 1024
                c.wait("tensor", pfx + f"recip{g}_{qc}")
                prev = (pfx + f"nm{g}_{qc - 1}_1" if qc > 0 else
                        (pfx + f"nm{g - 1}_3_1" if g > 0 else None))
                if prev is not None:
                    c.wait("tensor", prev)
                c.do("tensor", lambda e, qc=qc, PSA=PSA: e.matmul(
                    ar[0:64, PSA:PSA + 512],
                    onesf[0:1, 0:64],
                    sumsSB[0:1, qc * 512:(qc + 1) * 512],
                    start=True, stop=True, tile_position=(0, 0)), inc="ts")
                c.do("tensor", lambda e, qc=qc, PSA=PSA: e.matmul(
                    ar[64:128, PSA:PSA + 512],
                    onesf[32:33, 0:64],
                    sumsSB[32:33, qc * 512:(qc + 1) * 512],
                    start=True, stop=True, tile_position=(32, 64)), inc="ts")
                c.mark(pfx + f"nbc{g}_{qc}_1", "ts")
                c.wait("vector", pfx + f"nbc{g}_{qc}_1")
                c.do("vector", lambda e, qc=qc, PSA=PSA: e.tensor_tensor(
                    out=regB[:, g * TOK + qc * 512:g * TOK + (qc + 1) * 512],
                    in0=ar[:, PSA:PSA + 512],
                    in1=AVacc[:, qc * 512:(qc + 1) * 512],
                    op=ALU.mult), inc="vs")
                c.mark(pfx + f"nm{g}_{qc}_1", "vs")
            if qc == 3:
                c.mark(pfx + f"norm{g}", "vs")

        def emit_phase3(c, si, kind, pfx, nm_tc, mout_tc, WO):
            isffn = kind == FFN
            load_w_slabs(c, pfx + "o", WO,
                         (pfx + "qwf7", pfx + "qtfree"),
                         scale=(0.5 if isffn else None), dstT=QT)
            if isffn:
                c.wait("gpsimd", pfx + "vev31")
                c.dma("gpsimd", vbias, fobR)
                c.mark(pfx + "obr_in", "qg")
                c.wait("scalar", pfx + "obr_in")
                c.do("scalar", lambda e: e.mul(out=vbias, in_=vbias, mul=0.5), inc="ss")
                c.mark(pfx + "obhalf", "ss")
            npo = 0
            for tc in range(4):
                mo = mout_tc[tc]
                nmch = nm_tc[tc] if isffn else 8
                noch = (mo + 511) // 512
                for tb in range(tc * 4, tc * 4 + 4):
                    if isffn:
                        c.wait("vector", pfx + "owf7")
                        c.wait("vector", pfx + "obhalf")
                        c.dof("vector", lambda e, tb=tb: e.tensor_scalar(
                            out=x_res[:, tb * D:(tb + 1) * D],
                            in0=x_res[:, tb * D:(tb + 1) * D],
                            scalar1=small[:, 64 + tb:65 + tb],
                            scalar2=small[:, 80 + tb:81 + tb],
                            op0=ALU.mult, op1=ALU.add))
                        c.dof("vector", lambda e, tb=tb, mo=mo: e.tensor_tensor(
                            out=x_res[:, tb * D:tb * D + mo], in0=vbias[:, 0:mo],
                            in1=x_res[:, tb * D:tb * D + mo], op=ALU.add))
                        c.mark(pfx + f"pre{tb}", "vs")
                    for och in range(noch):
                        w = min(512, mo - och * 512)
                        par = npo % 2
                        if npo >= 2:
                            c.wait("tensor", pfx + f"oev{npo - 2}")
                        c.wait("tensor", pfx + "attn_done")
                        for m in range(nmch):
                            c.wait("tensor", pfx + f"owf{m}")
                            c.do("tensor", lambda e, m=m, tb=tb, och=och, par=par, w=w, nmch=nmch: e.matmul(
                                ar[:, par * 512:par * 512 + w],
                                regB[:, m * TOK + tb * 128:m * TOK + (tb + 1) * 128],
                                QT[:, m * D + och * 512:m * D + och * 512 + w],
                                start=(m == 0), stop=(m == nmch - 1)), inc="ts")
                        c.mark(pfx + f"omm{npo}", "ts")
                        c.wait("vector", pfx + f"omm{npo}")
                        c.do("vector", lambda e, par=par, tb=tb, och=och, w=w: e.tensor_tensor(
                            out=x_res[:, tb * D + och * 512:tb * D + och * 512 + w],
                            in0=ar[:, par * 512:par * 512 + w],
                            in1=x_res[:, tb * D + och * 512:tb * D + och * 512 + w],
                            op=ALU.add), inc="vs")
                        c.mark(pfx + f"oev{npo}", "vs")
                        npo += 1
                    c.mark(pfx + f"res{tb}", "vs")
            c.mark("x_ready", "vs")
            c.mark("x_ready_p", "ps")

        def emit_all(c: Ctx):
            c.dma("sync", x_res.rearrange("p (t f) -> p t f", f=D),
                  x_in.rearrange("(t p) f -> p t f", p=128))
            c.mark("x_in_dma", "qs")
            for dst, src in ((ident, ident_c), (identf, identf_c),
                             (ones32, ones32_c), (onesf, onesf_c),
                             (ehF, ehF_c), (ehA, ehA_c), (mask01, maskin)):
                c.dma("gpsimd", dst, src)
            c.mark("consts", "qg")
            c.mark("mask01_ready", "qg")
            c.wait("vector", "x_in_dma")
            c.do("vector", lambda e: e.memset(tiny, 0.0), inc="vs")
            c.mark("x_ready", "vs")
            c.mark("x_ready_p", "ps")


            for si, kind in enumerate(sublayers):
                emit_sublayer(c, si, kind)
                if si in taps:
                    c.wait("sync", "x_ready")
                    c.dma("sync", taps[si].rearrange("(t p) f -> p t f", p=128),
                          x_res.rearrange("p (t f) -> p t f", f=D))
                    c.mark(f"tap{si}", "qs")

            lastp = f"s{len(sublayers) - 1}_" if sublayers else ""
            xo_r = x_out.rearrange("(t p) f -> p t f", p=128)
            for tb in range(NTILE):
                if sublayers:
                    c.wait("sync", lastp + f"res{tb}")
                c.dma("sync", xo_r[:, tb:tb + 1, :],
                      x_res[:, tb * D:(tb + 1) * D].rearrange(
                          "p (t f) -> p t f", f=D))
            if DEBUG_DUMPS:
                dump_srcs = {"kloc0": klocA[0], "vloc0": vlocA[0], "kg0": kgA[0],
                             "vg0": vgA[0], "bndg0": bndg[0], "QT": QT,
                             "sums": sumsSB, "AVacc": AVacc, "attnT": regB,
                             "negC": negC, "x_res_d": x_res}
                for nm in DEBUG_DUMPS:
                    c.dma("sync", dumps_out[nm], dump_srcs[nm])
            c.mark("out_stored", "qs")
            c.wait("sync", "out_stored")

        blk.sync(lambda eng: emit_all(Ctx("sync", eng, sems)))
        blk.tensor(lambda eng: emit_all(Ctx("tensor", eng, sems)))
        blk.vector(lambda eng: emit_all(Ctx("vector", eng, sems)))
        blk.scalar(lambda eng: emit_all(Ctx("scalar", eng, sems)))
        blk.gpsimd(lambda eng: emit_all(Ctx("gpsimd", eng, sems)))

    return nc


def _mask_block(jk, cloc):
    m = np.zeros((128, 256), np.float32)
    for blkcol, tq in enumerate((cloc, 7 - cloc)):
        sl = slice(blkcol * 128, (blkcol + 1) * 128)
        if tq > jk:
            m[:, sl] = 1.0
        elif tq == jk:
            k = np.arange(128)[:, None]
            q = np.arange(128)[None, :]
            m[:, sl] = (k <= q).astype(np.float32)
    return m


def kernel(L0_attn_x, L0_attn_Wq, L0_attn_bq, L0_attn_Wk, L0_attn_bk, L0_attn_Wv,
           L0_attn_bv, L0_attn_Wo, L0_ffn_router_prob, L0_ffn_norm_w, L0_ffn_norm_b,
           L0_ffn_q_w, L0_ffn_q_b, L0_ffn_k_w, L0_ffn_k_b, L0_ffn_v_w, L0_ffn_v_b,
           L0_ffn_o_w, L0_ffn_o_b):
    x = np.ascontiguousarray(np.asarray(L0_attn_x, np.float32))
    B, T, _ = x.shape
    xt = x.reshape(B, 64, 128, D)
    assert np.asarray(L0_ffn_router_prob).shape[-1] == 4

    tileT = lambda v: np.ascontiguousarray(np.asarray(v, np.float32).reshape(8, 128).T)
    rep = lambda v: np.ascontiguousarray(
        np.broadcast_to(np.asarray(v, np.float32), (128, D)))
    tr = lambda w: np.ascontiguousarray(np.asarray(w, np.float32).T)
    natw = lambda w: np.ascontiguousarray(np.asarray(w, np.float32))

    common = dict(
        aWq=natw(L0_attn_Wq), aWk=natw(L0_attn_Wk), aWv=natw(L0_attn_Wv),
        aWo=natw(L0_attn_Wo),
        abqT=tileT(L0_attn_bq), abkT=tileT(L0_attn_bk), abvR=rep(L0_attn_bv),
        fqwT=tr(L0_ffn_q_w), fkwT=tr(L0_ffn_k_w), fvwT=tr(L0_ffn_v_w),
        fowT=tr(L0_ffn_o_w),
        fqbT=tileT(L0_ffn_q_b), fkbT=tileT(L0_ffn_k_b), fvbR=rep(L0_ffn_v_b),
        fnwT=tileT(L0_ffn_norm_w), fnbT=tileT(L0_ffn_norm_b), fobR=rep(L0_ffn_o_b),
    )

    in_maps = []
    for cidx in range(8):
        b, cloc = divmod(cidx, 4)
        tm = tmap(cloc)
        xs = np.ascontiguousarray(xt[b, tm].reshape(TOK, D))
        mk = np.concatenate([_mask_block(jk, cloc) for jk in range(8)], axis=1)
        im = dict(common)
        im["x_in"] = xs
        im["maskin"] = np.ascontiguousarray(mk)
        in_maps.append(im)

    nc = build_program()
    res = run_bass_kernel_spmd(nc, in_maps, core_ids=list(range(8)))

    out = np.empty((B, 64, 128, D), np.float32)
    for cidx in range(8):
        b, cloc = divmod(cidx, 4)
        tm = tmap(cloc)
        out[b, tm] = res.results[cidx]["x_out"].reshape(NTILE, 128, D)
    return out.reshape(B, T, D)


# version 10
# speedup vs baseline: 1.0869x; 1.0090x over previous
import numpy as np
import contextlib
import concourse.bass as bass
import concourse.mybir as mybir
from concourse.bass_utils import run_bass_kernel_spmd
import ml_dtypes

F32 = mybir.dt.float32
BF16 = mybir.dt.bfloat16
AF = mybir.ActivationFunctionType
ALU = mybir.AluOpType
AX = mybir.AxisListType

D = 1024
NTILE = 16
TOK = 2048
EPS = 1e-5
FFN = "ffn"
ATT = "attn"

DEBUG_TAPS = ()
DEBUG_DUMPS = ()


def tmap(cloc):
    return sorted([8 * B + cloc for B in range(8)] + [8 * B + 7 - cloc for B in range(8)])


class Ctx:

    def __init__(self, me, eng, sems):
        self.me = me
        self.eng = eng
        self.sems = sems
        self.cnt = {k: 0 for k in sems}
        self.marks = {}
        self.lastw = {}

    def do(self, engname, fn, inc=None, n=1):
        inst = None
        if self.me == engname:
            inst = fn(self.eng)
        if inc is not None:
            self.cnt[inc] += n
            if inst is not None:
                inst.then_inc(self.sems[inc], n)

    SELF = {"vector": "vs", "scalar": "ss", "tensor": "ts", "gpsimd": "ps"}

    def dof(self, engname, fn):
        sem = Ctx.SELF[engname]
        self.do(engname, fn, inc=sem)
        if self.me == engname:
            self.eng.wait_ge(self.sems[sem], self.cnt[sem])
            self.lastw[sem] = self.cnt[sem]

    def dma(self, engname, out, in_, **kw):
        q = "qs" if engname == "sync" else "qg"
        self.do(engname, lambda e: e.dma_start(out=out, in_=in_, **kw), inc=q, n=16)

    def mark(self, name, sem):
        self.marks[name] = (sem, self.cnt[sem])

    def wait(self, engname, markname):
        sem, val = self.marks[markname]
        if self.me == engname and val > 0 and self.lastw.get(sem, -1) < val:
            self.eng.wait_ge(self.sems[sem], val)
            self.lastw[sem] = val


def build_program(sublayers=(FFN, ATT, FFN), debug_taps=None):
    if debug_taps is None:
        debug_taps = DEBUG_TAPS
    nc = bass.Bass("TRN2", target_bir_lowering=False)

    x_in = nc.dram_tensor("x_in", [TOK, D], F32, kind="ExternalInput").ap()
    x_out = nc.dram_tensor("x_out", [TOK, D], F32, kind="ExternalOutput").ap()
    taps = {i: nc.dram_tensor(f"x_tap{i}", [TOK, D], F32, kind="ExternalOutput").ap()
            for i in debug_taps}
    _dump_shapes = {"kloc0": ([D, TOK // 2], BF16), "vloc0": ([TOK // 2, D], BF16),
                    "kg0": ([4 * D, TOK // 2], BF16), "vg0": ([2 * TOK, D], BF16),
                    "bndg0": ([128, 1], F32), "QT": ([128, 16384], BF16),
                    "sums": ([64, 2048], F32), "AVacc": ([128, 4096], F32),
                    "attnT": ([128, 16384], BF16), "negC": ([128, 1], F32),
                    "x_res_d": ([128, NTILE * D], F32)}
    dumps_out = {nm: nc.dram_tensor(f"dump_{nm}", _dump_shapes[nm][0],
                                    _dump_shapes[nm][1], kind="ExternalOutput").ap()
                 for nm in DEBUG_DUMPS}

    aWq = nc.dram_tensor("aWq", [D, D], F32, kind="ExternalInput").ap()
    aWk = nc.dram_tensor("aWk", [D, D], F32, kind="ExternalInput").ap()
    aWv = nc.dram_tensor("aWv", [D, D], F32, kind="ExternalInput").ap()
    aWo = nc.dram_tensor("aWo", [D, D], F32, kind="ExternalInput").ap()
    abqT = nc.dram_tensor("abqT", [128, 8], F32, kind="ExternalInput").ap()
    abkT = nc.dram_tensor("abkT", [128, 8], F32, kind="ExternalInput").ap()
    abvR = nc.dram_tensor("abvR", [128, D], F32, kind="ExternalInput").ap()
    fqwT = nc.dram_tensor("fqwT", [D, D], F32, kind="ExternalInput").ap()
    fkwT = nc.dram_tensor("fkwT", [D, D], F32, kind="ExternalInput").ap()
    fvwT = nc.dram_tensor("fvwT", [D, D], F32, kind="ExternalInput").ap()
    fowT = nc.dram_tensor("fowT", [D, D], F32, kind="ExternalInput").ap()
    fqbT = nc.dram_tensor("fqbT", [128, 8], F32, kind="ExternalInput").ap()
    fkbT = nc.dram_tensor("fkbT", [128, 8], F32, kind="ExternalInput").ap()
    fvbR = nc.dram_tensor("fvbR", [128, D], F32, kind="ExternalInput").ap()
    fnwT = nc.dram_tensor("fnwT", [128, 8], F32, kind="ExternalInput").ap()
    fnbT = nc.dram_tensor("fnbT", [128, 8], F32, kind="ExternalInput").ap()
    fobR = nc.dram_tensor("fobR", [128, D], F32, kind="ExternalInput").ap()
    maskin = nc.dram_tensor("maskin", [128, 8 * 256], F32, kind="ExternalInput").ap()

    ident_c = nc.inline_tensor(np.eye(128, dtype=ml_dtypes.bfloat16), name="ident_c").ap()
    identf_c = nc.inline_tensor(np.eye(128, dtype=np.float32), name="identf_c").ap()
    o32 = np.zeros((128, 32), dtype=ml_dtypes.bfloat16); o32[:, 0] = 1
    ones32_c = nc.inline_tensor(o32, name="ones32_c").ap()
    onesf_c = nc.inline_tensor(np.ones((128, 128), np.float32), name="onesf_c").ap()
    ehF_np = np.zeros((128, 256), dtype=ml_dtypes.bfloat16)
    for h in range(8):
        ehF_np[:, h * 32 + h] = 1
    ehF_c = nc.inline_tensor(ehF_np, name="ehF_c").ap()
    ehA_np = np.zeros((128, 256), dtype=ml_dtypes.bfloat16)
    for o in range(8):
        ehA_np[0:64, o * 32 + 2 * o] = 1
        ehA_np[64:128, o * 32 + 2 * o + 1] = 1
    ehA_c = nc.inline_tensor(ehA_np, name="ehA_c").ap()

    NS = len(sublayers)
    HT = TOK // 2
    klocA = [nc.dram_tensor(f"klocA{s}", [D, HT], BF16, kind="Internal").ap() for s in range(NS)]
    klocB = [nc.dram_tensor(f"klocB{s}", [D, HT], BF16, kind="Internal").ap() for s in range(NS)]
    vlocA = [nc.dram_tensor(f"vlocA{s}", [HT, D], BF16, kind="Internal").ap() for s in range(NS)]
    vlocB = [nc.dram_tensor(f"vlocB{s}", [HT, D], BF16, kind="Internal").ap() for s in range(NS)]
    kgA = [nc.dram_tensor(f"kgA{s}", [4 * D, HT], BF16, kind="Internal").ap() for s in range(NS)]
    kgB = [nc.dram_tensor(f"kgB{s}", [4 * D, HT], BF16, kind="Internal").ap() for s in range(NS)]
    vgA = [nc.dram_tensor(f"vgA{s}", [4 * HT, D], BF16, kind="Internal").ap() for s in range(NS)]
    vgB = [nc.dram_tensor(f"vgB{s}", [4 * HT, D], BF16, kind="Internal").ap() for s in range(NS)]
    bnd = [nc.dram_tensor(f"bnd{s}", [32, 1], F32, kind="Internal").ap() for s in range(NS)]
    bndg = [nc.dram_tensor(f"bndg{s}", [128, 1], F32, kind="Internal").ap() for s in range(NS)]

    S = lambda n, sh, dt=F32: nc.alloc_sbuf_tensor(n, sh, dt).ap()
    x_res = S("x_res", [128, NTILE * D])
    regA = S("regA", [128, 18560], BF16)
    regB = S("regB", [128, 16384], BF16)
    QT = S("QT", [128, 16384], BF16)
    AVacc = S("AVacc", [128, 4096])
    sumsSB = S("sumsSB", [64, 2048])
    wstage = S("wstage", [128, 1024])
    vbias = S("vbias", [128, D])
    biasQK = S("biasQK", [128, 136])
    bqkv = S("bqkv", [128, 24])
    sqbuf = S("sqbuf", [128, 512], BF16)
    small = S("small", [128, 96])
    kqmax = S("kqmax", [64, 1])
    kngT = S("kngT", [64, 4])
    negC = S("negC", [128, 1])
    tiny = S("tiny", [4, 4])
    ident = S("ident", [128, 128], BF16)
    identf = S("identf", [128, 128])
    ones32 = S("ones32", [128, 32], BF16)
    onesf = S("onesf", [128, 128])
    ehF = S("ehF", [128, 256], BF16)
    ehA = S("ehA", [128, 256], BF16)
    mask01 = S("mask01", [128, 8 * 256], BF16)

    A_WF = 0
    A_KST = 8192
    A_VST = 12288
    A_SLAB = 0
    A_P = 16512

    ar = nc.alloc_psum_tensor("ar", [128, 4096], F32).ap()

    C_LNS, C_MU, C_VAR, C_R1, C_R12, C_T1, C_T2 = 0, 1, 2, 3, 4, 5, 6
    C_MUSAVE, C_R1SAVE = 16, 32

    rg = [[0, 1, 2, 3], [4, 5, 6, 7]]

    with contextlib.ExitStack() as ctx_stack:
        sems = {}
        for name in ("qs", "qg", "cc", "ts", "vs", "ss", "ps"):
            sems[name] = ctx_stack.enter_context(nc.semaphore(name))
        blk = ctx_stack.enter_context(nc.Block())

        def load_w_slabs(c, tag, wt, gates, scale=None, dstT=None):
            sync_gate, dve_gate = gates if gates else (None, None)
            if dstT is None:
                dstT, dbase = regA, A_WF
            else:
                dstT, dbase = dstT, 0
            for m in range(8):
                if m >= 1:
                    c.wait("sync", tag + f"wf{m - 1}")
                if m == 0 and sync_gate is not None:
                    c.wait("sync", sync_gate)
                c.dma("sync", wstage, wt[m * 128:(m + 1) * 128, :])
                c.mark(tag + f"wl{m}", "qs")
                c.wait("vector", tag + f"wl{m}")
                if m == 0 and dve_gate is not None:
                    c.wait("vector", dve_gate)
                if scale is None:
                    c.do("vector", lambda e, m=m, dstT=dstT, dbase=dbase: e.tensor_copy(
                        out=dstT[:, dbase + m * D:dbase + (m + 1) * D],
                        in_=wstage), inc="vs")
                else:
                    c.do("vector", lambda e, m=m, dstT=dstT, dbase=dbase: e.tensor_scalar_mul(
                        out=dstT[:, dbase + m * D:dbase + (m + 1) * D],
                        in0=wstage, scalar1=scale), inc="vs")
                c.mark(tag + f"wf{m}", "vs")

        def emit_sublayer(c, si, kind):
            pfx = f"s{si}_"
            isffn = kind == FFN
            scale = (1.0 / 32.0) if isffn else (1.0 / 8.0)
            nm_tc = [8, 4, 2, 1] if isffn else [8, 8, 8, 8]
            mout_tc = [1024, 512, 256, 128] if isffn else [1024] * 4
            WQ, WK, WV, WO = (fqwT, fkwT, fvwT, fowT) if isffn else (aWq, aWk, aWv, aWo)
            BQ, BK, BVR = (fqbT, fkbT, fvbR) if isffn else (abqT, abkT, abvR)

            c.dma("gpsimd", bqkv[:, 0:8], BQ)
            c.dma("gpsimd", bqkv[:, 8:16], BK)
            if isffn:
                c.dma("gpsimd", bqkv[:, 16:24], fnbT)
                c.dma("gpsimd", biasQK[:, 128:136], fnwT)
            c.mark(pfx + "bias_in", "qg")

            c.wait("vector", "x_ready")
            c.wait("vector", "x_ready_p")
            if isffn:
                c.wait("vector", pfx + "bias_in")
            for tb in range(NTILE):
                c.dof("vector", lambda e, tb=tb: e.reduce_sum(
                    out=small[:, tb:tb + 1], in_=x_res[:, tb * D:(tb + 1) * D],
                    axis=AX.X))
            c.dof("vector", lambda e: e.tensor_scalar_mul(
                out=small[:, 16:32], in0=small[:, 0:16], scalar1=1.0 / D))
            for tb in range(NTILE):
                c.dof("vector", lambda e, tb=tb: e.scalar_tensor_tensor(
                    out=wstage, in0=x_res[:, tb * D:(tb + 1) * D],
                    scalar=small[:, 16 + tb:17 + tb],
                    in1=x_res[:, tb * D:(tb + 1) * D],
                    op0=ALU.subtract, op1=ALU.mult,
                    accum_out=small[:, tb:tb + 1]))
            c.dof("vector", lambda e: e.tensor_scalar(
                out=small[:, 0:16], in0=small[:, 0:16],
                scalar1=1.0 / D, scalar2=EPS, op0=ALU.mult, op1=ALU.add))
            c.mark(pfx + "lnv", "vs")
            c.wait("scalar", pfx + "lnv")
            c.dof("scalar", lambda e: e.activation(
                out=small[:, 48:64], in_=small[:, 0:16], func=AF.Ln))
            c.dof("scalar", lambda e: e.activation(
                out=small[:, C_R1SAVE:C_R1SAVE + 16], in_=small[:, 48:64],
                func=AF.Exp, scale=-0.5))
            c.mark(pfx + "r1b", "ss")
            if isffn:
                c.wait("vector", pfx + "r1b")
                c.dof("vector", lambda e: e.reciprocal(
                    out=small[:, 48:64], in_=small[:, 0:16]))
                c.dof("vector", lambda e: e.tensor_scalar(
                    out=small[:, 0:16], in0=small[:, 0:16],
                    scalar1=-EPS, scalar2=None, op0=ALU.add))
                c.dof("vector", lambda e: e.tensor_tensor(
                    out=small[:, 0:16], in0=small[:, 0:16],
                    in1=small[:, 48:64], op=ALU.mult))
                c.dof("vector", lambda e: e.tensor_scalar(
                    out=small[:, 0:16], in0=small[:, 0:16],
                    scalar1=EPS, scalar2=None, op0=ALU.add))
                c.mark(pfx + "vyb", "vs")
                c.wait("scalar", pfx + "vyb")
                c.dof("scalar", lambda e: e.activation(
                    out=small[:, 48:64], in_=small[:, 0:16], func=AF.Ln))
                c.dof("scalar", lambda e: e.activation(
                    out=small[:, 0:16], in_=small[:, 48:64],
                    func=AF.Exp, scale=-0.5))
                c.mark(pfx + "r2b", "ss")
                c.wait("vector", pfx + "r2b")
                c.dof("vector", lambda e: e.tensor_tensor(
                    out=small[:, 0:16], in0=small[:, C_R1SAVE:C_R1SAVE + 16],
                    in1=small[:, 0:16], op=ALU.mult))
                c.dof("vector", lambda e: e.tensor_scalar(
                    out=small[:, 64:80], in0=small[:, C_R1SAVE:C_R1SAVE + 16],
                    scalar1=0.5, scalar2=1.0, op0=ALU.mult, op1=ALU.add))
                c.dof("vector", lambda e: e.tensor_tensor(
                    out=small[:, 80:96], in0=small[:, 16:32],
                    in1=small[:, C_R1SAVE:C_R1SAVE + 16], op=ALU.mult))
                c.dof("vector", lambda e: e.tensor_scalar_mul(
                    out=small[:, 80:96], in0=small[:, 80:96], scalar1=-0.5))
            else:
                c.wait("vector", pfx + "r1b")
            zt_n = 0
            for tb in range(NTILE):
                if tb >= 1:
                    c.wait("vector", pfx + f"ztt{(tb - 1) * 8 + 7}")
                sc2 = (small[:, tb:tb + 1] if isffn
                       else small[:, C_R1SAVE + tb:C_R1SAVE + tb + 1])
                c.dof("vector", lambda e, tb=tb, sc2=sc2: e.tensor_scalar(
                    out=wstage, in0=x_res[:, tb * D:(tb + 1) * D],
                    scalar1=small[:, 16 + tb:17 + tb], scalar2=sc2,
                    op0=ALU.subtract, op1=ALU.mult))
                c.mark(pfx + f"zp{tb}", "vs")
                for mt in range(8):
                    sl = zt_n % 2
                    if zt_n >= 2:
                        c.wait("tensor", pfx + f"zte{zt_n - 2}")
                    c.wait("tensor", pfx + f"zp{tb}")
                    c.do("tensor", lambda e, mt=mt, sl=sl: e.transpose(
                        ar[:, sl * 512:sl * 512 + 128],
                        wstage[:, mt * 128:(mt + 1) * 128], identf), inc="ts")
                    c.mark(pfx + f"ztt{zt_n}", "ts")
                    c.wait("vector", pfx + f"ztt{zt_n}")
                    if isffn:
                        c.do("vector", lambda e, mt=mt, sl=sl, tb=tb: e.tensor_scalar(
                            out=regB[:, mt * TOK + tb * 128:mt * TOK + (tb + 1) * 128],
                            in0=ar[:, sl * 512:sl * 512 + 128],
                            scalar1=biasQK[:, 128 + mt:129 + mt],
                            scalar2=bqkv[:, 16 + mt:17 + mt],
                            op0=ALU.mult, op1=ALU.add), inc="vs")
                    else:
                        c.do("vector", lambda e, mt=mt, sl=sl, tb=tb: e.tensor_copy(
                            out=regB[:, mt * TOK + tb * 128:mt * TOK + (tb + 1) * 128],
                            in_=ar[:, sl * 512:sl * 512 + 128]), inc="vs")
                    c.mark(pfx + f"zte{zt_n}", "vs")
                    zt_n += 1
            c.mark(pfx + "zT_done", "vs")

            load_w_slabs(c, pfx + "v", WV, (pfx + "zT_done", "x_ready"))
            nvm = 0
            for tc in range(4):
                nm = nm_tc[tc]
                if tc == 0:
                    c.wait("gpsimd", "x_ready")
                    c.dma("gpsimd", vbias, BVR)
                    c.mark(pfx + "vbias0", "qg")
                for tb in range(tc * 4, tc * 4 + 4):
                    vslot = A_VST + (tb % 2) * D
                    if tb >= 2:
                        c.wait("vector", pfx + f"vw{tb - 2}")
                    for oh in range(2):
                        par = nvm % 2
                        if nvm >= 2:
                            c.wait("tensor", pfx + f"vev{nvm - 2}")
                        for m in range(nm):
                            c.wait("tensor", pfx + f"vwf{m}")
                            c.do("tensor", lambda e, m=m, tb=tb, oh=oh, par=par, nm=nm: e.matmul(
                                ar[:, 1024 + par * 512:1024 + (par + 1) * 512],
                                regB[:, m * TOK + tb * 128:m * TOK + (tb + 1) * 128],
                                regA[:, A_WF + m * D + oh * 512:A_WF + m * D + (oh + 1) * 512],
                                start=(m == 0), stop=(m == nm - 1)), inc="ts")
                        c.mark(pfx + f"vmm{nvm}", "ts")
                        c.wait("vector", pfx + f"vmm{nvm}")
                        c.wait("vector", pfx + "vbias0")
                        c.do("vector", lambda e, par=par, oh=oh, vslot=vslot: e.tensor_tensor(
                            out=regA[:, vslot + oh * 512:vslot + (oh + 1) * 512],
                            in0=ar[:, 1024 + par * 512:1024 + (par + 1) * 512],
                            in1=vbias[:, oh * 512:(oh + 1) * 512], op=ALU.add), inc="vs")
                        c.mark(pfx + f"vev{nvm}", "vs")
                        nvm += 1
                    c.wait("gpsimd", pfx + f"vev{nvm - 1}")
                    vdst = vlocA[si] if tb < 8 else vlocB[si]
                    c.dma("gpsimd", vdst[(tb % 8) * 128:(tb % 8 + 1) * 128, :],
                          regA[:, vslot:vslot + D])
                    c.mark(pfx + f"vw{tb}", "qg")
                    if tb == 7:
                        c.mark(pfx + "vwA", "qg")
            c.mark(pfx + "vwB", "qg")
            c.mark(pfx + "v_p1_done", "ts")

            load_w_slabs(c, pfx + "k", WK, (pfx + "vwf7", pfx + "v_p1_done"))
            c.wait("vector", pfx + "bias_in")
            c.do("vector", lambda e: e.tensor_copy(
                out=biasQK[:, 8:16], in_=bqkv[:, 8:16]), inc="vs")
            c.mark(pfx + "kbias", "vs")
            c.wait("tensor", pfx + "zT_done")
            eh = ehF if isffn else ehA
            nmm = 0
            for tc in range(4):
                nm = nm_tc[tc]
                for o in range(8):
                    par = nmm % 2
                    if nmm >= 2:
                        c.wait("tensor", pfx + f"kev{nmm - 2}")
                    for m in range(nm):
                        c.wait("tensor", pfx + f"kwf{m}")
                        c.do("tensor", lambda e, o=o, m=m, tc=tc, par=par, nm=nm: e.matmul(
                            ar[:, par * 512:(par + 1) * 512],
                            regA[:, A_WF + m * D + o * 128:A_WF + m * D + (o + 1) * 128],
                            regB[:, m * TOK + tc * 512:m * TOK + (tc + 1) * 512],
                            start=(m == 0), stop=(m == nm - 1)), inc="ts")
                    c.mark(pfx + f"kmm{nmm}", "ts")
                    c.wait("scalar", pfx + f"kmm{nmm}")
                    c.wait("scalar", pfx + "kbias")
                    if nmm >= 8:
                        c.wait("scalar", pfx + f"kw{nmm - 8}")
                        c.wait("scalar", pfx + f"ksq{nmm - 8}")
                    c.do("scalar", lambda e, o=o, par=par: e.activation(
                        out=regA[:, A_KST + o * 512:A_KST + (o + 1) * 512],
                        in_=ar[:, par * 512:(par + 1) * 512],
                        func=AF.Identity, bias=biasQK[:, 8 + o:9 + o]), inc="ss")
                    c.mark(pfx + f"kev{nmm}", "ss")
                    c.wait("vector", pfx + f"kev{nmm}")
                    if nmm > 0:
                        c.wait("vector", pfx + f"knmm{nmm - 1}")
                    c.do("vector", lambda e, o=o: e.tensor_tensor(
                        out=sqbuf,
                        in0=regA[:, A_KST + o * 512:A_KST + (o + 1) * 512],
                        in1=regA[:, A_KST + o * 512:A_KST + (o + 1) * 512],
                        op=ALU.mult), inc="vs")
                    c.mark(pfx + f"ksq{nmm}", "vs")
                    c.wait("tensor", pfx + f"ksq{nmm}")
                    c.wait("tensor", "consts")
                    if o == 0 and tc > 0:
                        c.wait("tensor", pfx + f"kqmax{tc - 1}")
                    c.do("tensor", lambda e, o=o, eh=eh: e.matmul(
                        ar[0:32, 2048:2560], eh[:, o * 32:(o + 1) * 32], sqbuf,
                        start=(o == 0), stop=(o == 7),
                        tile_position=(0, 0)), inc="ts")
                    c.mark(pfx + f"knmm{nmm}", "ts")
                    c.wait("gpsimd", pfx + f"kev{nmm}")
                    kdst = klocA[si] if tc < 2 else klocB[si]
                    c.dma("gpsimd",
                          kdst[o * 128:(o + 1) * 128, (tc % 2) * 512:(tc % 2 + 1) * 512],
                          regA[:, A_KST + o * 512:A_KST + (o + 1) * 512])
                    c.mark(pfx + f"kw{nmm}", "qg")
                    if nmm == 15:
                        c.mark(pfx + "kwA", "qg")
                    nmm += 1
                c.wait("vector", pfx + f"knmm{nmm - 1}")
                c.dof("vector", lambda e: e.reduce_max(
                    out=sumsSB[0:32, 0:1], in_=ar[0:32, 2048:2560], axis=AX.X))
                if tc == 0:
                    c.dof("vector", lambda e: e.tensor_copy(
                        out=kqmax[0:32, 0:1], in_=sumsSB[0:32, 0:1]))
                else:
                    c.dof("vector", lambda e: e.tensor_max(
                        out=kqmax[0:32, 0:1], in0=kqmax[0:32, 0:1],
                        in1=sumsSB[0:32, 0:1]))
                c.mark(pfx + f"kqmax{tc}", "vs")
            c.mark(pfx + "kwB", "qg")
            c.mark(pfx + "k_p2_done", "ts")
            c.wait("vector", pfx + "kqmax3")
            c.dof("vector", lambda e: e.tensor_copy(
                out=sumsSB[0:32, 1:2], in_=kqmax[0:32, 0:1]))
            c.mark(pfx + "knloc", "vs")
            c.wait("gpsimd", pfx + "knloc")
            c.dma("gpsimd", bnd[si], sumsSB[0:32, 1:2])
            c.mark(pfx + "bndw", "qg")
            c.wait("gpsimd", pfx + "bndw")
            c.wait("gpsimd", pfx + "kwB")
            c.wait("gpsimd", pfx + "vwB")
            c.do("gpsimd", lambda e: e.collective_compute(
                "AllGather", ALU.bypass, replica_groups=rg,
                ins=[bnd[si].opt()], outs=[bndg[si].opt()]), inc="cc")
            c.mark(pfx + "cc_bnd", "cc")
            c.do("gpsimd", lambda e: e.collective_compute(
                "AllGather", ALU.bypass, replica_groups=rg,
                ins=[klocA[si].opt()], outs=[kgA[si].opt()]), inc="cc")
            c.mark(pfx + "cc_kA", "cc")
            c.do("gpsimd", lambda e: e.collective_compute(
                "AllGather", ALU.bypass, replica_groups=rg,
                ins=[klocB[si].opt()], outs=[kgB[si].opt()]), inc="cc")
            c.mark(pfx + "cc_kB", "cc")
            c.do("gpsimd", lambda e: e.collective_compute(
                "AllGather", ALU.bypass, replica_groups=rg,
                ins=[vlocA[si].opt()], outs=[vgA[si].opt()]), inc="cc")
            c.mark(pfx + "cc_vA", "cc")
            c.do("gpsimd", lambda e: e.collective_compute(
                "AllGather", ALU.bypass, replica_groups=rg,
                ins=[vlocB[si].opt()], outs=[vgB[si].opt()]), inc="cc")
            c.mark(pfx + "cc_vB", "cc")
            c.wait("gpsimd", pfx + "cc_bnd")
            c.dma("gpsimd", kngT[32:64, 0:4],
                  bndg[si].rearrange("(r h) c -> h (r c)", h=32),
                  allow_slow_non_contiguous=True)
            c.mark(pfx + "bnd_rd", "qg")
            c.wait("vector", pfx + "bnd_rd")
            c.dof("vector", lambda e: e.reduce_max(
                out=kngT[32:64, 0:1], in_=kngT[32:64, 0:4], axis=AX.X))
            c.mark(pfx + "knglob", "vs")

            load_w_slabs(c, pfx + "q", WQ, (pfx + "kwf7", pfx + "k_p2_done"))
            c.wait("vector", pfx + "bias_in")
            c.do("vector", lambda e: e.tensor_copy(
                out=biasQK[:, 0:8], in_=bqkv[:, 0:8]), inc="vs")
            c.mark(pfx + "qbias", "vs")
            c.wait("tensor", pfx + "zT_done")
            nmm = 0
            for tc in range(4):
                nm = nm_tc[tc]
                for o in range(8):
                    par = nmm % 2
                    if nmm >= 2:
                        c.wait("tensor", pfx + f"qev{nmm - 2}")
                    else:
                        c.wait("tensor", pfx + f"kev{30 + nmm}")
                    for m in range(nm):
                        c.wait("tensor", pfx + f"qwf{m}")
                        c.do("tensor", lambda e, o=o, m=m, tc=tc, par=par, nm=nm: e.matmul(
                            ar[:, par * 512:(par + 1) * 512],
                            regA[:, A_WF + m * D + o * 128:A_WF + m * D + (o + 1) * 128],
                            regB[:, m * TOK + tc * 512:m * TOK + (tc + 1) * 512],
                            start=(m == 0), stop=(m == nm - 1)), inc="ts")
                    c.mark(pfx + f"qmm{nmm}", "ts")
                    c.wait("scalar", pfx + f"qmm{nmm}")
                    c.wait("scalar", pfx + "qbias")
                    c.do("scalar", lambda e, o=o, tc=tc, par=par: e.activation(
                        out=QT[:, o * TOK + tc * 512:o * TOK + (tc + 1) * 512],
                        in_=ar[:, par * 512:(par + 1) * 512],
                        func=AF.Identity, bias=biasQK[:, o:o + 1]), inc="ss")
                    c.mark(pfx + f"qev{nmm}", "ss")
                    nmm += 1
                for o in range(8):
                    if o == 0:
                        c.wait("vector", pfx + f"qev{tc * 8 + 7}")
                        c.wait("vector", pfx + ("knmm31" if tc == 0
                                                else f"qnmm{tc - 1}_7"))
                    else:
                        c.wait("vector", pfx + f"qnmm{tc}_{o - 1}")
                    c.do("vector", lambda e, o=o, tc=tc: e.tensor_tensor(
                        out=sqbuf,
                        in0=QT[:, o * TOK + tc * 512:o * TOK + (tc + 1) * 512],
                        in1=QT[:, o * TOK + tc * 512:o * TOK + (tc + 1) * 512],
                        op=ALU.mult), inc="vs")
                    c.mark(pfx + f"qsq{tc}_{o}", "vs")
                    c.wait("tensor", pfx + f"qsq{tc}_{o}")
                    if tc == 0 and o == 0:
                        c.wait("tensor", pfx + "kqmax3")
                    if o == 0 and tc > 0:
                        c.wait("tensor", pfx + f"qqmax{tc - 1}")
                    c.do("tensor", lambda e, o=o, eh=eh: e.matmul(
                        ar[32:64, 2048:2560], eh[:, o * 32:(o + 1) * 32], sqbuf,
                        start=(o == 0), stop=(o == 7),
                        tile_position=(0, 32)), inc="ts")
                    c.mark(pfx + f"qnmm{tc}_{o}", "ts")
                c.wait("vector", pfx + f"qnmm{tc}_7")
                c.dof("vector", lambda e: e.reduce_max(
                    out=sumsSB[32:64, 0:1], in_=ar[32:64, 2048:2560], axis=AX.X))
                if tc == 0:
                    c.dof("vector", lambda e: e.tensor_copy(
                        out=kqmax[32:64, 0:1], in_=sumsSB[32:64, 0:1]))
                else:
                    c.dof("vector", lambda e: e.tensor_max(
                        out=kqmax[32:64, 0:1], in0=kqmax[32:64, 0:1],
                        in1=sumsSB[32:64, 0:1]))
                c.mark(pfx + f"qqmax{tc}", "vs")
            c.mark(pfx + "q_p2_done", "ts")

            c.wait("vector", pfx + "qqmax3")
            c.wait("vector", pfx + "knglob")
            c.dof("vector", lambda e: e.tensor_tensor(
                out=kngT[32:64, 1:2], in0=kqmax[32:64, 0:1],
                in1=kngT[32:64, 0:1], op=ALU.add))
            c.mark(pfx + "cmb", "vs")
            c.wait("tensor", pfx + "cmb")
            c.wait("tensor", pfx + "q_p2_done")
            c.do("tensor", lambda e: e.transpose(
                ar[0:1, 2560:2592], kngT[32:64, 1:2], identf[32:64, 32:64]),
                inc="ts")
            c.mark(pfx + "cmbT", "ts")
            c.wait("vector", pfx + "cmbT")
            c.dof("vector", lambda e: e.reduce_max(
                out=tiny[0:1, 0:1], in_=ar[0:1, 2560:2592], axis=AX.X))
            c.dof("vector", lambda e, scale=scale: e.tensor_scalar_mul(
                out=tiny[0:1, 1:2], in0=tiny[0:1, 0:1], scalar1=-0.5 * scale))
            c.mark(pfx + "c11", "vs")
            c.wait("tensor", pfx + "c11")
            c.do("tensor", lambda e: e.matmul(
                ar[:, 2560:2561], onesf[0:1, :], tiny[0:1, 1:2],
                start=True, stop=True), inc="ts")
            c.mark(pfx + "cbc", "ts")
            c.wait("vector", pfx + "cbc")
            c.do("vector", lambda e: e.tensor_copy(out=negC, in_=ar[:, 2560:2561]), inc="vs")
            c.mark(pfx + "negC", "vs")

            emit_attention(c, si, kind, pfx, scale)

            emit_phase3(c, si, kind, pfx, nm_tc, mout_tc, WO)

        def emit_attention(c, si, kind, pfx, scale):
            isffn = kind == FFN
            ngrp = 4 if isffn else 8
            nktb = 4 if isffn else 2
            ktb_ranks = 1 if isffn else 2
            SLABW = 8192 if isffn else 8256
            if not isffn:
                for gate in ("q_p2_done", "ksq31", "vev31", "kwB", "vwB"):
                    c.wait("vector", pfx + gate)
                for par in range(2):
                    for hh in range(2):
                        for rr in range(2):
                            vr = A_SLAB + par * SLABW + 4096 + hh * 2080 + rr * 1040
                            c.do("vector", lambda e, vr=vr: e.memset(
                                regA[:, vr:vr + 1040]
                                .rearrange("p (j d) -> p j d", d=65)[:, :, 64:65],
                                1.0), inc="vs")
                c.mark(pfx + "vones", "vs")

            def emit_slab_dma(g, ktb):
                    par = ktb % 2
                    base = A_SLAB + par * SLABW
                    seq = g * nktb + ktb
                    if seq == 0:
                        c.wait("sync", pfx + "q_p2_done")
                    elif seq == 1:
                        c.wait("sync", pfx + "ksq31")
                        c.wait("sync", pfx + "vev31")
                        c.wait("sync", pfx + "kwB")
                        c.wait("sync", pfx + "vwB")
                    else:
                        pg, pk = divmod(seq - 2, nktb)
                        c.wait("sync", pfx + f"avdone_g{pg}k{pk}")
                    for half, ccm in (("A", "cc_kA"), ("B", "cc_kB")):
                        c.wait("sync", pfx + ccm)
                        kgh = (kgA if half == "A" else kgB)[si]
                        ho = 0 if half == "A" else 1024
                        for rr in range(ktb_ranks):
                            r = ktb * ktb_ranks + rr
                            for hh in range(2):
                                h = g * 2 + hh
                                if isffn:
                                    c.dma("sync",
                                          regA[:, base + hh * 2048 + ho:
                                               base + hh * 2048 + ho + 1024],
                                          kgh[r * D + h * 128:r * D + (h + 1) * 128, :])
                                else:
                                    c.dma("sync",
                                          regA[hh * 64:(hh + 1) * 64,
                                               base + rr * 2048 + ho:
                                               base + rr * 2048 + ho + 1024],
                                          kgh[r * D + h * 64:r * D + (h + 1) * 64, :])
                        c.mark(pfx + f"slabK{half}_g{g}k{ktb}", "qs")
                    for half, ccm in (("A", "cc_vA"), ("B", "cc_vB")):
                        c.wait("sync", pfx + ccm)
                        vgh = (vgA if half == "A" else vgB)[si]
                        ho = 0 if half == "A" else 1024
                        for rr in range(ktb_ranks):
                            r = ktb * ktb_ranks + rr
                            for hh in range(2):
                                h = g * 2 + hh
                                if isffn:
                                    c.dma("sync",
                                          regA[:, base + 4096 + hh * 2048 + ho:
                                               base + 4096 + hh * 2048 + ho + 1024]
                                          .rearrange("p (j d) -> p j d", d=128),
                                          vgh[r * HT:(r + 1) * HT, h * 128:(h + 1) * 128]
                                          .rearrange("(j p) d -> p j d", p=128))
                                else:
                                    vr = (base + 4096 + hh * 2080 + rr * 1040 +
                                          (0 if half == "A" else 520))
                                    c.dma("sync",
                                          regA[:, vr:vr + 520]
                                          .rearrange("p (j d) -> p j d", d=65)[:, :, 0:64],
                                          vgh[r * HT:(r + 1) * HT, h * 64:(h + 1) * 64]
                                          .rearrange("(j p) d -> p j d", p=128))
                        c.mark(pfx + f"slabV{half}_g{g}k{ktb}", "qs")

            normq = []
            for g in range(ngrp):
                if g == (2 if isffn else 4):
                    c.mark(pfx + "qtfree", "ts")
                def retire(p, g=g):
                    emit_av(c, pfx, kind, g, *p)
                    if p[11] and p[9] == nktb - 1:
                        emit_normalize_recip(c, pfx, kind, g, p[8])
                        normq.append((g, p[8]))
                emit_slab_dma(g, 0)
                if nktb > 1:
                    emit_slab_dma(g, 1)
                kt_seq = 0
                pends = []
                for ktb in range(nktb):
                    par = ktb % 2
                    base = A_SLAB + par * SLABW
                    for qc in range(4):
                        kts = []
                        for rr in range(ktb_ranks):
                            for j in range(16):
                                if isffn or j < 8 or qc >= j // 4:
                                    kts.append((rr, j))
                        for idx, (rr, j) in enumerate(kts):
                            sl = kt_seq % 2
                            masked = (not isffn) and j >= 8 and qc == j // 4
                            off = 256 if (masked and (j // 2) % 2 == 1) else 0
                            width = 512 - off
                            c.wait("tensor", pfx + f"slabK{'A' if j < 8 else 'B'}_g{g}k{ktb}")
                            c.wait("tensor", pfx + "negC")
                            if kt_seq >= 2:
                                c.wait("tensor", pfx + f"exp{g}_{kt_seq - 2}")
                            if isffn:
                                for hh in range(2):
                                    c.do("tensor", lambda e, hh=hh, j=j, sl=sl, qc=qc, base=base, g=g: e.matmul(
                                        ar[:, sl * 1024 + hh * 512:sl * 1024 + (hh + 1) * 512],
                                        regA[:, base + hh * 2048 + j * 128:base + hh * 2048 + (j + 1) * 128],
                                        QT[:, (g * 2 + hh) * TOK + qc * 512:(g * 2 + hh) * TOK + (qc + 1) * 512],
                                        start=True, stop=True), inc="ts")
                            else:
                                kb = base + rr * 2048 + j * 128
                                qb0 = g * TOK + qc * 512 + off
                                for hh in range(2):
                                    c.do("tensor", lambda e, kb=kb, hh=hh, sl=sl, qb0=qb0, off=off, width=width: e.matmul(
                                        ar[:, sl * 1024 + hh * 512 + off:sl * 1024 + (hh + 1) * 512],
                                        regA[hh * 64:(hh + 1) * 64, kb:kb + 128],
                                        QT[hh * 64:(hh + 1) * 64, qb0:qb0 + width],
                                        start=True, stop=True, tile_position=(hh * 64, 0)), inc="ts")
                            c.mark(pfx + f"qk{g}_{kt_seq}", "ts")
                            if len(pends) >= 2:
                                retire(pends.pop(0))
                            if idx == 2 and normq:
                                emit_normalize_qc(c, pfx, kind, *normq.pop(0))
                            c.wait("scalar", pfx + f"qk{g}_{kt_seq}")
                            c.wait("scalar", pfx + "negC")
                            if kt_seq >= 2:
                                c.wait("scalar", pfx + f"avP{g}_{kt_seq - 2}")
                            c.do("scalar", lambda e, sl=sl: e.activation(
                                out=regA[:, A_P + sl * 1024:A_P + (sl + 1) * 1024],
                                in_=ar[:, sl * 1024:(sl + 1) * 1024],
                                func=AF.Exp, bias=negC, scale=scale), inc="ss")
                            c.mark(pfx + f"exp{g}_{kt_seq}", "ss")
                            if masked:
                                r = ktb * ktb_ranks + rr
                                jk = r if j % 2 == 0 else 7 - r
                                c.wait("gpsimd", pfx + f"exp{g}_{kt_seq}")
                                c.wait("gpsimd", "mask01_ready")
                                for hh in range(2):
                                    c.do("gpsimd", lambda e, sl=sl, hh=hh, off=off, jk=jk: e.tensor_tensor(
                                        out=regA[:, A_P + sl * 1024 + hh * 512 + off:
                                                 A_P + sl * 1024 + hh * 512 + off + 256],
                                        in0=regA[:, A_P + sl * 1024 + hh * 512 + off:
                                                 A_P + sl * 1024 + hh * 512 + off + 256],
                                        in1=mask01[:, jk * 256:(jk + 1) * 256],
                                        op=ALU.mult), inc="ps")
                                c.mark(pfx + f"msk{g}_{kt_seq}", "ps")
                            pends.append((kt_seq, sl, masked, off, width, base, rr, j,
                                          qc, ktb, idx == 0, idx == len(kts) - 1))
                            kt_seq += 1
                    while pends:
                        retire(pends.pop(0))
                    c.mark(pfx + f"avdone_g{g}k{ktb}", "ts")
                    if ktb + 2 < nktb:
                        emit_slab_dma(g, ktb + 2)
            while normq:
                emit_normalize_qc(c, pfx, kind, *normq.pop(0))
            c.mark(pfx + "attn_done", "vs")

        def emit_av(c, pfx, kind, g, kt_seq, sl, masked, off, width, base, rr, j, qc,
                    ktb, is_first, is_last):
            isffn = kind == FFN
            c.wait("tensor", pfx + f"slabV{'A' if j < 8 else 'B'}_g{g}k{ktb}")
            c.wait("tensor", pfx + f"exp{g}_{kt_seq}")
            if masked:
                c.wait("tensor", pfx + f"msk{g}_{kt_seq}")
            if isffn and is_first and (ktb > 0 or qc > 0):
                c.wait("tensor", pfx + f"flushA{g}_{ktb}_{qc - 1}" if qc > 0
                       else pfx + f"flushA{g}_{ktb - 1}_3")
            elif isffn and is_first and g > 0:
                c.wait("tensor", pfx + f"flushA{g - 1}_3_3")
            if isffn:
                for hh in range(2):
                    c.do("tensor", lambda e, hh=hh, j=j, sl=sl, base=base, st=is_first: e.matmul(
                        ar[:, 2048 + hh * 512:2048 + (hh + 1) * 512],
                        regA[:, base + 4096 + hh * 2048 + j * 128:base + 4096 + hh * 2048 + (j + 1) * 128],
                        regA[:, A_P + sl * 1024 + hh * 512:A_P + sl * 1024 + (hh + 1) * 512],
                        start=st, stop=False), inc="ts")
                if is_first and (ktb > 0 or qc > 0):
                    c.wait("tensor", pfx + f"flush{g}_{ktb}_{qc - 1}" if qc > 0
                           else pfx + f"flush{g}_{ktb - 1}_3")
                elif is_first and g > 0:
                    c.wait("tensor", pfx + f"flush{g - 1}_3_3")
                for hh in range(2):
                    c.do("tensor", lambda e, hh=hh, sl=sl, st=is_first: e.matmul(
                        ar[hh * 32:(hh + 1) * 32, 3072:3584],
                        ones32,
                        regA[:, A_P + sl * 1024 + hh * 512:A_P + sl * 1024 + (hh + 1) * 512],
                        start=st, stop=False, tile_position=(0, hh * 32)), inc="ts")
            else:
                c.wait("tensor", pfx + "vones")
                if is_first:
                    pred = ktb * 4 + qc - 2
                    pg2 = g
                    if pred < 0:
                        pg2, pred = g - 1, (4 if isffn else 2) * 4 + pred
                    if pg2 >= 0:
                        pk2, pq2 = divmod(pred, 4)
                        if pk2 == (4 if isffn else 2) - 1:
                            c.wait("tensor", pfx + f"nm{pg2}_{pq2}_1")
                        else:
                            c.wait("tensor", pfx + f"flush{pg2}_{pk2}_{pq2}")
                vb = base + 4096
                pa = 2048 + (qc % 2) * 1024
                for hh in range(2):
                    lcol = vb + hh * 2080 + rr * 1040 + j * 65
                    c.do("tensor", lambda e, hh=hh, lcol=lcol, sl=sl, off=off, st=is_first, pa=pa: e.matmul(
                        ar[0:65, pa + hh * 512 + off:pa + hh * 512 + 512],
                        regA[:, lcol:lcol + 65],
                        regA[:, A_P + sl * 1024 + hh * 512 + off:A_P + sl * 1024 + (hh + 1) * 512],
                        start=st, stop=False), inc="ts")
            c.mark(pfx + f"avP{g}_{kt_seq}", "ts")
            if is_last:
                c.wait("vector", pfx + f"avP{g}_{kt_seq}")
                first_blk = ktb == 0
                if first_blk and g > 0:
                    c.wait("vector", pfx + f"norm{g - 1}")
                if not first_blk:
                    c.wait("vector", pfx + f"flush{g}_{ktb - 1}_{qc}")
                if isffn:
                    for hh in range(2):
                        src = ar[:, 2048 + hh * 512:2048 + (hh + 1) * 512]
                        dst = AVacc[:, hh * 2048 + qc * 512:hh * 2048 + (qc + 1) * 512]
                        if first_blk:
                            c.do("vector", lambda e, src=src, dst=dst: e.tensor_copy(
                                out=dst, in_=src), inc="vs")
                        else:
                            c.do("vector", lambda e, src=src, dst=dst: e.tensor_tensor(
                                out=dst, in0=src, in1=dst, op=ALU.add), inc="vs")
                    c.mark(pfx + f"flushA{g}_{ktb}_{qc}", "vs")
                else:
                    pa = 2048 + (qc % 2) * 1024
                    pairs = [
                        (ar[0:64, pa:pa + 512], AVacc[0:64, qc * 512:(qc + 1) * 512]),
                        (ar[0:64, pa + 512:pa + 1024], AVacc[64:128, qc * 512:(qc + 1) * 512]),
                        (ar[64:65, pa:pa + 512], sumsSB[0:1, qc * 512:(qc + 1) * 512]),
                        (ar[64:65, pa + 512:pa + 1024], sumsSB[32:33, qc * 512:(qc + 1) * 512]),
                    ]
                    for src, dst in pairs:
                        if first_blk:
                            c.do("vector", lambda e, src=src, dst=dst: e.tensor_copy(
                                out=dst, in_=src), inc="vs")
                        else:
                            c.do("vector", lambda e, src=src, dst=dst: e.tensor_tensor(
                                out=dst, in0=src, in1=dst, op=ALU.add), inc="vs")
                if isffn:
                    ssrc = ar[0:64, 3072:3584]
                    sdst = sumsSB[0:64, qc * 512:(qc + 1) * 512]
                    if first_blk:
                        c.do("vector", lambda e, ssrc=ssrc, sdst=sdst: e.tensor_copy(
                            out=sdst, in_=ssrc), inc="vs")
                    else:
                        c.do("vector", lambda e, ssrc=ssrc, sdst=sdst: e.tensor_tensor(
                            out=sdst, in0=ssrc, in1=sdst, op=ALU.add), inc="vs")
                c.mark(pfx + f"flush{g}_{ktb}_{qc}", "vs")

        def emit_normalize_recip(c, pfx, kind, g, qc):
            isffn = kind == FFN
            nktb = 4 if isffn else 2
            c.wait("vector", pfx + f"flush{g}_{nktb - 1}_{qc}")
            c.do("vector", lambda e, qc=qc: e.reciprocal(
                out=sumsSB[:, qc * 512:(qc + 1) * 512],
                in_=sumsSB[:, qc * 512:(qc + 1) * 512]), inc="vs")
            c.mark(pfx + f"recip{g}_{qc}", "vs")

        def emit_normalize_qc(c, pfx, kind, g, qc):
            isffn = kind == FFN
            PS = 3584
            if isffn:
                for hh in range(2):
                    c.wait("tensor", pfx + f"recip{g}_{qc}")
                    prev = (pfx + f"nm{g}_{qc}_{hh - 1}" if hh == 1 else
                            (pfx + f"nm{g}_{qc - 1}_1" if qc > 0 else
                             (pfx + f"nm{g - 1}_3_1" if g > 0 else None)))
                    if prev is not None:
                        c.wait("tensor", prev)
                    c.do("tensor", lambda e, hh=hh, qc=qc: e.matmul(
                        ar[:, PS:PS + 512],
                        onesf[hh * 32:hh * 32 + 1, :],
                        sumsSB[hh * 32:hh * 32 + 1, qc * 512:(qc + 1) * 512],
                        start=True, stop=True,
                        tile_position=(hh * 32, 0)), inc="ts")
                    c.mark(pfx + f"nbc{g}_{qc}_{hh}", "ts")
                    c.wait("vector", pfx + f"nbc{g}_{qc}_{hh}")
                    c.do("vector", lambda e, hh=hh, qc=qc: e.tensor_tensor(
                        out=regB[:, (2 * g + hh) * TOK + qc * 512:
                                 (2 * g + hh) * TOK + (qc + 1) * 512],
                        in0=ar[:, PS:PS + 512],
                        in1=AVacc[:, hh * 2048 + qc * 512:hh * 2048 + (qc + 1) * 512],
                        op=ALU.mult), inc="vs")
                    c.mark(pfx + f"nm{g}_{qc}_{hh}", "vs")
            else:
                PSA = 2048 + (qc % 2) * 1024
                c.wait("tensor", pfx + f"recip{g}_{qc}")
                prev = (pfx + f"nm{g}_{qc - 1}_1" if qc > 0 else
                        (pfx + f"nm{g - 1}_3_1" if g > 0 else None))
                if prev is not None:
                    c.wait("tensor", prev)
                c.do("tensor", lambda e, qc=qc, PSA=PSA: e.matmul(
                    ar[0:64, PSA:PSA + 512],
                    onesf[0:1, 0:64],
                    sumsSB[0:1, qc * 512:(qc + 1) * 512],
                    start=True, stop=True, tile_position=(0, 0)), inc="ts")
                c.do("tensor", lambda e, qc=qc, PSA=PSA: e.matmul(
                    ar[64:128, PSA:PSA + 512],
                    onesf[32:33, 0:64],
                    sumsSB[32:33, qc * 512:(qc + 1) * 512],
                    start=True, stop=True, tile_position=(32, 64)), inc="ts")
                c.mark(pfx + f"nbc{g}_{qc}_1", "ts")
                c.wait("vector", pfx + f"nbc{g}_{qc}_1")
                c.do("vector", lambda e, qc=qc, PSA=PSA: e.tensor_tensor(
                    out=regB[:, g * TOK + qc * 512:g * TOK + (qc + 1) * 512],
                    in0=ar[:, PSA:PSA + 512],
                    in1=AVacc[:, qc * 512:(qc + 1) * 512],
                    op=ALU.mult), inc="vs")
                c.mark(pfx + f"nm{g}_{qc}_1", "vs")
            if qc == 3:
                c.mark(pfx + f"norm{g}", "vs")

        def emit_phase3(c, si, kind, pfx, nm_tc, mout_tc, WO):
            isffn = kind == FFN
            load_w_slabs(c, pfx + "o", WO,
                         (pfx + "qwf7", pfx + "qtfree"),
                         scale=(0.5 if isffn else None), dstT=QT)
            if isffn:
                c.wait("gpsimd", pfx + "vev31")
                c.dma("gpsimd", vbias, fobR)
                c.mark(pfx + "obr_in", "qg")
                c.wait("scalar", pfx + "obr_in")
                c.do("scalar", lambda e: e.mul(out=vbias, in_=vbias, mul=0.5), inc="ss")
                c.mark(pfx + "obhalf", "ss")
            npo = 0
            for tc in range(4):
                mo = mout_tc[tc]
                nmch = nm_tc[tc] if isffn else 8
                noch = (mo + 511) // 512
                for tb in range(tc * 4, tc * 4 + 4):
                    if isffn:
                        c.wait("vector", pfx + "owf7")
                        c.wait("vector", pfx + "obhalf")
                        c.dof("vector", lambda e, tb=tb: e.tensor_scalar(
                            out=x_res[:, tb * D:(tb + 1) * D],
                            in0=x_res[:, tb * D:(tb + 1) * D],
                            scalar1=small[:, 64 + tb:65 + tb],
                            scalar2=small[:, 80 + tb:81 + tb],
                            op0=ALU.mult, op1=ALU.add))
                        c.dof("vector", lambda e, tb=tb, mo=mo: e.tensor_tensor(
                            out=x_res[:, tb * D:tb * D + mo], in0=vbias[:, 0:mo],
                            in1=x_res[:, tb * D:tb * D + mo], op=ALU.add))
                        c.mark(pfx + f"pre{tb}", "vs")
                    for och in range(noch):
                        w = min(512, mo - och * 512)
                        par = npo % 2
                        if npo >= 2:
                            c.wait("tensor", pfx + f"oev{npo - 2}")
                        c.wait("tensor", pfx + "attn_done")
                        for m in range(nmch):
                            c.wait("tensor", pfx + f"owf{m}")
                            c.do("tensor", lambda e, m=m, tb=tb, och=och, par=par, w=w, nmch=nmch: e.matmul(
                                ar[:, par * 512:par * 512 + w],
                                regB[:, m * TOK + tb * 128:m * TOK + (tb + 1) * 128],
                                QT[:, m * D + och * 512:m * D + och * 512 + w],
                                start=(m == 0), stop=(m == nmch - 1)), inc="ts")
                        c.mark(pfx + f"omm{npo}", "ts")
                        c.wait("vector", pfx + f"omm{npo}")
                        c.do("vector", lambda e, par=par, tb=tb, och=och, w=w: e.tensor_tensor(
                            out=x_res[:, tb * D + och * 512:tb * D + och * 512 + w],
                            in0=ar[:, par * 512:par * 512 + w],
                            in1=x_res[:, tb * D + och * 512:tb * D + och * 512 + w],
                            op=ALU.add), inc="vs")
                        c.mark(pfx + f"oev{npo}", "vs")
                        npo += 1
                    c.mark(pfx + f"res{tb}", "vs")
            c.mark("x_ready", "vs")
            c.mark("x_ready_p", "ps")

        def emit_all(c: Ctx):
            c.dma("sync", x_res.rearrange("p (t f) -> p t f", f=D),
                  x_in.rearrange("(t p) f -> p t f", p=128))
            c.mark("x_in_dma", "qs")
            for dst, src in ((ident, ident_c), (identf, identf_c),
                             (ones32, ones32_c), (onesf, onesf_c),
                             (ehF, ehF_c), (ehA, ehA_c), (mask01, maskin)):
                c.dma("gpsimd", dst, src)
            c.mark("consts", "qg")
            c.mark("mask01_ready", "qg")
            c.wait("vector", "x_in_dma")
            c.do("vector", lambda e: e.memset(tiny, 0.0), inc="vs")
            c.mark("x_ready", "vs")
            c.mark("x_ready_p", "ps")


            for si, kind in enumerate(sublayers):
                emit_sublayer(c, si, kind)
                if si in taps:
                    c.wait("sync", "x_ready")
                    c.dma("sync", taps[si].rearrange("(t p) f -> p t f", p=128),
                          x_res.rearrange("p (t f) -> p t f", f=D))
                    c.mark(f"tap{si}", "qs")

            lastp = f"s{len(sublayers) - 1}_" if sublayers else ""
            xo_r = x_out.rearrange("(t p) f -> p t f", p=128)
            for tb in range(NTILE):
                if sublayers:
                    c.wait("sync", lastp + f"res{tb}")
                c.dma("sync", xo_r[:, tb:tb + 1, :],
                      x_res[:, tb * D:(tb + 1) * D].rearrange(
                          "p (t f) -> p t f", f=D))
            if DEBUG_DUMPS:
                dump_srcs = {"kloc0": klocA[0], "vloc0": vlocA[0], "kg0": kgA[0],
                             "vg0": vgA[0], "bndg0": bndg[0], "QT": QT,
                             "sums": sumsSB, "AVacc": AVacc, "attnT": regB,
                             "negC": negC, "x_res_d": x_res}
                for nm in DEBUG_DUMPS:
                    c.dma("sync", dumps_out[nm], dump_srcs[nm])
            c.mark("out_stored", "qs")
            c.wait("sync", "out_stored")

        blk.sync(lambda eng: emit_all(Ctx("sync", eng, sems)))
        blk.tensor(lambda eng: emit_all(Ctx("tensor", eng, sems)))
        blk.vector(lambda eng: emit_all(Ctx("vector", eng, sems)))
        blk.scalar(lambda eng: emit_all(Ctx("scalar", eng, sems)))
        blk.gpsimd(lambda eng: emit_all(Ctx("gpsimd", eng, sems)))

    return nc


def _mask_block(jk, cloc):
    m = np.zeros((128, 256), np.float32)
    for blkcol, tq in enumerate((cloc, 7 - cloc)):
        sl = slice(blkcol * 128, (blkcol + 1) * 128)
        if tq > jk:
            m[:, sl] = 1.0
        elif tq == jk:
            k = np.arange(128)[:, None]
            q = np.arange(128)[None, :]
            m[:, sl] = (k <= q).astype(np.float32)
    return m


def kernel(L0_attn_x, L0_attn_Wq, L0_attn_bq, L0_attn_Wk, L0_attn_bk, L0_attn_Wv,
           L0_attn_bv, L0_attn_Wo, L0_ffn_router_prob, L0_ffn_norm_w, L0_ffn_norm_b,
           L0_ffn_q_w, L0_ffn_q_b, L0_ffn_k_w, L0_ffn_k_b, L0_ffn_v_w, L0_ffn_v_b,
           L0_ffn_o_w, L0_ffn_o_b):
    x = np.ascontiguousarray(np.asarray(L0_attn_x, np.float32))
    B, T, _ = x.shape
    xt = x.reshape(B, 64, 128, D)
    assert np.asarray(L0_ffn_router_prob).shape[-1] == 4

    tileT = lambda v: np.ascontiguousarray(np.asarray(v, np.float32).reshape(8, 128).T)
    rep = lambda v: np.ascontiguousarray(
        np.broadcast_to(np.asarray(v, np.float32), (128, D)))
    tr = lambda w: np.ascontiguousarray(np.asarray(w, np.float32).T)
    natw = lambda w: np.ascontiguousarray(np.asarray(w, np.float32))

    common = dict(
        aWq=natw(L0_attn_Wq), aWk=natw(L0_attn_Wk), aWv=natw(L0_attn_Wv),
        aWo=natw(L0_attn_Wo),
        abqT=tileT(L0_attn_bq), abkT=tileT(L0_attn_bk), abvR=rep(L0_attn_bv),
        fqwT=tr(L0_ffn_q_w), fkwT=tr(L0_ffn_k_w), fvwT=tr(L0_ffn_v_w),
        fowT=tr(L0_ffn_o_w),
        fqbT=tileT(L0_ffn_q_b), fkbT=tileT(L0_ffn_k_b), fvbR=rep(L0_ffn_v_b),
        fnwT=tileT(L0_ffn_norm_w), fnbT=tileT(L0_ffn_norm_b), fobR=rep(L0_ffn_o_b),
    )

    in_maps = []
    for cidx in range(8):
        b, cloc = divmod(cidx, 4)
        tm = tmap(cloc)
        xs = np.ascontiguousarray(xt[b, tm].reshape(TOK, D))
        mk = np.concatenate([_mask_block(jk, cloc) for jk in range(8)], axis=1)
        im = dict(common)
        im["x_in"] = xs
        im["maskin"] = np.ascontiguousarray(mk)
        in_maps.append(im)

    nc = build_program()
    res = run_bass_kernel_spmd(nc, in_maps, core_ids=list(range(8)))

    out = np.empty((B, 64, 128, D), np.float32)
    for cidx in range(8):
        b, cloc = divmod(cidx, 4)
        tm = tmap(cloc)
        out[b, tm] = res.results[cidx]["x_out"].reshape(NTILE, 128, D)
    return out.reshape(B, T, D)
